# version 01
# baseline (speedup 1.0000x reference)
import numpy as np
import ml_dtypes
from contextlib import ExitStack

import concourse.bass as bass
import concourse.mybir as mybir
from concourse.bass_utils import run_bass_kernel_spmd

F32 = mybir.dt.float32
BF16 = mybir.dt.bfloat16
I32 = mybir.dt.int32
FP8 = mybir.dt.float8e4
AX = mybir.AluOpType
AF = mybir.ActivationFunctionType
AXL = mybir.AxisListType

MAGIC = 12582912.0
NEG = -1.0e9
SEQ = 4096
DM = 2048
DFF = 8192
NH = 16
DH = 128
HALF = 2048
NSUB = 16
NSUBF = 32
NCH = 8
TOPK = 64

_PHASES = "wabc"
_DBG = ()


class Ctx:

    def __init__(self, nc, stack, dbg):
        self.nc = nc
        self.stack = stack
        self.dbg = set(dbg)
        self.dbg_dram = {}
        self.sems = {}
        self.cnt = {}
        self.sem("gbar")

    def sem(self, name):
        if name not in self.sems:
            self.sems[name] = self.stack.enter_context(self.nc.semaphore(name))
            self.cnt[name] = 0
        return self.sems[name]

    def inc(self, ins, name, n=1):
        ins.then_inc(self.sem(name), n)
        self.cnt[name] += n
        return self.cnt[name]

    def wait(self, eng, name, val=None):
        eng.wait_ge(self.sem(name), self.cnt[name] if val is None else val)

    def edge(self, eng, tag):
        name = "e_" + tag
        self.inc(eng.nop(), name)
        self.wait(eng, name)

    def barrier_all(self, name):
        nc = self.nc
        engines = [nc.sync, nc.scalar, nc.vector, nc.tensor, nc.gpsimd]
        s = self.sem(name)
        base = self.cnt[name]
        for e in engines:
            e.sem_inc(s, 1)
        self.cnt[name] = base + 5
        for e in engines:
            e.wait_ge(s, base + 5)

    def free_scope(self, prefixes):
        victims = [n for n in self.sems if any(n.startswith(p) for p in prefixes)]
        if not victims:
            return
        self.nc.clear_and_free_semaphores([self.sems[n] for n in victims])
        for n in victims:
            del self.sems[n]
            del self.cnt[n]
        self.barrier_all("gbar")

    def sb(self, name, shape, dt=F32):
        return self.stack.enter_context(self.nc.sbuf_tensor(name, shape, dt))

    def ps(self, name, shape, dt=F32):
        return self.stack.enter_context(self.nc.psum_tensor(name, shape, dt))

    def dbg_out(self, name, shape, dt=F32):
        if name not in self.dbg_dram:
            self.dbg_dram[name] = self.nc.dram_tensor(
                "dbg_" + name, shape, dt, kind="ExternalOutput").ap()
        return self.dbg_dram[name]

    def dbg_dump(self, name, src_ap, shape, dt=F32):
        if name not in self.dbg:
            return
        d = self.dbg_out(name, shape, dt)
        self.inc(self.nc.gpsimd.dma_start(d, src_ap), "dbgo_" + name, 16)
        self.wait(self.nc.gpsimd, "dbgo_" + name)


def build_program(dbg=(), phases=_PHASES):
    nc = bass.Bass("TRN2", target_bir_lowering=False)
    h = {}
    h["x"] = nc.dram_tensor("x_seq", [SEQ, DM], F32, kind="ExternalInput").ap()
    h["wq"] = nc.dram_tensor("Wq", [DM, DM], F32, kind="ExternalInput").ap()
    h["wk"] = nc.dram_tensor("Wk", [DM, DM], F32, kind="ExternalInput").ap()
    h["wv"] = nc.dram_tensor("Wv", [DM, DM], F32, kind="ExternalInput").ap()
    h["w1"] = nc.dram_tensor("w1", [DFF, DM], F32, kind="ExternalInput").ap()
    h["w2"] = nc.dram_tensor("w2", [DM, DFF], F32, kind="ExternalInput").ap()
    h["bq"] = nc.dram_tensor("bq_rep", [128, DM], F32, kind="ExternalInput").ap()
    h["bkh"] = nc.dram_tensor("bk_hm", [128, NH], F32, kind="ExternalInput").ap()
    h["bv"] = nc.dram_tensor("bv_rep", [128, DM], F32, kind="ExternalInput").ap()
    h["g1"] = nc.dram_tensor("g1_rep", [128, DM], F32, kind="ExternalInput").ap()
    h["g2"] = nc.dram_tensor("g2_rep", [128, DFF], F32, kind="ExternalInput").ap()
    h["idb"] = nc.dram_tensor("ident_bf", [128, 128], BF16, kind="ExternalInput").ap()
    h["idf"] = nc.dram_tensor("ident_f32", [128, 128], F32, kind="ExternalInput").ap()
    h["triu"] = nc.dram_tensor("triu_bf", [128, 128], BF16, kind="ExternalInput").ap()
    h["onescb"] = nc.dram_tensor("ones_col_bf", [128, 1], BF16, kind="ExternalInput").ap()
    h["onesrb"] = nc.dram_tensor("ones_row_bf", [1, 128], BF16, kind="ExternalInput").ap()
    h["onescf"] = nc.dram_tensor("ones_col_f", [128, 1], F32, kind="ExternalInput").ap()
    h["onesrf"] = nc.dram_tensor("ones_row_f", [1, 128], F32, kind="ExternalInput").ap()
    h["iota64"] = nc.dram_tensor("iota64_rep", [128, TOPK], F32, kind="ExternalInput").ap()
    h["maskadd"] = nc.dram_tensor("maskadd", [2, 128, 256], F32, kind="ExternalInput").ap()
    h["boseos"] = nc.dram_tensor("boseos", [NH, SEQ], F32, kind="ExternalInput").ap()
    h["y"] = nc.dram_tensor("y", [HALF, DM], F32, kind="ExternalOutput").ap()
    h["qsp"] = nc.dram_tensor("qspill", [NSUB, 128, DM], BF16, kind="Internal").ap()
    h["ctxh"] = nc.dram_tensor("ctxhbm", [NSUB, 128, DM], BF16, kind="Internal").ap()
    h["w1t"] = nc.dram_tensor("w1t_cache", [16, 128, DFF], FP8, kind="Internal").ap()
    h["w2t"] = nc.dram_tensor("w2t_cache", [16, 128, DFF], FP8, kind="Internal").ap()

    with ExitStack() as stack:
        C = Ctx(nc, stack, dbg)
        P = {}
        P["stats_m"] = C.sb("stats_m", [128, NSUBF], F32)
        P["stats_isd"] = C.sb("stats_isd", [128, NSUBF], F32)
        P["stats_nmi"] = C.sb("stats_nmi", [128, NSUBF], F32)
        P["sel_tm"] = C.sb("sel_tm", [128, NSUB * NH], F32)
        P["selb_tm"] = C.sb("selb_tm", [128, NSUB * NH], BF16)
        P["rank_tm"] = C.sb("rank_tm", [128, NSUB * NH], F32)
        P["qgt"] = C.sb("qgt", [128, NH * TOPK], BF16)
        P["gfin"] = C.sb("gfin", [TOPK, NH * DH], BF16)
        P["idb"] = C.sb("c_idb", [128, 128], BF16)
        P["idf"] = C.sb("c_idf", [128, 128], F32)
        P["triu"] = C.sb("c_triu", [128, 128], BF16)
        P["onescb"] = C.sb("c_onescb", [128, 1], BF16)
        P["onesrb"] = C.sb("c_onesrb", [1, 128], BF16)
        P["onescf"] = C.sb("c_onescf", [128, 1], F32)
        P["onesrf"] = C.sb("c_onesrf", [1, 128], F32)
        P["iota64"] = C.sb("c_iota64", [128, TOPK], F32)
        P["eps5"] = C.sb("c_eps5", [128, 1], F32)
        P["eps6"] = C.sb("c_eps6", [128, 1], F32)
        P["swv"] = C.sb("swv", [128, 2], F32)
        P["maskadd"] = C.sb("maskadd_sb", [128, 2 * 256], F32)
        P["bkh"] = C.sb("bkh_sb", [128, NH], F32)

        for nm, hap in (("idb", "idb"), ("idf", "idf"), ("triu", "triu"),
                        ("onescb", "onescb"), ("onesrb", "onesrb"),
                        ("onescf", "onescf"), ("onesrf", "onesrf"),
                        ("iota64", "iota64")):
            C.inc(nc.sync.dma_start(P[nm][:], h[hap]), "cld", 16)
        C.inc(nc.sync.dma_start(P["maskadd"][:, 0:256], h["maskadd"][0]), "cld", 16)
        C.inc(nc.sync.dma_start(P["maskadd"][:, 256:512], h["maskadd"][1]), "cld", 16)
        C.inc(nc.sync.dma_start(P["bkh"][:], h["bkh"]), "cld", 16)
        for eng in (nc.vector, nc.scalar, nc.tensor, nc.gpsimd, nc.sync):
            C.wait(eng, "cld")
        nc.vector.memset(P["eps5"][:], 1.0e-5)
        C.inc(nc.vector.memset(P["eps6"][:], 1.0e-6), "init")
        C.wait(nc.vector, "init")
        C.wait(nc.scalar, "init")

        if "w" in phases:
            phase_w(C, h, P)
            C.barrier_all("pw")
            C.free_scope(("w0", "w1", "winit", "e_w"))
        if "a" in phases:
            phase_a(C, h, P)
            C.barrier_all("pa")
            C.free_scope(("a_", "a2", "a3", "e_a", "pa1", "pa2", "pa3", "dbgo"))
        if "b" in phases:
            phase_b(C, h, P, 0)
            C.barrier_all("pb0")
            C.free_scope(("b0_",))
            phase_b(C, h, P, 1)
            C.barrier_all("pb1")
            C.free_scope(("b1_",))
            C.dbg_dump("qsp", h["qsp"], [NSUB, 128, DM], BF16)
            C.dbg_dump("ctxh", h["ctxh"], [NSUB, 128, DM], BF16)
            C.dbg_dump("gfin", P["gfin"][:], [TOPK, NH * DH], BF16)
        if "c" in phases:
            phase_c(C, h, P)
        else:
            zz = C.sb("zz", [128, DM], F32)
            nc.vector.memset(zz[:], 0.0)
            C.inc(nc.vector.nop(), "zzv")
            C.wait(nc.sync, "zzv")
            for s in range(NSUB):
                C.inc(nc.sync.dma_start(h["y"][128 * s:128 * s + 128, :], zz[:]),
                      "zzo", 16)
            C.wait(nc.sync, "zzo")
    return nc


def phase_w(C, h, P):
    nc = C.nc
    with ExitStack() as st:
        sb = lambda n, s, dt=F32: st.enter_context(nc.sbuf_tensor(n, s, dt))
        blk = [sb(f"w_blk{i}", [128, DFF]) for i in range(2)]
        acc = sb("w_acc", [128, 1])
        red = sb("w_red", [128, 1])
        tsum = sb("w_tsum", [1, 1])
        asb = sb("w_a", [128, DFF], BF16)
        bsb = sb("w_b", [128, DFF], BF16)
        tern = sb("w_tern", [128, DFF], BF16)
        stage = sb("w_stage", [128, DFF], BF16)
        stage8 = sb("w_stage8", [128, DFF], FP8)
        halfp = sb("w_halfp", [128, 1])
        halfn = sb("w_halfn", [128, 1])
        sum_ps = st.enter_context(nc.psum_tensor("w_sum", [1, 1], F32))
        bc_ps = st.enter_context(nc.psum_tensor("w_bc", [128, 1], F32))
        tr_ps = [st.enter_context(nc.psum_tensor(f"w_tr{i}", [128, 1024], BF16))
                 for i in range(2)]

        nc.vector.memset(halfp[:], 0.5)
        C.inc(nc.vector.memset(halfn[:], -0.5), "winit")
        C.wait(nc.scalar, "winit")

        specs = [(h["w1"], DFF // 128, DM, 0), (h["w2"], DM // 128, DFF, 1)]
        parts = sb("w_parts", [128, 64])
        for src, nb, ncol, wi in specs:
            if wi == 1:
                C.wait(nc.sync, "w0v")
                C.wait(nc.vector, "w0t1")
            for b in range(nb):
                buf = blk[b % 2]
                if b >= 2:
                    C.wait(nc.sync, "w0v", C.cnt["w0v"] - 1)
                C.inc(nc.sync.dma_start(buf[:, 0:ncol],
                                        src[128 * b:128 * b + 128, :]), "w0d", 16)
                C.wait(nc.vector, "w0d")
                C.inc(nc.vector.tensor_reduce(parts[:, b:b + 1], buf[:, 0:ncol],
                                              AXL.X, AX.add,
                                              apply_absolute_value=True), "w0v")
            C.wait(nc.vector, "w0v")
            C.inc(nc.vector.tensor_reduce(acc[:], parts[:, 0:nb], AXL.X, AX.add),
                  "w0r")
            C.wait(nc.tensor, "w0r")
            C.inc(nc.tensor.matmul(sum_ps[:], P["onescf"][:], acc[:],
                                   start=True, stop=True), "w0t1")
            C.wait(nc.vector, "w0t1")
            C.inc(nc.vector.tensor_copy(tsum[:], sum_ps[:]), "w0ts")
            C.wait(nc.tensor, "w0ts")
            C.inc(nc.tensor.matmul(bc_ps[:], P["onesrf"][:], tsum[:],
                                   start=True, stop=True), "w0t2")
            C.wait(nc.vector, "w0t2")
            ntot = float(nb * 128 * ncol)
            C.inc(nc.vector.tensor_scalar(red[:], bc_ps[:], 1.0 / ntot, 1.0e-5,
                                          AX.mult, AX.max), "w0m")
            C.wait(nc.vector, "w0m")
            C.inc(nc.vector.reciprocal(P["swv"][:, wi:wi + 1], red[:]), "w0sw")
        C.wait(nc.scalar, "w0sw")

        C.wait(nc.sync, "w0v")
        for src, nb, ncol, wi in specs:
            if wi == 1:
                C.wait(nc.sync, "w1s")
            nkc = ncol // 128
            group = 4 if wi == 0 else 1
            cache = h["w1t"] if wi == 0 else h["w2t"]
            sw_col = P["swv"][:, wi:wi + 1]
            for b in range(nb):
                buf = blk[b % 2]
                if b >= 2:
                    C.wait(nc.sync, "w1s", C.cnt["w1s"] - 1)
                C.inc(nc.sync.dma_start(buf[:, 0:ncol],
                                        src[128 * b:128 * b + 128, :]), "w1d", 16)
                C.wait(nc.scalar, "w1d")
                if C.cnt.get("w1v", 0) > 0:
                    C.wait(nc.scalar, "w1v")
                nc.scalar.activation(asb[:, 0:ncol], buf[:, 0:ncol], AF.Sign,
                                     bias=halfn[:], scale=sw_col)
                C.inc(nc.scalar.activation(bsb[:, 0:ncol], buf[:, 0:ncol], AF.Sign,
                                           bias=halfp[:], scale=sw_col), "w1s")
                C.wait(nc.vector, "w1s")
                if C.cnt.get("w1tp", 0) > 0:
                    C.wait(nc.vector, "w1tp")
                C.inc(nc.vector.tensor_add(tern[:, 0:ncol], asb[:, 0:ncol],
                                           bsb[:, 0:ncol]), "w1v")
                C.wait(nc.tensor, "w1v")
                nhalf = nkc // 8
                for half in range(nhalf):
                    pb = tr_ps[half % 2]
                    hglob = C.cnt.get("w1tp", 0)
                    if hglob >= 2:
                        C.wait(nc.tensor, "w1ev", 8 * (hglob - 1))
                    for t in range(8):
                        k = 8 * half + t
                        ins = nc.tensor.transpose(pb[:, 128 * t:128 * t + 128],
                                                  tern[:, 128 * k:128 * k + 128],
                                                  P["idb"][:])
                    C.inc(ins, "w1tp")
                    C.wait(nc.vector, "w1tp")
                    if C.cnt.get("w1c8", 0) > 0:
                        C.wait(nc.vector, "w1c8")
                    for t in range(8):
                        k = 8 * half + t
                        dstc = (512 * k + 128 * (b % group)) if wi == 0 else 128 * k
                        C.inc(nc.vector.tensor_copy(stage[:, dstc:dstc + 128],
                                                    pb[:, 128 * t:128 * t + 128]),
                              "w1ev")
                if b % group == group - 1:
                    C.wait(nc.vector, "w1ev")
                    if C.cnt.get("w1o", 0) > 0:
                        C.wait(nc.vector, "w1o")
                    C.inc(nc.vector.tensor_copy(stage8[:], stage[:]), "w1c8")
                    C.wait(nc.scalar, "w1c8")
                    C.inc(nc.scalar.dma_start(cache[b // group], stage8[:]),
                          "w1o", 16)
        C.wait(nc.sync, "w1o")
        C.barrier_all("pwd")
        C.dbg_dump("swv", P["swv"][:], [128, 2])

        if "w1t" in C.dbg or "w2t" in C.dbg:
            dsb = blk[0]
            dsb8 = stage8
            for nm, cache in (("w1t", h["w1t"]), ("w2t", h["w2t"])):
                if nm not in C.dbg:
                    continue
                d = C.dbg_out(nm, [16, 128, DFF], F32)
                for i in range(16):
                    C.inc(nc.sync.dma_start(dsb8[:], cache[i]), "wdbd" + nm, 16)
                    C.wait(nc.vector, "wdbd" + nm)
                    C.inc(nc.vector.tensor_copy(dsb[:], dsb8[:]), "wdbv" + nm)
                    C.wait(nc.sync, "wdbv" + nm)
                    C.inc(nc.sync.dma_start(d[i], dsb[:]), "wdbo" + nm, 16)
                    C.wait(nc.sync, "wdbo" + nm)


def phase_a(C, h, P):
    nc = C.nc
    with ExitStack() as st:
        sb = lambda n, s, dt=F32: st.enter_context(nc.sbuf_tensor(n, s, dt))
        P["norms"] = sb("norms_h", [NH, SEQ], F32)
        P["sel_h"] = sb("sel_h", [NH, SEQ], F32)
        P["boseos"] = sb("boseos_sb", [NH, SEQ], F32)
        C.inc(nc.sync.dma_start(P["boseos"][:], h["boseos"]), "a_bos", 16)
        C.wait(nc.vector, "a_bos")
        xs = [sb(f"a_x{i}", [128, DM]) for i in range(2)]
        sq = sb("a_sq", [128, DM])
        tq = sb("a_tq", [128, 512])
        usq = sb("a_usq", [128, 512])
        xnb = sb("a_xnb", [128, DM], BF16)
        xnt = sb("a_xnt", [128, DM], BF16)
        qsub = sb("a_qsub", [128, DM], BF16)
        nrm = sb("a_nrm", [128, NH])
        wq_sb = [sb(f"a_wq{k}", [128, DM], BF16) for k in range(16)]
        bq_sb = sb("a_bq", [128, DM])
        lnvar = sb("a_lnvar", [128, NSUBF])
        t1 = sb("a_t1", [128, 1])
        t2 = sb("a_t2", [128, 1])
        t3 = sb("a_t3", [128, 1])
        t4 = sb("a_t4", [128, 1])
        t5 = sb("a_t5", [128, 1])
        t6 = sb("a_t6", [128, 1])
        st1 = ExitStack()
        psq = [st1.enter_context(nc.psum_tensor(f"a_q{i}", [128, 512], F32))
               for i in range(4)]
        ptr = st1.enter_context(nc.psum_tensor("a_tr", [128, 1024], BF16))
        pnrm = st1.enter_context(nc.psum_tensor("a_pnrm", [16, 128], F32))

        for k in range(16):
            C.inc(nc.gpsimd.dma_start(wq_sb[k][:], h["wq"][128 * k:128 * k + 128, :]),
                  "a_wqd", 16)
        C.inc(nc.sync.dma_start(bq_sb[:], h["bq"]), "a_bqd", 16)
        C.wait(nc.tensor, "a_wqd")
        C.wait(nc.vector, "a_bqd")

        for sub in range(NSUBF):
            buf = xs[sub % 2]
            if sub >= 2:
                C.wait(nc.sync, "a_xs", C.cnt["a_xs"] - 1)
                C.wait(nc.sync, "a_xv", C.cnt["a_xv"] - 1)
            C.inc(nc.sync.dma_start(buf[:], h["x"][128 * sub:128 * sub + 128, :]),
                  "a_xd", 16)
            C.wait(nc.vector, "a_xd")
            C.inc(nc.vector.reduce_sum(t1[:], buf[:], AXL.X), "a_xv")
            C.wait(nc.scalar, "a_xd")
            if sub > 0:
                C.wait(nc.scalar, "a_t2r")
                C.wait(nc.scalar, "a_tp")
            C.inc(nc.scalar.square(sq[:], buf[:]), "a_sqs")
            C.wait(nc.vector, "a_xv")
            C.inc(nc.vector.tensor_scalar(P["stats_m"][:, sub:sub + 1], t1[:],
                                          1.0 / DM, None, AX.mult), "a_m")
            C.wait(nc.vector, "a_sqs")
            C.inc(nc.vector.reduce_sum(t2[:], sq[:], AXL.X), "a_t2r")
            C.wait(nc.vector, "a_m")
            m_col = P["stats_m"][:, sub:sub + 1]
            C.inc(nc.vector.tensor_mul(t3[:], m_col, m_col), "a_msq")
            C.wait(nc.vector, "a_t2r")
            C.inc(nc.vector.tensor_scalar(t4[:], t2[:], 1.0 / DM, None, AX.mult),
                  "a_s2m")
            C.wait(nc.vector, "a_msq")
            C.wait(nc.vector, "a_s2m")
            C.inc(nc.vector.tensor_sub(t5[:], t4[:], t3[:]), "a_var")
            C.wait(nc.vector, "a_var")
            nc.vector.tensor_copy(lnvar[:, sub:sub + 1], t5[:])
            C.wait(nc.scalar, "a_var")
            C.inc(nc.scalar.activation(t3[:], t5[:], AF.Sqrt, bias=P["eps5"][:]),
                  "a_sd")
            C.wait(nc.vector, "a_sd")
            C.inc(nc.vector.reciprocal(P["stats_isd"][:, sub:sub + 1], t3[:]),
                  "a_isd")
            C.wait(nc.vector, "a_isd")
            isd_col = P["stats_isd"][:, sub:sub + 1]
            C.inc(nc.vector.tensor_mul(t6[:], m_col, isd_col), "a_mi")
            C.wait(nc.vector, "a_mi")
            C.inc(nc.vector.tensor_scalar(P["stats_nmi"][:, sub:sub + 1], t6[:],
                                          -1.0, None, AX.mult), "a_nmi")
            C.wait(nc.scalar, "a_nmi")
            C.inc(nc.scalar.activation(xnb[:], buf[:], AF.Identity,
                                       bias=P["stats_nmi"][:, sub:sub + 1],
                                       scale=isd_col), "a_xs")
            C.wait(nc.tensor, "a_xs")
            if sub > 0:
                C.wait(nc.tensor, "a_te")
            for hh in range(2):
                if hh == 1:
                    C.wait(nc.tensor, "a_te", C.cnt["a_te"] - 0)
                for t in range(8):
                    k = 8 * hh + t
                    ins = nc.tensor.transpose(ptr[:, 128 * t:128 * t + 128],
                                              xnb[:, 128 * k:128 * k + 128],
                                              P["idb"][:])
                C.inc(ins, "a_tp")
                C.wait(nc.vector, "a_tp")
                for t in range(8):
                    k = 8 * hh + t
                    C.inc(nc.vector.tensor_copy(xnt[:, 128 * k:128 * k + 128],
                                                ptr[:, 128 * t:128 * t + 128]),
                          "a_te")
            C.wait(nc.tensor, "a_te")
            if sub == 0 and "e3" in C.dbg:
                C.barrier_all("e3b")
            if sub > 0:
                C.wait(nc.tensor, "a_tqe")
            for k in range(16):
                for n in range(4):
                    ins = nc.tensor.matmul(psq[n][:], xnt[:, 128 * k:128 * k + 128],
                                           wq_sb[k][:, 512 * n:512 * n + 512],
                                           start=(k == 0), stop=(k == 15))
            C.inc(ins, "a_mm")
            if sub == 0 and "e2" in C.dbg:
                C.barrier_all("e2b")
            C.wait(nc.vector, "a_mm")
            if sub >= 1 and sub <= NSUB:
                C.wait(nc.vector, "a_qo", 16 * min(sub, NSUB))
            for n in range(4):
                if n > 0:
                    C.wait(nc.vector, "a_us", C.cnt["a_us"])
                C.inc(nc.vector.tensor_add(
                    tq[:], psq[n][:], bq_sb[:, 512 * n:512 * n + 512]), "a_tqe")
                C.inc(nc.vector.tensor_copy(qsub[:, 512 * n:512 * n + 512], tq[:]),
                      "a_q2s")
                C.wait(nc.scalar, "a_q2s")
                if C.cnt.get("a_nr", 0) > 0:
                    C.wait(nc.scalar, "a_nr")
                C.inc(nc.scalar.square(usq[:], tq[:]), "a_us")
                C.wait(nc.vector, "a_us")
                if n == 0:
                    C.wait(nc.vector, "a_ntp", C.cnt.get("a_ntp", 0))
                u3 = usq[:].rearrange("p (g d) -> p g d", g=4)
                C.inc(nc.vector.reduce_sum(nrm[:, 4 * n:4 * n + 4], u3, AXL.X),
                      "a_nr")
            if sub < NSUB:
                C.wait(nc.scalar, "a_q2s")
                C.inc(nc.scalar.dma_start(h["qsp"][sub], qsub[:]), "a_qo", 16)
            if sub == 0 and "qsub0" in C.dbg:
                C.wait(nc.gpsimd, "a_q2s", C.cnt["a_q2s"])
                C.dbg_dump("qsub0", qsub[:], [128, DM], BF16)
                C.dbg_dump("xnt0", xnt[:], [128, DM], BF16)
            C.wait(nc.tensor, "a_nr")
            if sub > 0:
                C.wait(nc.tensor, "a_nev", C.cnt["a_nev"])
            C.inc(nc.tensor.transpose(pnrm[:], nrm[:], P["idf"][:]), "a_ntp")
            C.wait(nc.vector, "a_ntp")
            C.inc(nc.vector.tensor_copy(P["norms"][0:16, 128 * sub:128 * sub + 128],
                                        pnrm[0:16, :]), "a_nev")
        C.wait(nc.sync, "a_qo", C.cnt.get("a_qo", 0))
        st1.close()
        C.barrier_all("pa1")
        C.dbg_dump("qspA", h["qsp"], [NSUB, 128, DM], BF16)
        C.dbg_dump("stats_m", P["stats_m"][:], [128, NSUBF])
        C.dbg_dump("lnvar", lnvar[:], [128, NSUBF])
        C.dbg_dump("stats_isd", P["stats_isd"][:], [128, NSUBF])
        C.dbg_dump("norms", P["norms"][:], [NH, SEQ])

        wrk = sb("a_wrk", [NH, SEQ])
        top8 = sb("a_top8", [NH, 8])
        topv = sb("a_topv", [NH, TOPK])
        nc.vector.tensor_copy(wrk[:], P["norms"][:])
        for r in range(8):
            C.inc(nc.vector.max(top8[:], wrk[:]), "a2m")
            C.wait(nc.vector, "a2m")
            C.inc(nc.vector.tensor_copy(topv[:, 8 * r:8 * r + 8], top8[:]), "a2tv")
            if r < 7:
                C.inc(nc.vector.match_replace(wrk[:], top8[:], wrk[:], -1.0e30),
                      "a2m")
                C.wait(nc.vector, "a2m")
        C.wait(nc.vector, "a2tv")
        C.inc(nc.vector.tensor_single_scalar(wrk[:], P["norms"][:],
                                              topv[:, 61:62], AX.is_ge), "a2sr")
        C.wait(nc.vector, "a2sr")
        C.inc(nc.vector.tensor_max(P["sel_h"][:], wrk[:], P["boseos"][:]), "a2s")
        C.wait(nc.vector, "a2s")
        C.wait(nc.tensor, "a2s")

        st2 = ExitStack()
        selp = st2.enter_context(nc.psum_tensor("a_selp", [128, NH], F32))
        rankp = st2.enter_context(nc.psum_tensor("a_rankp", [128, NH], F32))
        totp = st2.enter_context(nc.psum_tensor("a_totp", [1, NH], F32))
        cumA = sb("a_cumA", [1, NH])
        cumB = sb("a_cumB", [1, NH])
        cumb = sb("a_cumb", [1, NH], BF16)
        C.inc(nc.vector.memset(cumA[:], 0.0), "a2ci")
        C.wait(nc.vector, "a2ci")
        C.inc(nc.vector.tensor_copy(cumb[:], cumA[:]), "a2cu")
        C.wait(nc.tensor, "a2cu")
        for s in range(NSUB):
            if s > 0:
                C.wait(nc.tensor, "a2se", C.cnt["a2se"])
            C.inc(nc.tensor.transpose(selp[:], P["sel_h"][0:16, 128 * s:128 * s + 128],
                                      P["idf"][0:16, 0:16]), "a2tp")
            C.wait(nc.vector, "a2tp")
            nc.vector.tensor_copy(P["sel_tm"][:, 16 * s:16 * s + 16], selp[:])
            C.inc(nc.vector.tensor_copy(P["selb_tm"][:, 16 * s:16 * s + 16],
                                        selp[:]), "a2se")
            C.wait(nc.tensor, "a2se")
            if s > 0:
                C.wait(nc.tensor, "a2rv", C.cnt["a2rv"])
            nc.tensor.matmul(rankp[:], P["triu"][:],
                             P["selb_tm"][:, 16 * s:16 * s + 16],
                             start=True, stop=False)
            C.inc(nc.tensor.matmul(rankp[:], P["onesrb"][:], cumb[:],
                                   start=False, stop=True), "a2rk")
            C.inc(nc.tensor.matmul(totp[:], P["onescb"][:],
                                   P["selb_tm"][:, 16 * s:16 * s + 16],
                                   start=True, stop=True), "a2tt")
            C.wait(nc.vector, "a2rk")
            C.wait(nc.vector, "a2tt")
            C.inc(nc.vector.tensor_copy(P["rank_tm"][:, 16 * s:16 * s + 16],
                                        rankp[:]), "a2rv")
            src_c, dst_c = (cumA, cumB) if s % 2 == 0 else (cumB, cumA)
            C.inc(nc.vector.tensor_add(dst_c[:], src_c[:], totp[0:1, :]), "a2ca")
            C.wait(nc.vector, "a2ca")
            C.inc(nc.vector.tensor_copy(cumb[:], dst_c[:]), "a2cu")
            C.wait(nc.tensor, "a2cu")
        st2.close()
        C.barrier_all("pa2")
        C.dbg_dump("sel_tm", P["sel_tm"][:], [128, NSUB * NH])
        C.dbg_dump("rank_tm", P["rank_tm"][:], [128, NSUB * NH])

        C.inc(nc.vector.memset(P["qgt"][:], 0.0), "a3qz")
        C.wait(nc.vector, "a3qz")
        mtb = sb("a_mtb", [128, TOPK], BF16)
        qgs = sb("a_qgs", [TOPK, DH], BF16)
        st3 = ExitStack()
        qg_ps = [st3.enter_context(nc.psum_tensor(f"a_qg{i}", [TOPK, DH], F32))
                 for i in range(4)]
        qgt_ps = st3.enter_context(nc.psum_tensor("a_qgt", [128, TOPK], BF16))
        for sweep in range(4):
            for s in range(NSUB):
                C.wait(nc.sync, "a3fr", C.cnt.get("a3fr", 0))
                C.inc(nc.sync.dma_start(qsub[:], h["qsp"][s]), "a3qd", 16)
                C.wait(nc.vector, "a3qd")
                for hh in range(4):
                    hd = 4 * sweep + hh
                    C.inc(nc.vector.tensor_scalar(
                        mtb[:], P["iota64"][:],
                        P["rank_tm"][:, 16 * s + hd:16 * s + hd + 1],
                        P["sel_tm"][:, 16 * s + hd:16 * s + hd + 1],
                        AX.is_equal, AX.mult), "a3mt")
                    C.wait(nc.tensor, "a3mt")
                    ins = nc.tensor.matmul(qg_ps[hh][:], mtb[:],
                                           qsub[:, 128 * hd:128 * hd + 128],
                                           start=(s == 0), stop=(s == NSUB - 1))
                    C.inc(ins, "a3mb")
                    C.wait(nc.vector, "a3mb")
                C.inc(nc.tensor.nop(), "a3fr")
            C.inc(nc.tensor.nop(), "a3mm")
            C.wait(nc.vector, "a3mm")
            for hh in range(4):
                hd = 4 * sweep + hh
                C.inc(nc.vector.tensor_copy(qgs[:], qg_ps[hh][:]), "a3ev")
                C.wait(nc.tensor, "a3ev")
                if C.cnt.get("a3qg", 0) > 0:
                    C.wait(nc.tensor, "a3qg")
                C.inc(nc.tensor.transpose(qgt_ps[0:128, 0:TOPK], qgs[:],
                                          P["idb"][0:TOPK, 0:TOPK]), "a3tp")
                C.wait(nc.vector, "a3tp")
                C.inc(nc.vector.tensor_scalar(P["qgt"][:, 64 * hd:64 * hd + 64],
                                              qgt_ps[0:128, 0:64],
                                              0.08838834764831845, None, AX.mult),
                      "a3qg")
        st3.close()
        C.barrier_all("pa3")
        C.dbg_dump("qgt", P["qgt"][:], [128, NH * TOPK])


def phase_b(C, h, P, hg):
    nc = C.nc
    s_ = lambda n: f"b{hg}_" + n
    with ExitStack() as st:
        sb = lambda n, sh, dt=F32: st.enter_context(nc.sbuf_tensor(s_(n), sh, dt))
        ps = lambda n, sh, dt=F32: st.enter_context(nc.psum_tensor(s_(n), sh, dt))
        wk_sb = [sb(f"wk{k}", [128, 1024], BF16) for k in range(16)]
        wv_sb = [sb(f"wv{k}", [128, 1024], BF16) for k in range(16)]
        bv_sb = sb("bv", [128, 1024])
        qsb = sb("qsb", [128, 1024], BF16)
        xs = [sb(f"x{i}", [128, DM]) for i in range(2)]
        xnb = sb("xnb", [128, DM], BF16)
        xnt = sb("xnt", [128, 16 * 576], BF16)
        stash = sb("stash", [128, 16 * 64], BF16)
        kt = [sb(f"kt{i}", [128, 8 * 576], BF16) for i in range(2)]
        kt0 = sb("ktz", [128, 8 * 576], BF16)
        vt = [sb(f"vt{i}", [128, 1024], BF16) for i in range(9)]
        qt_t = sb("qt_t", [128, 128], BF16)
        probs = sb("probs", [128, 256], BF16)
        probsT = sb("probsT", [128, 256], BF16)
        pgb = sb("pgb", [TOPK, 512], BF16)
        pgbT = sb("pgbT", [128, 256], BF16)
        ctx_st = [sb(f"cst{i}", [128, 1024], BF16) for i in range(2)]
        maskmul = sb("mskm", [128, 2 * 256], BF16)
        probs2 = sb("probs2", [128, 256], BF16)
        ncolt = sb("ncolt", [128, 12])
        nmx = ncolt[:, 0:1]
        ssum = ncolt[:, 1:2]
        rsum = ncolt[:, 2:3]
        gmc = ncolt[0:TOPK, 3:4]
        gda = ncolt[0:TOPK, 4:5]
        gal = ncolt[0:TOPK, 5:6]
        gnn = ncolt[0:TOPK, 6:7]
        gcs = ncolt[0:TOPK, 7:8]
        grr = ncolt[0:TOPK, 8:9]
        rm = [sb(f"rm{i}", [TOPK, 8]) for i in range(2)]
        rs = [sb(f"rs{i}", [TOPK, 8]) for i in range(2)]
        gc = [sb(f"gc{i}", [TOPK, 1024]) for i in range(2)]
        ptrb = [ps(f"ptr{i}", [128, 1024], BF16) for i in range(2)]
        pk = ps("pk", [128, 512])
        pv0 = ps("pv0", [128, 512])
        pv1 = ps("pv1", [128, 512])
        psc = ps("psc", [128, 512])
        pctx = ps("pctx", [128, 128])
        ppv = ps("ppv", [TOPK, DH])

        def tp_into(i, pairs, ident):
            C.wait(nc.tensor, s_(f"pe{i}"), C.cnt.get(s_(f"pe{i}"), 0))
            ins = None
            for dst, srcp in pairs:
                ins = nc.tensor.transpose(dst, srcp, ident)
            return ins

        C.inc(nc.vector.tensor_scalar(maskmul[:], P["maskadd"][:], -0.5e9, None,
                                      AX.is_ge), s_("mm"))
        nc.vector.memset(rm[0][:], -1.0e30)
        nc.vector.memset(rs[0][:], 0.0)
        C.inc(nc.vector.memset(gc[0][:], 0.0), s_("fi"))
        C.wait(nc.vector, s_("fi"))
        C.wait(nc.scalar, s_("fi"))

        for k in range(16):
            C.inc(nc.gpsimd.dma_start(
                wk_sb[k][:], h["wk"][128 * k:128 * k + 128,
                                     1024 * hg:1024 * hg + 1024]), s_("wd"), 16)
            C.inc(nc.gpsimd.dma_start(
                wv_sb[k][:], h["wv"][128 * k:128 * k + 128,
                                     1024 * hg:1024 * hg + 1024]), s_("wd"), 16)
        C.inc(nc.sync.dma_start(bv_sb[:], h["bv"][0:128, 1024 * hg:1024 * hg + 1024]),
              s_("bvd"), 16)
        C.wait(nc.tensor, s_("wd"))
        C.wait(nc.vector, s_("bvd"))

        fcnt = [0]
        vslot = lambda m: m if m <= 3 else 4 + ((m - 4) % 5)

        def xn_rebuild(c):
            if c > 0:
                C.wait(nc.vector, s_("kmm"), C.cnt.get(s_("kmm"), 0))
                C.wait(nc.vector, s_("vmm"), C.cnt.get(s_("vmm"), 0))
            if c == 0:
                for k in range(16):
                    C.inc(nc.vector.memset(xnt[:, 576 * k:576 * k + 64], 0.0),
                          s_("xe"))
            else:
                for k in range(16):
                    C.inc(nc.vector.tensor_copy(xnt[:, 576 * k:576 * k + 64],
                                                stash[:, 64 * k:64 * k + 64]),
                          s_("xe"))
            for ls in range(4):
                sub = 4 * c + ls
                buf = xs[ls % 2]
                if 4 * c + ls >= 2:
                    C.wait(nc.sync, s_("xs"), C.cnt[s_("xs")] - 1)
                C.inc(nc.sync.dma_start(buf[:], h["x"][128 * sub:128 * sub + 128, :]),
                      s_("xd"), 16)
                C.wait(nc.scalar, s_("xd"))
                if C.cnt.get(s_("xtp"), 0) > 0:
                    C.wait(nc.scalar, s_("xtp"))
                C.inc(nc.scalar.activation(xnb[:], buf[:], AF.Identity,
                                           bias=P["stats_nmi"][:, sub:sub + 1],
                                           scale=P["stats_isd"][:, sub:sub + 1]),
                      s_("xs"))
                C.wait(nc.tensor, s_("xs"))
                for hh2 in range(2):
                    prs = [(ptrb[hh2][:, 128 * t:128 * t + 128],
                            xnb[:, 128 * (8 * hh2 + t):128 * (8 * hh2 + t) + 128])
                           for t in range(8)]
                    ins = tp_into(hh2, prs, P["idb"][:])
                    C.inc(ins, s_("xtp"))
                    C.wait(nc.vector, s_("xtp"))
                    for t in range(8):
                        k = 8 * hh2 + t
                        cc = nc.vector.tensor_copy(
                            xnt[:, 576 * k + 64 + 128 * ls:
                                576 * k + 64 + 128 * ls + 128],
                            ptrb[hh2][:, 128 * t:128 * t + 128])
                        C.inc(cc, s_(f"pe{hh2}"))
            C.wait(nc.vector, s_("xe"))
            C.wait(nc.vector, s_("pe0"))
            C.wait(nc.vector, s_("pe1"))
            for k in range(16):
                C.inc(nc.vector.tensor_copy(stash[:, 64 * k:64 * k + 64],
                                            xnt[:, 576 * k + 512:576 * k + 576]),
                      s_("sst"))
            C.inc(nc.vector.nop(), s_("xrd"))
            C.wait(nc.tensor, s_("xrd"))

        def k_proj(c):
            slot = kt[c % 2]
            if c >= 2:
                C.wait(nc.scalar, s_("lsc"), C.cnt.get(s_("lsc"), 0))
                C.wait(nc.scalar, s_("gsc"), C.cnt.get(s_("gsc"), 0))
            for hh in range(8):
                hd = 8 * hg + hh
                if C.cnt.get(s_("kev"), 0) > 0:
                    C.wait(nc.tensor, s_("kev"), C.cnt[s_("kev")])
                C.wait(nc.tensor, s_("lex"), C.cnt.get(s_("lex"), 0))
                C.wait(nc.tensor, s_("gex"), C.cnt.get(s_("gex"), 0))
                for k in range(16):
                    nc.tensor.matmul(pk[:], wk_sb[k][:, 128 * hh:128 * hh + 128],
                                     xnt[:, 576 * k + 64:576 * k + 576],
                                     start=(k == 0), stop=(k == 15))
                    ins = nc.tensor.matmul(psc[0:128, 0:64],
                                           wk_sb[k][:, 128 * hh:128 * hh + 128],
                                           xnt[:, 576 * k:576 * k + 64],
                                           start=(k == 0), stop=(k == 15))
                C.inc(ins, s_("kmm"))
                C.wait(nc.scalar, s_("kmm"))
                nc.scalar.activation(slot[:, 576 * hh + 64:576 * hh + 576],
                                     pk[:], AF.Identity,
                                     bias=P["bkh"][:, hd:hd + 1])
                a2 = nc.scalar.activation(slot[:, 576 * hh:576 * hh + 64],
                                          psc[0:128, 0:64], AF.Identity,
                                          bias=P["bkh"][:, hd:hd + 1])
                C.inc(a2, s_("kev"))
                if c == 0:
                    C.inc(nc.scalar.copy(kt0[:, 576 * hh:576 * hh + 576],
                                         slot[:, 576 * hh:576 * hh + 576]),
                          s_("kev"))

        def v_proj(c):
            if c >= 2:
                C.wait(nc.vector, s_("pve"), C.cnt.get(s_("pve"), 0))
                C.wait(nc.vector, s_("gpv"), C.cnt.get(s_("gpv"), 0))
            for i in range(4):
                m = 4 * c + i
                slot = vt[vslot(m)]
                if C.cnt.get(s_("vev"), 0) > 0:
                    C.wait(nc.tensor, s_("vev"), C.cnt[s_("vev")])
                for k in range(16):
                    nc.tensor.matmul(pv0[:],
                                     xnt[:, 576 * k + 128 * i:576 * k + 128 * i + 128],
                                     wv_sb[k][:, 0:512], start=(k == 0), stop=(k == 15))
                    ins = nc.tensor.matmul(pv1[:],
                                           xnt[:, 576 * k + 128 * i:576 * k + 128 * i + 128],
                                           wv_sb[k][:, 512:1024],
                                           start=(k == 0), stop=(k == 15))
                C.inc(ins, s_("vmm"))
                C.wait(nc.vector, s_("vmm"))
                nc.vector.tensor_add(slot[:, 0:512], pv0[:], bv_sb[:, 0:512])
                C.inc(nc.vector.tensor_add(slot[:, 512:1024], pv1[:],
                                           bv_sb[:, 512:1024]), s_("vev"))
            if c == 7:
                C.wait(nc.tensor, s_("vev"), C.cnt[s_("vev")])
                for k in range(16):
                    nc.tensor.matmul(pv0[0:64, :], xnt[:, 576 * k + 512:576 * k + 576],
                                     wv_sb[k][:, 0:512], start=(k == 0), stop=(k == 15))
                    ins = nc.tensor.matmul(pv1[0:64, :],
                                           xnt[:, 576 * k + 512:576 * k + 576],
                                           wv_sb[k][:, 512:1024],
                                           start=(k == 0), stop=(k == 15))
                C.inc(ins, s_("vmm"))
                C.wait(nc.vector, s_("vmm"))
                nc.vector.tensor_add(vt[0][0:64, 0:512], pv0[0:64, :],
                                     bv_sb[0:64, 0:512])
                C.inc(nc.vector.tensor_add(vt[0][0:64, 512:1024], pv1[0:64, :],
                                           bv_sb[0:64, 512:1024]), s_("vev"))
                for hh in range(8):
                    hd = 8 * hg + hh
                    C.wait(nc.tensor, s_("kev"), C.cnt[s_("kev")])
                    for k in range(16):
                        ins = nc.tensor.matmul(
                            psc[0:128, 0:64],
                            wk_sb[k][:, 128 * hh:128 * hh + 128],
                            xnt[:, 576 * k + 512:576 * k + 576],
                            start=(k == 0), stop=(k == 15))
                    C.inc(ins, s_("kmm"))
                    C.wait(nc.scalar, s_("kmm"))
                    C.inc(nc.scalar.activation(kt0[:, 576 * hh:576 * hh + 64],
                                               psc[0:128, 0:64], AF.Identity,
                                               bias=P["bkh"][:, hd:hd + 1]),
                          s_("kev"))

        def kt_cols(tok0, tok1, hh, use_kt0):
            pieces = []
            t = tok0
            while t < tok1:
                cprime = (t + 64) // 512
                ten = kt0 if (cprime == 0 and use_kt0) else kt[cprime % 2]
                base = 512 * cprime - 64
                hi = min(tok1, base + 576)
                pieces.append((ten, 576 * hh + (t - base), 576 * hh + (hi - base),
                               t - tok0))
                t = hi
            return pieces

        def local_qb(qb, use_kt0=False):
            if C.cnt.get(s_("qtp"), 0) > 0:
                C.wait(nc.sync, s_("qtp"))
            C.inc(nc.sync.dma_start(qsb[:],
                                    h["qsp"][qb, 0:128, 1024 * hg:1024 * hg + 1024]),
                  s_("qd"), 16)
            C.wait(nc.tensor, s_("qd"))
            for hh in range(8):
                hd = 8 * hg + hh
                prs = [(ptrb[0][:, 0:128],
                        qsb[:, 128 * hh:128 * hh + 128])]
                ins = tp_into(0, prs, P["idb"][:])
                C.inc(ins, s_("qtp"))
                C.wait(nc.vector, s_("qtp"))
                cc = nc.vector.tensor_scalar(qt_t[:], ptrb[0][:, 0:128],
                                             0.08838834764831845, None, AX.mult)
                C.inc(cc, s_("pe0"))
                C.wait(nc.tensor, s_("pe0"))
                C.wait(nc.tensor, s_("kev"), C.cnt.get(s_("kev"), 0))
                C.wait(nc.tensor, s_("lex"), C.cnt.get(s_("lex"), 0))
                C.wait(nc.tensor, s_("gex"), C.cnt.get(s_("gex"), 0))
                qs = 128 * qb
                for ten, c0, c1, d0 in kt_cols(qs - 64, qs + 192, hh, use_kt0):
                    ins = nc.tensor.matmul(psc[0:128, d0:d0 + (c1 - c0)], qt_t[:],
                                           ten[:, c0:c1], start=True, stop=True)
                C.inc(ins, s_("lsc"))
                C.wait(nc.vector, s_("lsc"))
                C.inc(nc.vector.reduce_max(nmx, psc[0:128, 0:256], AXL.X,
                                           negate=True), s_("lmx"))
                C.wait(nc.scalar, s_("lmx"))
                C.wait(nc.scalar, s_("ptp"), C.cnt.get(s_("ptp"), 0))
                masked = qb in (0, 15)
                e = nc.scalar.activation(probs[:], psc[0:128, 0:256], AF.Exp,
                                         bias=nmx)
                C.inc(e, s_("lex"))
                C.wait(nc.vector, s_("lex"))
                psrc = probs
                if masked:
                    moff = 0 if qb == 0 else 256
                    C.inc(nc.vector.tensor_mul(probs2[:], probs[:],
                                               maskmul[:, moff:moff + 256]),
                          s_("lpm"))
                    C.wait(nc.vector, s_("lpm"))
                    psrc = probs2
                C.inc(nc.vector.reduce_sum(ssum, psrc[:], AXL.X), s_("lsm"))
                C.wait(nc.vector, s_("lsm"))
                C.inc(nc.vector.reciprocal(rsum, ssum), s_("lrc"))
                C.wait(nc.tensor, s_("lex"))
                if masked:
                    C.wait(nc.tensor, s_("lpm"))
                prs = [(ptrb[1][:, 128 * i2:128 * i2 + 128],
                        psrc[:, 128 * i2:128 * i2 + 128]) for i2 in range(2)]
                ins = tp_into(1, prs, P["idb"][:])
                C.inc(ins, s_("ptp"))
                C.wait(nc.vector, s_("ptp"))
                cc = nc.vector.tensor_copy(probsT[:], ptrb[1][:, 0:256])
                C.inc(cc, s_("pe1"))
                C.wait(nc.tensor, s_("pe1"))
                C.wait(nc.tensor, s_("vev"), C.cnt.get(s_("vev"), 0))
                C.wait(nc.tensor, s_("cev"), C.cnt.get(s_("cev"), 0))
                nc.tensor.matmul(pctx[:], probsT[:, 0:128],
                                 vt[vslot(qb)][:, 128 * hh:128 * hh + 128],
                                 start=True, stop=False)
                C.inc(nc.tensor.matmul(pctx[:], probsT[:, 128:256],
                                       vt[vslot(qb + 1)][:, 128 * hh:128 * hh + 128],
                                       start=False, stop=True), s_("pve"))
                C.wait(nc.vector, s_("pve"))
                C.wait(nc.vector, s_("lrc"))
                cst = ctx_st[qb % 2]
                if hh == 0 and C.cnt.get(s_("cso"), 0) >= 32:
                    C.wait(nc.vector, s_("cso"), C.cnt[s_("cso")] - 16)
                C.inc(nc.vector.tensor_scalar(cst[:, 128 * hh:128 * hh + 128],
                                              pctx[:], rsum, None, AX.mult),
                      s_("cev"))
            C.wait(nc.scalar, s_("cev"))
            C.inc(nc.scalar.dma_start(
                h["ctxh"][qb, 0:128, 1024 * hg:1024 * hg + 1024],
                ctx_st[qb % 2][:]), s_("cso"), 16)

        def flash_step(fc, use_kt0=False):
            p = fcnt[0] % 2
            for hh in range(8):
                hd = 8 * hg + hh
                ten = kt0 if use_kt0 else kt[fc % 2]
                C.wait(nc.tensor, s_("gex"), C.cnt.get(s_("gex"), 0))
                C.wait(nc.tensor, s_("lex"), C.cnt.get(s_("lex"), 0))
                C.wait(nc.tensor, s_("kev"), C.cnt.get(s_("kev"), 0))
                C.inc(nc.tensor.matmul(psc[0:TOPK, 0:512],
                                       P["qgt"][:, TOPK * hd:TOPK * hd + TOPK],
                                       ten[:, 576 * hh:576 * hh + 512],
                                       start=True, stop=True), s_("gsc"))
                C.wait(nc.vector, s_("gsc"))
                C.inc(nc.vector.reduce_max(gmc, psc[0:TOPK, 0:512], AXL.X),
                      s_("gmx"))
                C.wait(nc.vector, s_("gmx"))
                C.inc(nc.vector.tensor_max(rm[1 - p][:, hh:hh + 1],
                                           rm[p][:, hh:hh + 1], gmc), s_("gnm"))
                C.wait(nc.vector, s_("gnm"))
                C.inc(nc.vector.tensor_sub(gda, rm[p][:, hh:hh + 1],
                                           rm[1 - p][:, hh:hh + 1]), s_("gda"))
                C.inc(nc.vector.tensor_scalar(gnn, rm[1 - p][:, hh:hh + 1],
                                              -1.0, None, AX.mult), s_("gnn"))
                C.wait(nc.scalar, s_("gda"))
                C.inc(nc.scalar.activation(gal, gda, AF.Exp), s_("gal"))
                C.wait(nc.scalar, s_("gnn"))
                C.wait(nc.scalar, s_("gtp"), C.cnt.get(s_("gtp"), 0))
                e = nc.scalar.activation(pgb[:], psc[0:TOPK, 0:512], AF.Exp,
                                         bias=gnn)
                C.inc(e, s_("gex"))
                C.wait(nc.vector, s_("gex"))
                C.inc(nc.vector.reduce_sum(gcs, pgb[:], AXL.X), s_("gcr"))
                C.wait(nc.vector, s_("gcr"))
                C.wait(nc.vector, s_("gal"))
                C.inc(nc.vector.scalar_tensor_tensor(
                    rs[1 - p][:, hh:hh + 1], rs[p][:, hh:hh + 1], gal, gcs,
                    AX.mult, AX.add), s_("grs"))
                prs = [(ptrb[1][:, 64 * i2:64 * i2 + 64],
                        pgb[:, 128 * i2:128 * i2 + 128]) for i2 in range(4)]
                C.wait(nc.tensor, s_("gex"))
                ins = tp_into(1, prs, P["idb"][0:TOPK, 0:TOPK])
                C.inc(ins, s_("gtp"))
                C.wait(nc.vector, s_("gtp"))
                cc = nc.vector.tensor_copy(pgbT[:], ptrb[1][:, 0:256])
                C.inc(cc, s_("pe1"))
                C.wait(nc.tensor, s_("pe1"))
                C.wait(nc.tensor, s_("vev"), C.cnt.get(s_("vev"), 0))
                for i2 in range(4):
                    m = 4 * fc + i2
                    ins = nc.tensor.matmul(ppv[:], pgbT[:, 64 * i2:64 * i2 + 64],
                                           vt[vslot(m)][:, 128 * hh:128 * hh + 128],
                                           start=(i2 == 0), stop=(i2 == 3))
                C.inc(ins, s_("gpv"))
                C.wait(nc.vector, s_("gpv"))
                C.inc(nc.vector.scalar_tensor_tensor(
                    gc[1 - p][:, 128 * hh:128 * hh + 128],
                    gc[p][:, 128 * hh:128 * hh + 128], gal, ppv[:],
                    AX.mult, AX.add), s_("ggc"))
                C.wait(nc.tensor, s_("ggc"))
            fcnt[0] += 1

        ready = {c: [] for c in range(NCH)}
        for qb in range(1, NSUB):
            ready[(128 * qb + 191) // 512].append(qb)

        for c in range(NCH):
            xn_rebuild(c)
            k_proj(c)
            v_proj(c)
            if c >= 1:
                flash_step(c)
            for qb in ready[c]:
                local_qb(qb)
        flash_step(0, use_kt0=True)
        local_qb(0, use_kt0=True)

        pfin = fcnt[0] % 2
        for hh in range(8):
            hd = 8 * hg + hh
            C.inc(nc.vector.reciprocal(grr, rs[pfin][:, hh:hh + 1]), s_("gfr"))
            C.wait(nc.vector, s_("gfr"))
            C.inc(nc.vector.tensor_scalar(P["gfin"][:, 128 * hd:128 * hd + 128],
                                          gc[pfin][:, 128 * hh:128 * hh + 128],
                                          grr, None, AX.mult), s_("gfw"))
        C.wait(nc.sync, s_("cso"))
        C.wait(nc.sync, s_("gfw"))
        if hg == 0 and "ktz" in C.dbg:
            C.wait(nc.gpsimd, s_("kev"), C.cnt[s_("kev")])
            C.wait(nc.gpsimd, s_("vev"), C.cnt[s_("vev")])
            C.dbg_dump("ktz", kt0[:], [128, 8 * 576], BF16)
            C.dbg_dump("kta", kt[0][:], [128, 8 * 576], BF16)
            C.dbg_dump("ktb", kt[1][:], [128, 8 * 576], BF16)
            for i in range(4):
                C.dbg_dump(f"vt{i}", vt[i][:], [128, 1024], BF16)


def phase_c(C, h, P):
    nc = C.nc
    s_ = lambda n: "c_" + n
    SC = 1.0 / float(DM)
    with ExitStack() as st:
        sb = lambda n, sh, dt=F32: st.enter_context(nc.sbuf_tensor(s_(n), sh, dt))
        ps = lambda n, sh, dt=F32: st.enter_context(nc.psum_tensor(s_(n), sh, dt))
        r_sb = sb("r", [128, DFF])
        xqt = sb("xqt", [128, 16 * 256], BF16)
        x2qt = sb("x2qt", [128, 64 * 256], BF16)
        w1sb = sb("w1sb", [128, DFF], BF16)
        w2sb = sb("w2sb", [128, DFF], BF16)
        g2b = sb("g2b", [128, DFF], BF16)
        g1b = sb("g1b", [128, DM], BF16)
        xsub = [sb(f"xs{i}", [128, DM]) for i in range(2)]
        ctxb = [sb(f"cb{i}", [128, DM], BF16) for i in range(2)]
        accp = [sb(f"acc{i}", [128, DM]) for i in range(2)]
        yst = [sb(f"yst{i}", [128, DM]) for i in range(2)]
        scrA = sb("scrA", [128, 512])
        scrB = sb("scrB", [128, 512])
        scrC = sb("scrC", [128, 512])
        scrI = sb("scrI", [128, 512], I32)
        qs_bf = sb("qs_bf", [128, 512], BF16)
        mtb = sb("mtb", [128, TOPK], BF16)
        mbf = sb("mbf", [TOPK, 128], BF16)
        mxp = sb("mxp", [128, 16])
        s4p = sb("s4p", [128, 16])
        selbar = sb("selbar", [128, NH])
        n_ = {}
        for nm in ("u1 u2 u3 rms1 rr1 mxa mxn mxn2 mxc rmx s1 cA colsc d1 rdc c1 "
                   "c2 c4 tt tt2 rms2 rr2 m2a m2b m2c rm2 s2 cs2a cs2b d2 rd2 c2f "
                   "mxu s4").split():
            n_[nm] = sb("n_" + nm, [128, 1])
        c1c = sb("c1c", [128, 2])
        c2fc = sb("c2fc", [128, 2])
        ptrb = ps("ptrb", [128, 1024], BF16)
        ptrf = ps("ptrf", [128, 512])
        g = [ps(f"g{i}", [128, 512]) for i in range(6)]

        C.inc(nc.gpsimd.dma_start(g2b[:], h["g2"][0:128, :]), s_("gld"), 16)
        C.inc(nc.gpsimd.dma_start(g1b[:], h["g1"][0:128, :]), s_("gld"), 16)
        C.wait(nc.vector, s_("gld"))

        def gate(ins, tag, eng):
            C.inc(ins, s_(tag))
            C.wait(eng, s_(tag))

        def tp_into(pairs, ident, pbank):
            C.wait(nc.tensor, s_("pe"), C.cnt.get(s_("pe"), 0))
            ins = None
            for dst, srcp in pairs:
                ins = nc.tensor.transpose(dst, srcp, ident)
            return ins

        for T in range(NCH):
            for ls in range(2):
                sub = 2 * T + ls
                if T > 0:
                    C.wait(nc.sync, s_("acc"), C.cnt.get(s_("acc"), 0))
                    C.wait(nc.sync, s_("bl"), C.cnt.get(s_("bl"), 0))
                C.inc(nc.sync.dma_start(xsub[ls][:],
                                        h["x"][128 * sub:128 * sub + 128, :]),
                      s_("xd"), 16)
                C.inc(nc.sync.dma_start(ctxb[ls][:], h["ctxh"][sub]), s_("cd"), 16)
            for ls in range(2):
                sub = 2 * T + ls
                C.wait(nc.vector, s_("xd"), 16 * (2 * T + ls + 1))
                for sl in range(4):
                    c0 = 512 * sl
                    gate(nc.vector.scalar_tensor_tensor(
                        scrA[:], xsub[ls][:, c0:c0 + 512],
                        P["stats_m"][:, sub:sub + 1], g1b[:, c0:c0 + 512],
                        AX.subtract, AX.mult), "t1", nc.vector)
                    C.inc(nc.vector.reduce_max(mxp[:, sl:sl + 1], scrA[:], AXL.X,
                                               apply_absolute_value=True), s_("t2p"))
                C.wait(nc.vector, s_("t2p"))
                gate(nc.vector.reduce_max(n_["mxa"][:], mxp[:, 0:4], AXL.X),
                     "nx", nc.vector)
                isd = P["stats_isd"][:, sub:sub + 1]
                gate(nc.vector.tensor_mul(n_["u1"][:], isd, isd), "nx", nc.vector)
                gate(nc.vector.tensor_scalar(n_["u2"][:], n_["u1"][:], -1.0e-5,
                                             None, AX.mult), "nx", nc.vector)
                gate(nc.vector.tensor_scalar(n_["u3"][:], n_["u2"][:], 1.000001,
                                             None, AX.add), "nx", nc.vector)
                C.wait(nc.scalar, s_("nx"))
                gate(nc.scalar.activation(n_["rms1"][:], n_["u3"][:], AF.Sqrt),
                     "ns", nc.vector)
                gate(nc.vector.reciprocal(n_["rr1"][:], n_["rms1"][:]), "nx", nc.vector)
                gate(nc.vector.tensor_mul(n_["mxn"][:], n_["mxa"][:], isd),
                     "nx", nc.vector)
                gate(nc.vector.tensor_mul(n_["mxn2"][:], n_["mxn"][:], n_["rr1"][:]),
                     "nx", nc.vector)
                gate(nc.vector.tensor_scalar(n_["mxc"][:], n_["mxn2"][:], 1.0e-5,
                                             None, AX.max), "nx", nc.vector)
                gate(nc.vector.reciprocal(n_["rmx"][:], n_["mxc"][:]), "nx", nc.vector)
                gate(nc.vector.tensor_scalar(n_["s1"][:], n_["rmx"][:], 127.0,
                                             None, AX.mult), "nx", nc.vector)
                gate(nc.vector.tensor_mul(n_["cA"][:], isd, n_["rr1"][:]),
                     "nx", nc.vector)
                gate(nc.vector.tensor_mul(n_["colsc"][:], n_["cA"][:], n_["s1"][:]),
                     "nx", nc.vector)
                gate(nc.vector.tensor_mul(n_["d1"][:], n_["s1"][:],
                                          P["swv"][:, 0:1]), "nx", nc.vector)
                gate(nc.vector.reciprocal(n_["rdc"][:], n_["d1"][:]), "nx", nc.vector)
                gate(nc.vector.tensor_scalar(c1c[:, ls:ls + 1], n_["rdc"][:], 0.5,
                                             None, AX.mult), "nx", nc.vector)
                for sl in range(4):
                    c0 = 512 * sl
                    gate(nc.vector.scalar_tensor_tensor(
                        scrB[:], xsub[ls][:, c0:c0 + 512],
                        P["stats_m"][:, sub:sub + 1], g1b[:, c0:c0 + 512],
                        AX.subtract, AX.mult), "t1b", nc.vector)
                    gate(nc.vector.tensor_scalar(scrA[:], scrB[:],
                                                 n_["colsc"][:], MAGIC,
                                                 AX.mult, AX.add), "qv", nc.vector)
                    gate(nc.vector.tensor_scalar(scrI[:], scrA[:], MAGIC, None,
                                                 AX.subtract), "qv", nc.vector)
                    C.wait(nc.scalar, s_("qv"))
                    if C.cnt.get(s_("qtp"), 0) > 0:
                        C.wait(nc.scalar, s_("qtp"))
                    C.inc(nc.scalar.copy(qs_bf[:], scrI[:]), s_("qb"))
                    C.wait(nc.tensor, s_("qb"))
                    prs = [(ptrb[:, 128 * j:128 * j + 128],
                            qs_bf[:, 128 * j:128 * j + 128]) for j in range(4)]
                    ins = tp_into(prs, P["idb"][:], ptrb)
                    C.inc(ins, s_("qtp"))
                    C.wait(nc.vector, s_("qtp"))
                    for j in range(4):
                        k = 4 * sl + j
                        cc = nc.vector.tensor_copy(
                            xqt[:, 256 * k + 128 * ls:256 * k + 128 * ls + 128],
                            ptrb[:, 128 * j:128 * j + 128])
                        C.inc(cc, s_("pe"))
                C.wait(nc.tensor, s_("pe"))
                C.wait(nc.scalar, s_("x2a"), C.cnt.get(s_("x2a"), 0))
                for ffs in range(16):
                    if C.cnt.get(s_("f1m"), 0) > 0:
                        C.wait(nc.gpsimd, s_("f1m"))
                    C.inc(nc.gpsimd.dma_start(w1sb[:], h["w1t"][ffs]), s_("w1d"), 16)
                    C.wait(nc.tensor, s_("w1d"))
                    pb = g[ffs % 2]
                    if C.cnt.get(s_("rel"), 0) > 1:
                        C.wait(nc.tensor, s_("rel"), C.cnt[s_("rel")] - 1)
                    for k in range(16):
                        ins = nc.tensor.matmul(
                            pb[:], xqt[:, 256 * k + 128 * ls:256 * k + 128 * ls + 128],
                            w1sb[:, 512 * k:512 * k + 512],
                            start=(k == 0), stop=(k == 15))
                    C.inc(ins, s_("f1m"))
                    C.wait(nc.scalar, s_("f1m"))
                    C.inc(nc.scalar.activation(r_sb[:, 512 * ffs:512 * ffs + 512],
                                               pb[:], AF.Relu), s_("rel"))
                C.wait(nc.vector, s_("rel"), C.cnt[s_("rel")])
                if T == 0 and ls == 0 and "r0" in C.dbg:
                    C.wait(nc.gpsimd, s_("rel"), C.cnt[s_("rel")])
                    C.dbg_dump("r0", r_sb[:], [128, DFF])
                C.wait(nc.scalar, s_("ytp"), C.cnt.get(s_("ytp"), 0))
                for sl in range(16):
                    c0 = 512 * sl
                    nc.vector.tensor_mul(scrA[:], r_sb[:, c0:c0 + 512], g2b[:, c0:c0 + 512])
                    gate(nc.vector.tensor_mul(scrB[:], scrA[:], r_sb[:, c0:c0 + 512]),
                         "st1", nc.vector)
                    C.inc(nc.vector.reduce_max(mxp[:, sl:sl + 1], scrB[:], AXL.X,
                                               apply_absolute_value=True), s_("st2"))
                    C.wait(nc.scalar, s_("rel"), C.cnt[s_("rel")])
                    C.inc(nc.scalar.square(scrC[:], r_sb[:, c0:c0 + 512]), s_("sq1"))
                    C.wait(nc.scalar, s_("sq1"))
                    C.wait(nc.scalar, s_("st1"), C.cnt.get(s_("st1"), 0))
                    C.inc(nc.scalar.square(scrA[:], scrC[:]), s_("sq2"))
                    C.wait(nc.vector, s_("sq2"))
                    C.inc(nc.vector.reduce_sum(s4p[:, sl:sl + 1], scrA[:], AXL.X),
                          s_("st3"))
                C.wait(nc.vector, s_("st2"))
                C.wait(nc.vector, s_("st3"))
                gate(nc.vector.reduce_max(n_["mxu"][:], mxp[:], AXL.X), "nx", nc.vector)
                gate(nc.vector.reduce_sum(n_["s4"][:], s4p[:], AXL.X), "nx", nc.vector)
                c1 = c1c[:, ls:ls + 1]
                gate(nc.vector.tensor_mul(n_["c2"][:], c1, c1), "nx", nc.vector)
                gate(nc.vector.tensor_mul(n_["c4"][:], n_["c2"][:], n_["c2"][:]),
                     "nx", nc.vector)
                gate(nc.vector.tensor_mul(n_["tt"][:], n_["s4"][:], n_["c4"][:]),
                     "nx", nc.vector)
                gate(nc.vector.tensor_scalar(n_["tt2"][:], n_["tt"][:], 1.0 / DFF,
                                             None, AX.mult), "nx", nc.vector)
                C.wait(nc.scalar, s_("nx"))
                gate(nc.scalar.activation(n_["rms2"][:], n_["tt2"][:], AF.Sqrt,
                                          bias=P["eps6"][:]), "ns", nc.vector)
                gate(nc.vector.reciprocal(n_["rr2"][:], n_["rms2"][:]), "nx", nc.vector)
                gate(nc.vector.tensor_mul(n_["m2a"][:], n_["mxu"][:], n_["c2"][:]),
                     "nx", nc.vector)
                gate(nc.vector.tensor_mul(n_["m2b"][:], n_["m2a"][:], n_["rr2"][:]),
                     "nx", nc.vector)
                gate(nc.vector.tensor_scalar(n_["m2c"][:], n_["m2b"][:], 1.0e-5,
                                             None, AX.max), "nx", nc.vector)
                gate(nc.vector.reciprocal(n_["rm2"][:], n_["m2c"][:]), "nx", nc.vector)
                gate(nc.vector.tensor_scalar(n_["s2"][:], n_["rm2"][:], 127.0,
                                             None, AX.mult), "nx", nc.vector)
                gate(nc.vector.tensor_mul(n_["cs2a"][:], n_["c2"][:], n_["rr2"][:]),
                     "nx", nc.vector)
                gate(nc.vector.tensor_mul(n_["cs2b"][:], n_["cs2a"][:], n_["s2"][:]),
                     "nx", nc.vector)
                gate(nc.vector.tensor_mul(n_["d2"][:], n_["s2"][:], P["swv"][:, 1:2]),
                     "nx", nc.vector)
                gate(nc.vector.reciprocal(n_["rd2"][:], n_["d2"][:]), "nx", nc.vector)
                gate(nc.vector.tensor_scalar(c2fc[:, ls:ls + 1], n_["rd2"][:], 0.5,
                                             None, AX.mult), "nx", nc.vector)
                for sl in range(16):
                    c0 = 512 * sl
                    nc.vector.tensor_mul(scrA[:], r_sb[:, c0:c0 + 512],
                                         g2b[:, c0:c0 + 512])
                    gate(nc.vector.tensor_mul(scrB[:], scrA[:], r_sb[:, c0:c0 + 512]),
                         "x2a", nc.vector)
                    gate(nc.vector.tensor_scalar(scrA[:], scrB[:], n_["cs2b"][:],
                                                 MAGIC, AX.mult, AX.add),
                         "x2b", nc.vector)
                    gate(nc.vector.tensor_scalar(scrI[:], scrA[:], MAGIC, None,
                                                 AX.subtract), "x2c", nc.vector)
                    C.wait(nc.scalar, s_("x2c"))
                    if C.cnt.get(s_("x2tp"), 0) > 0:
                        C.wait(nc.scalar, s_("x2tp"))
                    C.inc(nc.scalar.copy(qs_bf[:], scrI[:]), s_("x2q"))
                    C.wait(nc.tensor, s_("x2q"))
                    prs = [(ptrb[:, 128 * j:128 * j + 128],
                            qs_bf[:, 128 * j:128 * j + 128]) for j in range(4)]
                    ins = tp_into(prs, P["idb"][:], ptrb)
                    C.inc(ins, s_("x2tp"))
                    C.wait(nc.vector, s_("x2tp"))
                    for j in range(4):
                        f = 4 * sl + j
                        cc = nc.vector.tensor_copy(
                            x2qt[:, 256 * f + 128 * ls:256 * f + 128 * ls + 128],
                            ptrb[:, 128 * j:128 * j + 128])
                        C.inc(cc, s_("pe"))
            C.wait(nc.vector, s_("yo"), C.cnt.get(s_("yo"), 0))
            for ls in range(2):
                sub = 2 * T + ls
                gate(nc.vector.tensor_scalar(selbar[:], P["sel_tm"][:, 16 * sub:16 * sub + 16],
                                             -1.0, 1.0, AX.mult, AX.add),
                     "sbv", nc.vector)
                C.wait(nc.vector, s_("cd"), 16 * (2 * T + ls + 1))
                for hd in range(NH):
                    gate(nc.vector.tensor_scalar(
                        mtb[:], P["iota64"][:],
                        P["rank_tm"][:, 16 * sub + hd:16 * sub + hd + 1],
                        P["sel_tm"][:, 16 * sub + hd:16 * sub + hd + 1],
                        AX.is_equal, AX.mult), "mtv", nc.tensor)
                    ins = tp_into([(ptrb[0:TOPK, 0:128], mtb[:])], P["idb"][:], ptrb)
                    C.inc(ins, s_("mtp"))
                    C.wait(nc.vector, s_("mtp"))
                    cc = nc.vector.tensor_copy(mbf[:], ptrb[0:TOPK, 0:128])
                    C.inc(cc, s_("pe"))
                    C.wait(nc.tensor, s_("pe"))
                    if hd == 0:
                        C.wait(nc.tensor, s_("bl"), C.cnt.get(s_("bl"), 0))
                    C.inc(nc.tensor.matmul(
                        g[2 + hd // 4][:, 128 * (hd % 4):128 * (hd % 4) + 128],
                        mbf[:], P["gfin"][:, 128 * hd:128 * hd + 128],
                        start=True, stop=True), s_("cgm"))
                C.wait(nc.vector, s_("cgm"))
                for hd in range(NH):
                    C.inc(nc.vector.scalar_tensor_tensor(
                        yst[ls][:, 128 * hd:128 * hd + 128],
                        ctxb[ls][:, 128 * hd:128 * hd + 128],
                        selbar[:, hd:hd + 1],
                        g[2 + hd // 4][:, 128 * (hd % 4):128 * (hd % 4) + 128],
                        AX.mult, AX.add), s_("bl"))
                C.wait(nc.vector, s_("bl"))
                C.inc(nc.vector.tensor_add(accp[ls][:], yst[ls][:], xsub[ls][:]),
                      s_("acc"))
            C.wait(nc.tensor, s_("pe"))
            if T == 0 and "acc0" in C.dbg:
                C.wait(nc.gpsimd, s_("acc"), C.cnt[s_("acc")])
                C.wait(nc.gpsimd, s_("pe"), C.cnt[s_("pe")])
                C.wait(nc.gpsimd, s_("nx"), C.cnt[s_("nx")])
                C.dbg_dump("acc0", accp[0][:], [128, DM])
                C.dbg_dump("acc1", accp[1][:], [128, DM])
                C.dbg_dump("x2qt0", x2qt[:], [128, 64 * 256], BF16)
                C.dbg_dump("xqt0", xqt[:], [128, 16 * 256], BF16)
                C.dbg_dump("c1c", c1c[:], [128, 2])
                C.dbg_dump("c2fc", c2fc[:], [128, 2])
            C.wait(nc.vector, s_("acc"))
            for b in range(16):
                if C.cnt.get(s_("f2m"), 0) > 0:
                    C.wait(nc.gpsimd, s_("f2m"))
                C.inc(nc.gpsimd.dma_start(w2sb[:], h["w2t"][b]), s_("w2d"), 16)
                C.wait(nc.tensor, s_("w2d"))
                pb = g[b % 2]
                if C.cnt.get(s_("o2e"), 0) > 1:
                    C.wait(nc.tensor, s_("o2e"), C.cnt[s_("o2e")] - 1)
                for k in range(64):
                    ins = nc.tensor.matmul(pb[:, 0:256], w2sb[:, 128 * k:128 * k + 128],
                                           x2qt[:, 256 * k:256 * k + 256],
                                           start=(k == 0), stop=(k == 63))
                C.inc(ins, s_("f2m"))
                C.wait(nc.vector, s_("f2m"))
                C.inc(nc.vector.tensor_copy(scrC[:, 0:256], pb[:, 0:256]), s_("o2e"))
                C.wait(nc.tensor, s_("o2e"))
                C.wait(nc.tensor, s_("yw"), C.cnt.get(s_("yw"), 0))
                for ls in range(2):
                    ins = nc.tensor.transpose(ptrf[:, 128 * ls:128 * ls + 128],
                                              scrC[:, 128 * ls:128 * ls + 128],
                                              P["idf"][:])
                C.inc(ins, s_("ytp"))
                C.wait(nc.vector, s_("ytp"))
                for ls in range(2):
                    cc = nc.vector.scalar_tensor_tensor(
                        yst[ls][:, 128 * b:128 * b + 128],
                        ptrf[:, 128 * ls:128 * ls + 128], c2fc[:, ls:ls + 1],
                        accp[ls][:, 128 * b:128 * b + 128], AX.mult, AX.add)
                    C.inc(cc, s_("yw"))
            C.wait(nc.scalar, s_("yw"))
            for ls in range(2):
                sub = 2 * T + ls
                C.inc(nc.scalar.dma_start(h["y"][128 * sub:128 * sub + 128, :],
                                          yst[ls][:]), s_("yo"), 16)
        C.wait(nc.sync, s_("yo"))


def _consts():
    c = {}
    c["ident_bf"] = np.eye(128, dtype=ml_dtypes.bfloat16)
    c["ident_f32"] = np.eye(128, dtype=np.float32)
    c["triu_bf"] = np.triu(np.ones((128, 128))).astype(ml_dtypes.bfloat16)
    c["ones_col_bf"] = np.ones((128, 1), dtype=ml_dtypes.bfloat16)
    c["ones_row_bf"] = np.ones((1, 128), dtype=ml_dtypes.bfloat16)
    c["ones_col_f"] = np.ones((128, 1), dtype=np.float32)
    c["ones_row_f"] = np.ones((1, 128), dtype=np.float32)
    c["iota64_rep"] = np.ascontiguousarray(
        np.tile(np.arange(1, TOPK + 1, dtype=np.float32), (128, 1)))
    return c


def _stage_core(inputs, core):
    seq = core // 2
    half = core % 2
    off = HALF * half
    x = np.asarray(inputs["L0_attn_hidden_states"][seq])
    m = {}
    m["x_seq"] = np.ascontiguousarray(np.roll(x, -off, axis=0))
    m["Wq"] = np.asarray(inputs["L0_attn_Wq"])
    m["Wk"] = np.asarray(inputs["L0_attn_Wk"])
    m["Wv"] = np.asarray(inputs["L0_attn_Wv"])
    m["w1"] = np.asarray(inputs["L0_ffn_w1"])
    m["w2"] = np.asarray(inputs["L0_ffn_w2"])
    m["bq_rep"] = np.ascontiguousarray(
        np.tile(np.asarray(inputs["L0_attn_bq"])[None, :], (128, 1)))
    m["bv_rep"] = np.ascontiguousarray(
        np.tile(np.asarray(inputs["L0_attn_bv"])[None, :], (128, 1)))
    m["bk_hm"] = np.ascontiguousarray(
        np.asarray(inputs["L0_attn_bk"]).reshape(NH, DH).T)
    m["g1_rep"] = np.ascontiguousarray(
        np.tile(np.asarray(inputs["L0_ffn_g1"])[None, :], (128, 1)))
    m["g2_rep"] = np.ascontiguousarray(
        np.tile(np.asarray(inputs["L0_ffn_g2"])[None, :], (128, 1)))
    m.update(_consts())
    mask = np.zeros((2, 128, 256), dtype=np.float32)
    if half == 0:
        mask[0, :, 0:64] = NEG
    else:
        mask[1, :, 192:256] = NEG
    m["maskadd"] = mask
    be = np.zeros((NH, SEQ), dtype=np.float32)
    be[:, (0 - off) % SEQ] = 1.0
    be[:, (SEQ - 1 - off) % SEQ] = 1.0
    m["boseos"] = be
    return m


def kernel(**inputs):
    nc = build_program(dbg=_DBG, phases=_PHASES)
    in_maps = [_stage_core(inputs, c) for c in range(8)]
    res = run_bass_kernel_spmd(nc, in_maps, core_ids=list(range(8)))
    out = np.empty((4, SEQ, DM), dtype=np.float32)
    for c in range(8):
        seq, half = c // 2, c % 2
        out[seq, HALF * half:HALF * half + HALF] = res.results[c]["y"]
    return out


# version 02
# speedup vs baseline: 1.1249x; 1.1249x over previous
import numpy as np
import ml_dtypes
from contextlib import ExitStack

import concourse.bass as bass
import concourse.mybir as mybir
from concourse.bass_utils import run_bass_kernel_spmd

F32 = mybir.dt.float32
BF16 = mybir.dt.bfloat16
I32 = mybir.dt.int32
FP8 = mybir.dt.float8e4
AX = mybir.AluOpType
AF = mybir.ActivationFunctionType
AXL = mybir.AxisListType

MAGIC = 12582912.0
NEG = -1.0e9
SEQ = 4096
DM = 2048
DFF = 8192
NH = 16
DH = 128
HALF = 2048
NSUB = 16
NSUBF = 32
NCH = 8
TOPK = 64

_PHASES = "wabc"
_DBG = ()


class Ctx:

    def __init__(self, nc, stack, dbg):
        self.nc = nc
        self.stack = stack
        self.dbg = set(dbg)
        self.dbg_dram = {}
        self.sems = {}
        self.cnt = {}
        self.sem("gbar")

    def sem(self, name):
        if name not in self.sems:
            self.sems[name] = self.stack.enter_context(self.nc.semaphore(name))
            self.cnt[name] = 0
        return self.sems[name]

    def inc(self, ins, name, n=1):
        ins.then_inc(self.sem(name), n)
        self.cnt[name] += n
        return self.cnt[name]

    def wait(self, eng, name, val=None):
        eng.wait_ge(self.sem(name), self.cnt[name] if val is None else val)

    def edge(self, eng, tag):
        name = "e_" + tag
        self.inc(eng.nop(), name)
        self.wait(eng, name)

    def barrier_all(self, name):
        nc = self.nc
        engines = [nc.sync, nc.scalar, nc.vector, nc.tensor, nc.gpsimd]
        s = self.sem(name)
        base = self.cnt[name]
        for e in engines:
            e.sem_inc(s, 1)
        self.cnt[name] = base + 5
        for e in engines:
            e.wait_ge(s, base + 5)

    def free_scope(self, prefixes):
        victims = [n for n in self.sems if any(n.startswith(p) for p in prefixes)]
        if not victims:
            return
        self.nc.clear_and_free_semaphores([self.sems[n] for n in victims])
        for n in victims:
            del self.sems[n]
            del self.cnt[n]
        self.barrier_all("gbar")

    def sb(self, name, shape, dt=F32):
        return self.stack.enter_context(self.nc.sbuf_tensor(name, shape, dt))

    def ps(self, name, shape, dt=F32):
        return self.stack.enter_context(self.nc.psum_tensor(name, shape, dt))

    def dbg_out(self, name, shape, dt=F32):
        if name not in self.dbg_dram:
            self.dbg_dram[name] = self.nc.dram_tensor(
                "dbg_" + name, shape, dt, kind="ExternalOutput").ap()
        return self.dbg_dram[name]

    def dbg_dump(self, name, src_ap, shape, dt=F32):
        if name not in self.dbg:
            return
        d = self.dbg_out(name, shape, dt)
        self.inc(self.nc.gpsimd.dma_start(d, src_ap), "dbgo_" + name, 16)
        self.wait(self.nc.gpsimd, "dbgo_" + name)


def build_program(dbg=(), phases=_PHASES):
    nc = bass.Bass("TRN2", target_bir_lowering=False)
    h = {}
    h["x"] = nc.dram_tensor("x_seq", [SEQ, DM], F32, kind="ExternalInput").ap()
    h["wq"] = nc.dram_tensor("Wq", [DM, DM], F32, kind="ExternalInput").ap()
    h["wk"] = nc.dram_tensor("Wk", [DM, DM], F32, kind="ExternalInput").ap()
    h["wv"] = nc.dram_tensor("Wv", [DM, DM], F32, kind="ExternalInput").ap()
    h["w1"] = nc.dram_tensor("w1", [DFF, DM], F32, kind="ExternalInput").ap()
    h["w2"] = nc.dram_tensor("w2", [DM, DFF], F32, kind="ExternalInput").ap()
    h["bq"] = nc.dram_tensor("bq_rep", [128, DM], F32, kind="ExternalInput").ap()
    h["bkh"] = nc.dram_tensor("bk_hm", [128, NH], F32, kind="ExternalInput").ap()
    h["bv"] = nc.dram_tensor("bv_rep", [128, DM], F32, kind="ExternalInput").ap()
    h["g1"] = nc.dram_tensor("g1_rep", [128, DM], F32, kind="ExternalInput").ap()
    h["g2"] = nc.dram_tensor("g2_rep", [128, DFF], F32, kind="ExternalInput").ap()
    h["idb"] = nc.dram_tensor("ident_bf", [128, 128], BF16, kind="ExternalInput").ap()
    h["idf"] = nc.dram_tensor("ident_f32", [128, 128], F32, kind="ExternalInput").ap()
    h["triu"] = nc.dram_tensor("triu_bf", [128, 128], BF16, kind="ExternalInput").ap()
    h["onescb"] = nc.dram_tensor("ones_col_bf", [128, 1], BF16, kind="ExternalInput").ap()
    h["onesrb"] = nc.dram_tensor("ones_row_bf", [1, 128], BF16, kind="ExternalInput").ap()
    h["onescf"] = nc.dram_tensor("ones_col_f", [128, 1], F32, kind="ExternalInput").ap()
    h["onesrf"] = nc.dram_tensor("ones_row_f", [1, 128], F32, kind="ExternalInput").ap()
    h["iota64"] = nc.dram_tensor("iota64_rep", [128, TOPK], F32, kind="ExternalInput").ap()
    h["maskadd"] = nc.dram_tensor("maskadd", [2, 128, 256], F32, kind="ExternalInput").ap()
    h["boseos"] = nc.dram_tensor("boseos", [NH, SEQ], F32, kind="ExternalInput").ap()
    h["y"] = nc.dram_tensor("y", [HALF, DM], F32, kind="ExternalOutput").ap()
    h["qsp"] = nc.dram_tensor("qspill", [NSUB, 128, DM], BF16, kind="Internal").ap()
    h["ctxh"] = nc.dram_tensor("ctxhbm", [NSUB, 128, DM], BF16, kind="Internal").ap()
    h["w1t"] = nc.dram_tensor("w1t_cache", [16, 128, DFF], FP8, kind="Internal").ap()
    h["w2t"] = nc.dram_tensor("w2t_cache", [16, 128, DFF], FP8, kind="Internal").ap()

    with ExitStack() as stack:
        C = Ctx(nc, stack, dbg)
        P = {}
        P["stats_m"] = C.sb("stats_m", [128, NSUBF], F32)
        P["stats_isd"] = C.sb("stats_isd", [128, NSUBF], F32)
        P["stats_nmi"] = C.sb("stats_nmi", [128, NSUBF], F32)
        P["sel_tm"] = C.sb("sel_tm", [128, NSUB * NH], F32)
        P["selb_tm"] = C.sb("selb_tm", [128, NSUB * NH], BF16)
        P["rank_tm"] = C.sb("rank_tm", [128, NSUB * NH], F32)
        P["qgt"] = C.sb("qgt", [128, NH * TOPK], BF16)
        P["gfin"] = C.sb("gfin", [TOPK, NH * DH], BF16)
        P["idb"] = C.sb("c_idb", [128, 128], BF16)
        P["idf"] = C.sb("c_idf", [128, 128], F32)
        P["triu"] = C.sb("c_triu", [128, 128], BF16)
        P["onescb"] = C.sb("c_onescb", [128, 1], BF16)
        P["onesrb"] = C.sb("c_onesrb", [1, 128], BF16)
        P["onescf"] = C.sb("c_onescf", [128, 1], F32)
        P["onesrf"] = C.sb("c_onesrf", [1, 128], F32)
        P["iota64"] = C.sb("c_iota64", [128, TOPK], F32)
        P["eps5"] = C.sb("c_eps5", [128, 1], F32)
        P["eps6"] = C.sb("c_eps6", [128, 1], F32)
        P["swv"] = C.sb("swv", [128, 2], F32)
        P["maskadd"] = C.sb("maskadd_sb", [128, 2 * 256], F32)
        P["bkh"] = C.sb("bkh_sb", [128, NH], F32)

        for nm, hap in (("idb", "idb"), ("idf", "idf"), ("triu", "triu"),
                        ("onescb", "onescb"), ("onesrb", "onesrb"),
                        ("onescf", "onescf"), ("onesrf", "onesrf"),
                        ("iota64", "iota64")):
            C.inc(nc.sync.dma_start(P[nm][:], h[hap]), "cld", 16)
        C.inc(nc.sync.dma_start(P["maskadd"][:, 0:256], h["maskadd"][0]), "cld", 16)
        C.inc(nc.sync.dma_start(P["maskadd"][:, 256:512], h["maskadd"][1]), "cld", 16)
        C.inc(nc.sync.dma_start(P["bkh"][:], h["bkh"]), "cld", 16)
        for eng in (nc.vector, nc.scalar, nc.tensor, nc.gpsimd, nc.sync):
            C.wait(eng, "cld")
        nc.vector.memset(P["eps5"][:], 1.0e-5)
        C.inc(nc.vector.memset(P["eps6"][:], 1.0e-6), "init")
        C.wait(nc.vector, "init")
        C.wait(nc.scalar, "init")

        if "w" in phases:
            phase_w(C, h, P)
            C.barrier_all("pw")
            C.free_scope(("w0", "w1", "winit", "e_w"))
        if "a" in phases:
            phase_a(C, h, P)
            C.barrier_all("pa")
            C.free_scope(("a_", "a2", "a3", "e_a", "pa1", "pa2", "pa3", "dbgo"))
        if "b" in phases:
            phase_b(C, h, P, 0)
            C.barrier_all("pb0")
            C.free_scope(("b0_",))
            phase_b(C, h, P, 1)
            C.barrier_all("pb1")
            C.free_scope(("b1_",))
            C.dbg_dump("qsp", h["qsp"], [NSUB, 128, DM], BF16)
            C.dbg_dump("ctxh", h["ctxh"], [NSUB, 128, DM], BF16)
            C.dbg_dump("gfin", P["gfin"][:], [TOPK, NH * DH], BF16)
        if "c" in phases:
            phase_c(C, h, P)
        else:
            zz = C.sb("zz", [128, DM], F32)
            nc.vector.memset(zz[:], 0.0)
            C.inc(nc.vector.nop(), "zzv")
            C.wait(nc.sync, "zzv")
            for s in range(NSUB):
                C.inc(nc.sync.dma_start(h["y"][128 * s:128 * s + 128, :], zz[:]),
                      "zzo", 16)
            C.wait(nc.sync, "zzo")
    return nc


def phase_w(C, h, P):
    nc = C.nc
    with ExitStack() as st:
        sb = lambda n, s, dt=F32: st.enter_context(nc.sbuf_tensor(n, s, dt))
        blk = [sb(f"w_blk{i}", [128, DFF]) for i in range(2)]
        acc = sb("w_acc", [128, 1])
        red = sb("w_red", [128, 1])
        tsum = sb("w_tsum", [1, 1])
        asb = sb("w_a", [128, DFF], BF16)
        bsb = sb("w_b", [128, DFF], BF16)
        tern = sb("w_tern", [128, DFF], BF16)
        stage = sb("w_stage", [128, DFF], BF16)
        stage8 = sb("w_stage8", [128, DFF], FP8)
        halfp = sb("w_halfp", [128, 1])
        halfn = sb("w_halfn", [128, 1])
        sum_ps = st.enter_context(nc.psum_tensor("w_sum", [1, 1], F32))
        bc_ps = st.enter_context(nc.psum_tensor("w_bc", [128, 1], F32))
        tr_ps = [st.enter_context(nc.psum_tensor(f"w_tr{i}", [128, 1024], BF16))
                 for i in range(2)]

        nc.vector.memset(halfp[:], 0.5)
        C.inc(nc.vector.memset(halfn[:], -0.5), "winit")
        C.wait(nc.scalar, "winit")

        specs = [(h["w1"], DFF // 128, DM, 0), (h["w2"], DM // 128, DFF, 1)]
        parts = sb("w_parts", [128, 64])
        for src, nb, ncol, wi in specs:
            if wi == 1:
                C.wait(nc.sync, "w0v")
                C.wait(nc.vector, "w0t1")
            for b in range(nb):
                buf = blk[b % 2]
                if b >= 2:
                    C.wait(nc.sync, "w0v", C.cnt["w0v"] - 1)
                C.inc(nc.sync.dma_start(buf[:, 0:ncol],
                                        src[128 * b:128 * b + 128, :]), "w0d", 16)
                C.wait(nc.vector, "w0d")
                C.inc(nc.vector.tensor_reduce(parts[:, b:b + 1], buf[:, 0:ncol],
                                              AXL.X, AX.add,
                                              apply_absolute_value=True), "w0v")
            C.wait(nc.vector, "w0v")
            C.inc(nc.vector.tensor_reduce(acc[:], parts[:, 0:nb], AXL.X, AX.add),
                  "w0r")
            C.wait(nc.tensor, "w0r")
            C.inc(nc.tensor.matmul(sum_ps[:], P["onescf"][:], acc[:],
                                   start=True, stop=True), "w0t1")
            C.wait(nc.vector, "w0t1")
            C.inc(nc.vector.tensor_copy(tsum[:], sum_ps[:]), "w0ts")
            C.wait(nc.tensor, "w0ts")
            C.inc(nc.tensor.matmul(bc_ps[:], P["onesrf"][:], tsum[:],
                                   start=True, stop=True), "w0t2")
            C.wait(nc.vector, "w0t2")
            ntot = float(nb * 128 * ncol)
            C.inc(nc.vector.tensor_scalar(red[:], bc_ps[:], 1.0 / ntot, 1.0e-5,
                                          AX.mult, AX.max), "w0m")
            C.wait(nc.vector, "w0m")
            C.inc(nc.vector.reciprocal(P["swv"][:, wi:wi + 1], red[:]), "w0sw")
        C.wait(nc.scalar, "w0sw")

        C.wait(nc.sync, "w0v")
        for src, nb, ncol, wi in specs:
            if wi == 1:
                C.wait(nc.sync, "w1s")
            nkc = ncol // 128
            group = 4 if wi == 0 else 1
            cache = h["w1t"] if wi == 0 else h["w2t"]
            sw_col = P["swv"][:, wi:wi + 1]
            for b in range(nb):
                buf = blk[b % 2]
                if b >= 2:
                    C.wait(nc.sync, "w1s", C.cnt["w1s"] - 1)
                C.inc(nc.sync.dma_start(buf[:, 0:ncol],
                                        src[128 * b:128 * b + 128, :]), "w1d", 16)
                C.wait(nc.scalar, "w1d")
                if C.cnt.get("w1v", 0) > 0:
                    C.wait(nc.scalar, "w1v")
                nc.scalar.activation(asb[:, 0:ncol], buf[:, 0:ncol], AF.Sign,
                                     bias=halfn[:], scale=sw_col)
                C.inc(nc.scalar.activation(bsb[:, 0:ncol], buf[:, 0:ncol], AF.Sign,
                                           bias=halfp[:], scale=sw_col), "w1s")
                C.wait(nc.vector, "w1s")
                if C.cnt.get("w1tp", 0) > 0:
                    C.wait(nc.vector, "w1tp")
                C.inc(nc.vector.tensor_add(tern[:, 0:ncol], asb[:, 0:ncol],
                                           bsb[:, 0:ncol]), "w1v")
                C.wait(nc.tensor, "w1v")
                nhalf = nkc // 8
                for half in range(nhalf):
                    pb = tr_ps[half % 2]
                    hglob = C.cnt.get("w1tp", 0)
                    if hglob >= 2:
                        C.wait(nc.tensor, "w1ev", 8 * (hglob - 1))
                    for t in range(8):
                        k = 8 * half + t
                        ins = nc.tensor.transpose(pb[:, 128 * t:128 * t + 128],
                                                  tern[:, 128 * k:128 * k + 128],
                                                  P["idb"][:])
                    C.inc(ins, "w1tp")
                    C.wait(nc.vector, "w1tp")
                    if C.cnt.get("w1c8", 0) > 0:
                        C.wait(nc.vector, "w1c8")
                    for t in range(8):
                        k = 8 * half + t
                        dstc = (512 * k + 128 * (b % group)) if wi == 0 else 128 * k
                        C.inc(nc.vector.tensor_copy(stage[:, dstc:dstc + 128],
                                                    pb[:, 128 * t:128 * t + 128]),
                              "w1ev")
                if b % group == group - 1:
                    C.wait(nc.vector, "w1ev")
                    if C.cnt.get("w1o", 0) > 0:
                        C.wait(nc.vector, "w1o")
                    C.inc(nc.vector.tensor_copy(stage8[:], stage[:]), "w1c8")
                    C.wait(nc.scalar, "w1c8")
                    C.inc(nc.scalar.dma_start(cache[b // group], stage8[:]),
                          "w1o", 16)
        C.wait(nc.sync, "w1o")
        C.barrier_all("pwd")
        C.dbg_dump("swv", P["swv"][:], [128, 2])

        if "w1t" in C.dbg or "w2t" in C.dbg:
            dsb = blk[0]
            dsb8 = stage8
            for nm, cache in (("w1t", h["w1t"]), ("w2t", h["w2t"])):
                if nm not in C.dbg:
                    continue
                d = C.dbg_out(nm, [16, 128, DFF], F32)
                for i in range(16):
                    C.inc(nc.sync.dma_start(dsb8[:], cache[i]), "wdbd" + nm, 16)
                    C.wait(nc.vector, "wdbd" + nm)
                    C.inc(nc.vector.tensor_copy(dsb[:], dsb8[:]), "wdbv" + nm)
                    C.wait(nc.sync, "wdbv" + nm)
                    C.inc(nc.sync.dma_start(d[i], dsb[:]), "wdbo" + nm, 16)
                    C.wait(nc.sync, "wdbo" + nm)


def phase_a(C, h, P):
    nc = C.nc
    with ExitStack() as st:
        sb = lambda n, s, dt=F32: st.enter_context(nc.sbuf_tensor(n, s, dt))
        P["norms"] = sb("norms_h", [NH, SEQ], F32)
        P["sel_h"] = sb("sel_h", [NH, SEQ], F32)
        P["boseos"] = sb("boseos_sb", [NH, SEQ], F32)
        C.inc(nc.sync.dma_start(P["boseos"][:], h["boseos"]), "a_bos", 16)
        C.wait(nc.vector, "a_bos")
        xs = [sb(f"a_x{i}", [128, DM]) for i in range(2)]
        sq = sb("a_sq", [128, DM])
        tq = sb("a_tq", [128, 512])
        usq = sb("a_usq", [128, 512])
        xnb = sb("a_xnb", [128, DM], BF16)
        xnt = sb("a_xnt", [128, DM], BF16)
        qsub = sb("a_qsub", [128, DM], BF16)
        nrm = sb("a_nrm", [128, NH])
        wq_sb = [sb(f"a_wq{k}", [128, DM], BF16) for k in range(16)]
        bq_sb = sb("a_bq", [128, DM])
        lnvar = sb("a_lnvar", [128, NSUBF])
        t1 = sb("a_t1", [128, 1])
        t2 = sb("a_t2", [128, 1])
        t3 = sb("a_t3", [128, 1])
        t4 = sb("a_t4", [128, 1])
        t5 = sb("a_t5", [128, 1])
        t6 = sb("a_t6", [128, 1])
        st1 = ExitStack()
        psq = [st1.enter_context(nc.psum_tensor(f"a_q{i}", [128, 512], F32))
               for i in range(4)]
        ptr = st1.enter_context(nc.psum_tensor("a_tr", [128, 1024], BF16))
        pnrm = st1.enter_context(nc.psum_tensor("a_pnrm", [16, 128], F32))

        for k in range(16):
            C.inc(nc.gpsimd.dma_start(wq_sb[k][:], h["wq"][128 * k:128 * k + 128, :]),
                  "a_wqd", 16)
        C.inc(nc.sync.dma_start(bq_sb[:], h["bq"]), "a_bqd", 16)
        C.wait(nc.tensor, "a_wqd")
        C.wait(nc.vector, "a_bqd")

        for sub in range(NSUBF):
            buf = xs[sub % 2]
            if sub >= 2:
                C.wait(nc.sync, "a_xs", C.cnt["a_xs"] - 1)
                C.wait(nc.sync, "a_xv", C.cnt["a_xv"] - 1)
            C.inc(nc.sync.dma_start(buf[:], h["x"][128 * sub:128 * sub + 128, :]),
                  "a_xd", 16)
            C.wait(nc.vector, "a_xd")
            C.inc(nc.vector.reduce_sum(t1[:], buf[:], AXL.X), "a_xv")
            C.wait(nc.scalar, "a_xd")
            if sub > 0:
                C.wait(nc.scalar, "a_t2r")
                C.wait(nc.scalar, "a_tp")
            C.inc(nc.scalar.square(sq[:], buf[:]), "a_sqs")
            C.wait(nc.vector, "a_xv")
            C.inc(nc.vector.tensor_scalar(P["stats_m"][:, sub:sub + 1], t1[:],
                                          1.0 / DM, None, AX.mult), "a_m")
            C.wait(nc.vector, "a_sqs")
            C.inc(nc.vector.reduce_sum(t2[:], sq[:], AXL.X), "a_t2r")
            C.wait(nc.vector, "a_m")
            m_col = P["stats_m"][:, sub:sub + 1]
            C.inc(nc.vector.tensor_mul(t3[:], m_col, m_col), "a_msq")
            C.wait(nc.vector, "a_t2r")
            C.inc(nc.vector.tensor_scalar(t4[:], t2[:], 1.0 / DM, None, AX.mult),
                  "a_s2m")
            C.wait(nc.vector, "a_msq")
            C.wait(nc.vector, "a_s2m")
            C.inc(nc.vector.tensor_sub(t5[:], t4[:], t3[:]), "a_var")
            C.wait(nc.vector, "a_var")
            nc.vector.tensor_copy(lnvar[:, sub:sub + 1], t5[:])
            C.wait(nc.scalar, "a_var")
            C.inc(nc.scalar.activation(t3[:], t5[:], AF.Sqrt, bias=P["eps5"][:]),
                  "a_sd")
            C.wait(nc.vector, "a_sd")
            C.inc(nc.vector.reciprocal(P["stats_isd"][:, sub:sub + 1], t3[:]),
                  "a_isd")
            C.wait(nc.vector, "a_isd")
            isd_col = P["stats_isd"][:, sub:sub + 1]
            C.inc(nc.vector.tensor_mul(t6[:], m_col, isd_col), "a_mi")
            C.wait(nc.vector, "a_mi")
            C.inc(nc.vector.tensor_scalar(P["stats_nmi"][:, sub:sub + 1], t6[:],
                                          -1.0, None, AX.mult), "a_nmi")
            C.wait(nc.scalar, "a_nmi")
            C.inc(nc.scalar.activation(xnb[:], buf[:], AF.Identity,
                                       bias=P["stats_nmi"][:, sub:sub + 1],
                                       scale=isd_col), "a_xs")
            C.wait(nc.tensor, "a_xs")
            if sub > 0:
                C.wait(nc.tensor, "a_te")
            for hh in range(2):
                if hh == 1:
                    C.wait(nc.tensor, "a_te", C.cnt["a_te"] - 0)
                for t in range(8):
                    k = 8 * hh + t
                    ins = nc.tensor.transpose(ptr[:, 128 * t:128 * t + 128],
                                              xnb[:, 128 * k:128 * k + 128],
                                              P["idb"][:])
                C.inc(ins, "a_tp")
                C.wait(nc.vector, "a_tp")
                for t in range(8):
                    k = 8 * hh + t
                    C.inc(nc.vector.tensor_copy(xnt[:, 128 * k:128 * k + 128],
                                                ptr[:, 128 * t:128 * t + 128]),
                          "a_te")
            C.wait(nc.tensor, "a_te")
            if sub == 0 and "e3" in C.dbg:
                C.barrier_all("e3b")
            if sub > 0:
                C.wait(nc.tensor, "a_tqe")
            for k in range(16):
                for n in range(4):
                    ins = nc.tensor.matmul(psq[n][:], xnt[:, 128 * k:128 * k + 128],
                                           wq_sb[k][:, 512 * n:512 * n + 512],
                                           start=(k == 0), stop=(k == 15))
            C.inc(ins, "a_mm")
            if sub == 0 and "e2" in C.dbg:
                C.barrier_all("e2b")
            C.wait(nc.vector, "a_mm")
            if sub >= 1 and sub <= NSUB:
                C.wait(nc.vector, "a_qo", 16 * min(sub, NSUB))
            for n in range(4):
                if n > 0:
                    C.wait(nc.vector, "a_us", C.cnt["a_us"])
                C.inc(nc.vector.tensor_add(
                    tq[:], psq[n][:], bq_sb[:, 512 * n:512 * n + 512]), "a_tqe")
                C.inc(nc.vector.tensor_copy(qsub[:, 512 * n:512 * n + 512], tq[:]),
                      "a_q2s")
                C.wait(nc.scalar, "a_q2s")
                if C.cnt.get("a_nr", 0) > 0:
                    C.wait(nc.scalar, "a_nr")
                C.inc(nc.scalar.square(usq[:], tq[:]), "a_us")
                C.wait(nc.vector, "a_us")
                if n == 0:
                    C.wait(nc.vector, "a_ntp", C.cnt.get("a_ntp", 0))
                u3 = usq[:].rearrange("p (g d) -> p g d", g=4)
                C.inc(nc.vector.reduce_sum(nrm[:, 4 * n:4 * n + 4], u3, AXL.X),
                      "a_nr")
            if sub < NSUB:
                C.wait(nc.scalar, "a_q2s")
                C.inc(nc.scalar.dma_start(h["qsp"][sub], qsub[:]), "a_qo", 16)
            if sub == 0 and "qsub0" in C.dbg:
                C.wait(nc.gpsimd, "a_q2s", C.cnt["a_q2s"])
                C.dbg_dump("qsub0", qsub[:], [128, DM], BF16)
                C.dbg_dump("xnt0", xnt[:], [128, DM], BF16)
            C.wait(nc.tensor, "a_nr")
            if sub > 0:
                C.wait(nc.tensor, "a_nev", C.cnt["a_nev"])
            C.inc(nc.tensor.transpose(pnrm[:], nrm[:], P["idf"][:]), "a_ntp")
            C.wait(nc.vector, "a_ntp")
            C.inc(nc.vector.tensor_copy(P["norms"][0:16, 128 * sub:128 * sub + 128],
                                        pnrm[0:16, :]), "a_nev")
        C.wait(nc.sync, "a_qo", C.cnt.get("a_qo", 0))
        st1.close()
        C.barrier_all("pa1")
        C.dbg_dump("qspA", h["qsp"], [NSUB, 128, DM], BF16)
        C.dbg_dump("stats_m", P["stats_m"][:], [128, NSUBF])
        C.dbg_dump("lnvar", lnvar[:], [128, NSUBF])
        C.dbg_dump("stats_isd", P["stats_isd"][:], [128, NSUBF])
        C.dbg_dump("norms", P["norms"][:], [NH, SEQ])

        wrk = sb("a_wrk", [NH, SEQ])
        top8 = sb("a_top8", [NH, 8])
        topv = sb("a_topv", [NH, TOPK])
        nc.vector.tensor_copy(wrk[:], P["norms"][:])
        for r in range(8):
            C.inc(nc.vector.max(top8[:], wrk[:]), "a2m")
            C.wait(nc.vector, "a2m")
            C.inc(nc.vector.tensor_copy(topv[:, 8 * r:8 * r + 8], top8[:]), "a2tv")
            if r < 7:
                C.inc(nc.vector.match_replace(wrk[:], top8[:], wrk[:], -1.0e30),
                      "a2m")
                C.wait(nc.vector, "a2m")
        C.wait(nc.vector, "a2tv")
        C.inc(nc.vector.tensor_single_scalar(wrk[:], P["norms"][:],
                                              topv[:, 61:62], AX.is_ge), "a2sr")
        C.wait(nc.vector, "a2sr")
        C.inc(nc.vector.tensor_max(P["sel_h"][:], wrk[:], P["boseos"][:]), "a2s")
        C.wait(nc.vector, "a2s")
        C.wait(nc.tensor, "a2s")

        st2 = ExitStack()
        selp = st2.enter_context(nc.psum_tensor("a_selp", [128, NH], F32))
        rankp = st2.enter_context(nc.psum_tensor("a_rankp", [128, NH], F32))
        totp = st2.enter_context(nc.psum_tensor("a_totp", [1, NH], F32))
        cumA = sb("a_cumA", [1, NH])
        cumB = sb("a_cumB", [1, NH])
        cumb = sb("a_cumb", [1, NH], BF16)
        C.inc(nc.vector.memset(cumA[:], 0.0), "a2ci")
        C.wait(nc.vector, "a2ci")
        C.inc(nc.vector.tensor_copy(cumb[:], cumA[:]), "a2cu")
        C.wait(nc.tensor, "a2cu")
        for s in range(NSUB):
            if s > 0:
                C.wait(nc.tensor, "a2se", C.cnt["a2se"])
            C.inc(nc.tensor.transpose(selp[:], P["sel_h"][0:16, 128 * s:128 * s + 128],
                                      P["idf"][0:16, 0:16]), "a2tp")
            C.wait(nc.vector, "a2tp")
            nc.vector.tensor_copy(P["sel_tm"][:, 16 * s:16 * s + 16], selp[:])
            C.inc(nc.vector.tensor_copy(P["selb_tm"][:, 16 * s:16 * s + 16],
                                        selp[:]), "a2se")
            C.wait(nc.tensor, "a2se")
            if s > 0:
                C.wait(nc.tensor, "a2rv", C.cnt["a2rv"])
            nc.tensor.matmul(rankp[:], P["triu"][:],
                             P["selb_tm"][:, 16 * s:16 * s + 16],
                             start=True, stop=False)
            C.inc(nc.tensor.matmul(rankp[:], P["onesrb"][:], cumb[:],
                                   start=False, stop=True), "a2rk")
            C.inc(nc.tensor.matmul(totp[:], P["onescb"][:],
                                   P["selb_tm"][:, 16 * s:16 * s + 16],
                                   start=True, stop=True), "a2tt")
            C.wait(nc.vector, "a2rk")
            C.wait(nc.vector, "a2tt")
            C.inc(nc.vector.tensor_copy(P["rank_tm"][:, 16 * s:16 * s + 16],
                                        rankp[:]), "a2rv")
            src_c, dst_c = (cumA, cumB) if s % 2 == 0 else (cumB, cumA)
            C.inc(nc.vector.tensor_add(dst_c[:], src_c[:], totp[0:1, :]), "a2ca")
            C.wait(nc.vector, "a2ca")
            C.inc(nc.vector.tensor_copy(cumb[:], dst_c[:]), "a2cu")
            C.wait(nc.tensor, "a2cu")
        st2.close()
        C.barrier_all("pa2")
        C.dbg_dump("sel_tm", P["sel_tm"][:], [128, NSUB * NH])
        C.dbg_dump("rank_tm", P["rank_tm"][:], [128, NSUB * NH])

        C.inc(nc.vector.memset(P["qgt"][:], 0.0), "a3qz")
        C.wait(nc.vector, "a3qz")
        mtb = sb("a_mtb", [128, TOPK], BF16)
        qgs = sb("a_qgs", [TOPK, DH], BF16)
        st3 = ExitStack()
        qg_ps = [st3.enter_context(nc.psum_tensor(f"a_qg{i}", [TOPK, DH], F32))
                 for i in range(4)]
        qgt_ps = st3.enter_context(nc.psum_tensor("a_qgt", [128, TOPK], BF16))
        for sweep in range(4):
            for s in range(NSUB):
                C.wait(nc.sync, "a3fr", C.cnt.get("a3fr", 0))
                C.inc(nc.sync.dma_start(qsub[:], h["qsp"][s]), "a3qd", 16)
                C.wait(nc.vector, "a3qd")
                for hh in range(4):
                    hd = 4 * sweep + hh
                    C.inc(nc.vector.tensor_scalar(
                        mtb[:], P["iota64"][:],
                        P["rank_tm"][:, 16 * s + hd:16 * s + hd + 1],
                        P["sel_tm"][:, 16 * s + hd:16 * s + hd + 1],
                        AX.is_equal, AX.mult), "a3mt")
                    C.wait(nc.tensor, "a3mt")
                    ins = nc.tensor.matmul(qg_ps[hh][:], mtb[:],
                                           qsub[:, 128 * hd:128 * hd + 128],
                                           start=(s == 0), stop=(s == NSUB - 1))
                    C.inc(ins, "a3mb")
                    C.wait(nc.vector, "a3mb")
                C.inc(nc.tensor.nop(), "a3fr")
            C.inc(nc.tensor.nop(), "a3mm")
            C.wait(nc.vector, "a3mm")
            for hh in range(4):
                hd = 4 * sweep + hh
                C.inc(nc.vector.tensor_copy(qgs[:], qg_ps[hh][:]), "a3ev")
                C.wait(nc.tensor, "a3ev")
                if C.cnt.get("a3qg", 0) > 0:
                    C.wait(nc.tensor, "a3qg")
                C.inc(nc.tensor.transpose(qgt_ps[0:128, 0:TOPK], qgs[:],
                                          P["idb"][0:TOPK, 0:TOPK]), "a3tp")
                C.wait(nc.vector, "a3tp")
                C.inc(nc.vector.tensor_scalar(P["qgt"][:, 64 * hd:64 * hd + 64],
                                              qgt_ps[0:128, 0:64],
                                              0.08838834764831845, None, AX.mult),
                      "a3qg")
        st3.close()
        C.barrier_all("pa3")
        C.dbg_dump("qgt", P["qgt"][:], [128, NH * TOPK])


def phase_b(C, h, P, hg):
    nc = C.nc
    s_ = lambda n: f"b{hg}_" + n
    with ExitStack() as st:
        sb = lambda n, sh, dt=F32: st.enter_context(nc.sbuf_tensor(s_(n), sh, dt))
        ps = lambda n, sh, dt=F32: st.enter_context(nc.psum_tensor(s_(n), sh, dt))
        wk_sb = [sb(f"wk{k}", [128, 1024], BF16) for k in range(16)]
        wv_sb = [sb(f"wv{k}", [128, 1024], BF16) for k in range(16)]
        bv_sb = sb("bv", [128, 1024])
        qsb = sb("qsb", [128, 1024], BF16)
        xs = [sb(f"x{i}", [128, DM]) for i in range(2)]
        xnb = sb("xnb", [128, DM], BF16)
        xnt = sb("xnt", [128, 16 * 576], BF16)
        stash = sb("stash", [128, 16 * 64], BF16)
        kt = [sb(f"kt{i}", [128, 8 * 576], BF16) for i in range(2)]
        kt0 = sb("ktz", [128, 8 * 576], BF16)
        vt = [sb(f"vt{i}", [128, 1024], BF16) for i in range(9)]
        qt_t = sb("qt_t", [128, 128], BF16)
        probs = sb("probs", [128, 256], BF16)
        probsT = sb("probsT", [128, 256], BF16)
        pgb = sb("pgb", [TOPK, 512], BF16)
        pgbT = sb("pgbT", [128, 256], BF16)
        ctx_st = [sb(f"cst{i}", [128, 1024], BF16) for i in range(2)]
        maskmul = sb("mskm", [128, 2 * 256], BF16)
        probs2 = sb("probs2", [128, 256], BF16)
        ncolt = sb("ncolt", [128, 12])
        nmx = ncolt[:, 0:1]
        ssum = ncolt[:, 1:2]
        rsum = ncolt[:, 2:3]
        gmc = ncolt[0:TOPK, 3:4]
        gda = ncolt[0:TOPK, 4:5]
        gal = ncolt[0:TOPK, 5:6]
        gnn = ncolt[0:TOPK, 6:7]
        gcs = ncolt[0:TOPK, 7:8]
        grr = ncolt[0:TOPK, 8:9]
        rm = [sb(f"rm{i}", [TOPK, 8]) for i in range(2)]
        rs = [sb(f"rs{i}", [TOPK, 8]) for i in range(2)]
        gc = [sb(f"gc{i}", [TOPK, 1024]) for i in range(2)]
        ptrb = [ps(f"ptr{i}", [128, 1024], BF16) for i in range(2)]
        pk = ps("pk", [128, 512])
        pv0 = ps("pv0", [128, 512])
        pv1 = ps("pv1", [128, 512])
        psc = ps("psc", [128, 512])
        pctx = ps("pctx", [128, 128])
        ppv = ps("ppv", [TOPK, DH])

        def tp_into(i, pairs, ident):
            C.wait(nc.tensor, s_(f"pe{i}"), C.cnt.get(s_(f"pe{i}"), 0))
            ins = None
            for dst, srcp in pairs:
                ins = nc.tensor.transpose(dst, srcp, ident)
            return ins

        C.inc(nc.vector.tensor_scalar(maskmul[:], P["maskadd"][:], -0.5e9, None,
                                      AX.is_ge), s_("mm"))
        nc.vector.memset(rm[0][:], -1.0e30)
        nc.vector.memset(rs[0][:], 0.0)
        C.inc(nc.vector.memset(gc[0][:], 0.0), s_("fi"))
        C.wait(nc.vector, s_("fi"))
        C.wait(nc.scalar, s_("fi"))

        for k in range(16):
            C.inc(nc.gpsimd.dma_start(
                wk_sb[k][:], h["wk"][128 * k:128 * k + 128,
                                     1024 * hg:1024 * hg + 1024]), s_("wd"), 16)
            C.inc(nc.gpsimd.dma_start(
                wv_sb[k][:], h["wv"][128 * k:128 * k + 128,
                                     1024 * hg:1024 * hg + 1024]), s_("wd"), 16)
        C.inc(nc.sync.dma_start(bv_sb[:], h["bv"][0:128, 1024 * hg:1024 * hg + 1024]),
              s_("bvd"), 16)
        C.wait(nc.tensor, s_("wd"))
        C.wait(nc.vector, s_("bvd"))

        fcnt = [0]
        vslot = lambda m: m if m <= 3 else 4 + ((m - 4) % 5)

        def xn_rebuild(c):
            if c > 0:
                C.wait(nc.vector, s_("kmm"), C.cnt.get(s_("kmm"), 0))
                C.wait(nc.vector, s_("vmm"), C.cnt.get(s_("vmm"), 0))
            if c == 0:
                for k in range(16):
                    C.inc(nc.vector.memset(xnt[:, 576 * k:576 * k + 64], 0.0),
                          s_("xe"))
            else:
                for k in range(16):
                    C.inc(nc.vector.tensor_copy(xnt[:, 576 * k:576 * k + 64],
                                                stash[:, 64 * k:64 * k + 64]),
                          s_("xe"))
            for ls in range(4):
                sub = 4 * c + ls
                buf = xs[ls % 2]
                if 4 * c + ls >= 2:
                    C.wait(nc.sync, s_("xs"), C.cnt[s_("xs")] - 1)
                C.inc(nc.sync.dma_start(buf[:], h["x"][128 * sub:128 * sub + 128, :]),
                      s_("xd"), 16)
                C.wait(nc.scalar, s_("xd"))
                if C.cnt.get(s_("xtp"), 0) > 0:
                    C.wait(nc.scalar, s_("xtp"))
                C.inc(nc.scalar.activation(xnb[:], buf[:], AF.Identity,
                                           bias=P["stats_nmi"][:, sub:sub + 1],
                                           scale=P["stats_isd"][:, sub:sub + 1]),
                      s_("xs"))
                C.wait(nc.tensor, s_("xs"))
                for hh2 in range(2):
                    prs = [(ptrb[hh2][:, 128 * t:128 * t + 128],
                            xnb[:, 128 * (8 * hh2 + t):128 * (8 * hh2 + t) + 128])
                           for t in range(8)]
                    ins = tp_into(hh2, prs, P["idb"][:])
                    C.inc(ins, s_("xtp"))
                    C.wait(nc.vector, s_("xtp"))
                    for t in range(8):
                        k = 8 * hh2 + t
                        cc = nc.vector.tensor_copy(
                            xnt[:, 576 * k + 64 + 128 * ls:
                                576 * k + 64 + 128 * ls + 128],
                            ptrb[hh2][:, 128 * t:128 * t + 128])
                        C.inc(cc, s_(f"pe{hh2}"))
            C.wait(nc.vector, s_("xe"))
            C.wait(nc.vector, s_("pe0"))
            C.wait(nc.vector, s_("pe1"))
            for k in range(16):
                C.inc(nc.vector.tensor_copy(stash[:, 64 * k:64 * k + 64],
                                            xnt[:, 576 * k + 512:576 * k + 576]),
                      s_("sst"))
            C.inc(nc.vector.nop(), s_("xrd"))
            C.wait(nc.tensor, s_("xrd"))

        def k_proj(c):
            slot = kt[c % 2]
            if c >= 2:
                C.wait(nc.scalar, s_("lsc"), C.cnt.get(s_("lsc"), 0))
                C.wait(nc.scalar, s_("gsc"), C.cnt.get(s_("gsc"), 0))
            for hh in range(8):
                hd = 8 * hg + hh
                if C.cnt.get(s_("kev"), 0) > 0:
                    C.wait(nc.tensor, s_("kev"), C.cnt[s_("kev")])
                C.wait(nc.tensor, s_("lex"), C.cnt.get(s_("lex"), 0))
                C.wait(nc.tensor, s_("gex"), C.cnt.get(s_("gex"), 0))
                for k in range(16):
                    nc.tensor.matmul(pk[:], wk_sb[k][:, 128 * hh:128 * hh + 128],
                                     xnt[:, 576 * k + 64:576 * k + 576],
                                     start=(k == 0), stop=(k == 15))
                    ins = nc.tensor.matmul(psc[0:128, 0:64],
                                           wk_sb[k][:, 128 * hh:128 * hh + 128],
                                           xnt[:, 576 * k:576 * k + 64],
                                           start=(k == 0), stop=(k == 15))
                C.inc(ins, s_("kmm"))
                C.wait(nc.scalar, s_("kmm"))
                nc.scalar.activation(slot[:, 576 * hh + 64:576 * hh + 576],
                                     pk[:], AF.Identity,
                                     bias=P["bkh"][:, hd:hd + 1])
                a2 = nc.scalar.activation(slot[:, 576 * hh:576 * hh + 64],
                                          psc[0:128, 0:64], AF.Identity,
                                          bias=P["bkh"][:, hd:hd + 1])
                C.inc(a2, s_("kev"))
                if c == 0:
                    C.inc(nc.scalar.copy(kt0[:, 576 * hh:576 * hh + 576],
                                         slot[:, 576 * hh:576 * hh + 576]),
                          s_("kev"))

        def v_proj(c):
            if c >= 2:
                C.wait(nc.vector, s_("pve"), C.cnt.get(s_("pve"), 0))
                C.wait(nc.vector, s_("gpv"), C.cnt.get(s_("gpv"), 0))
            for i in range(4):
                m = 4 * c + i
                slot = vt[vslot(m)]
                if C.cnt.get(s_("vev"), 0) > 0:
                    C.wait(nc.tensor, s_("vev"), C.cnt[s_("vev")])
                for k in range(16):
                    nc.tensor.matmul(pv0[:],
                                     xnt[:, 576 * k + 128 * i:576 * k + 128 * i + 128],
                                     wv_sb[k][:, 0:512], start=(k == 0), stop=(k == 15))
                    ins = nc.tensor.matmul(pv1[:],
                                           xnt[:, 576 * k + 128 * i:576 * k + 128 * i + 128],
                                           wv_sb[k][:, 512:1024],
                                           start=(k == 0), stop=(k == 15))
                C.inc(ins, s_("vmm"))
                C.wait(nc.vector, s_("vmm"))
                nc.vector.tensor_add(slot[:, 0:512], pv0[:], bv_sb[:, 0:512])
                C.inc(nc.vector.tensor_add(slot[:, 512:1024], pv1[:],
                                           bv_sb[:, 512:1024]), s_("vev"))
            if c == 7:
                C.wait(nc.tensor, s_("vev"), C.cnt[s_("vev")])
                for k in range(16):
                    nc.tensor.matmul(pv0[0:64, :], xnt[:, 576 * k + 512:576 * k + 576],
                                     wv_sb[k][:, 0:512], start=(k == 0), stop=(k == 15))
                    ins = nc.tensor.matmul(pv1[0:64, :],
                                           xnt[:, 576 * k + 512:576 * k + 576],
                                           wv_sb[k][:, 512:1024],
                                           start=(k == 0), stop=(k == 15))
                C.inc(ins, s_("vmm"))
                C.wait(nc.vector, s_("vmm"))
                nc.vector.tensor_add(vt[0][0:64, 0:512], pv0[0:64, :],
                                     bv_sb[0:64, 0:512])
                C.inc(nc.vector.tensor_add(vt[0][0:64, 512:1024], pv1[0:64, :],
                                           bv_sb[0:64, 512:1024]), s_("vev"))
                for hh in range(8):
                    hd = 8 * hg + hh
                    C.wait(nc.tensor, s_("kev"), C.cnt[s_("kev")])
                    for k in range(16):
                        ins = nc.tensor.matmul(
                            psc[0:128, 0:64],
                            wk_sb[k][:, 128 * hh:128 * hh + 128],
                            xnt[:, 576 * k + 512:576 * k + 576],
                            start=(k == 0), stop=(k == 15))
                    C.inc(ins, s_("kmm"))
                    C.wait(nc.scalar, s_("kmm"))
                    C.inc(nc.scalar.activation(kt0[:, 576 * hh:576 * hh + 64],
                                               psc[0:128, 0:64], AF.Identity,
                                               bias=P["bkh"][:, hd:hd + 1]),
                          s_("kev"))

        def kt_cols(tok0, tok1, hh, use_kt0):
            pieces = []
            t = tok0
            while t < tok1:
                cprime = (t + 64) // 512
                ten = kt0 if (cprime == 0 and use_kt0) else kt[cprime % 2]
                base = 512 * cprime - 64
                hi = min(tok1, base + 576)
                pieces.append((ten, 576 * hh + (t - base), 576 * hh + (hi - base),
                               t - tok0))
                t = hi
            return pieces

        def local_qb(qb, use_kt0=False):
            if C.cnt.get(s_("qtp"), 0) > 0:
                C.wait(nc.sync, s_("qtp"))
            C.inc(nc.sync.dma_start(qsb[:],
                                    h["qsp"][qb, 0:128, 1024 * hg:1024 * hg + 1024]),
                  s_("qd"), 16)
            C.wait(nc.tensor, s_("qd"))
            for hh in range(8):
                hd = 8 * hg + hh
                prs = [(ptrb[0][:, 0:128],
                        qsb[:, 128 * hh:128 * hh + 128])]
                ins = tp_into(0, prs, P["idb"][:])
                C.inc(ins, s_("qtp"))
                C.wait(nc.vector, s_("qtp"))
                cc = nc.vector.tensor_scalar(qt_t[:], ptrb[0][:, 0:128],
                                             0.08838834764831845, None, AX.mult)
                C.inc(cc, s_("pe0"))
                C.wait(nc.tensor, s_("pe0"))
                C.wait(nc.tensor, s_("kev"), C.cnt.get(s_("kev"), 0))
                C.wait(nc.tensor, s_("lex"), C.cnt.get(s_("lex"), 0))
                C.wait(nc.tensor, s_("gex"), C.cnt.get(s_("gex"), 0))
                qs = 128 * qb
                for ten, c0, c1, d0 in kt_cols(qs - 64, qs + 192, hh, use_kt0):
                    ins = nc.tensor.matmul(psc[0:128, d0:d0 + (c1 - c0)], qt_t[:],
                                           ten[:, c0:c1], start=True, stop=True)
                C.inc(ins, s_("lsc"))
                C.wait(nc.vector, s_("lsc"))
                C.inc(nc.vector.reduce_max(nmx, psc[0:128, 0:256], AXL.X,
                                           negate=True), s_("lmx"))
                C.wait(nc.scalar, s_("lmx"))
                C.wait(nc.scalar, s_("ptp"), C.cnt.get(s_("ptp"), 0))
                masked = qb in (0, 15)
                e = nc.scalar.activation(probs[:], psc[0:128, 0:256], AF.Exp,
                                         bias=nmx)
                C.inc(e, s_("lex"))
                C.wait(nc.vector, s_("lex"))
                psrc = probs
                if masked:
                    moff = 0 if qb == 0 else 256
                    C.inc(nc.vector.tensor_mul(probs2[:], probs[:],
                                               maskmul[:, moff:moff + 256]),
                          s_("lpm"))
                    C.wait(nc.vector, s_("lpm"))
                    psrc = probs2
                C.inc(nc.vector.reduce_sum(ssum, psrc[:], AXL.X), s_("lsm"))
                C.wait(nc.vector, s_("lsm"))
                C.inc(nc.vector.reciprocal(rsum, ssum), s_("lrc"))
                C.wait(nc.tensor, s_("lex"))
                if masked:
                    C.wait(nc.tensor, s_("lpm"))
                prs = [(ptrb[1][:, 128 * i2:128 * i2 + 128],
                        psrc[:, 128 * i2:128 * i2 + 128]) for i2 in range(2)]
                ins = tp_into(1, prs, P["idb"][:])
                C.inc(ins, s_("ptp"))
                C.wait(nc.vector, s_("ptp"))
                cc = nc.vector.tensor_copy(probsT[:], ptrb[1][:, 0:256])
                C.inc(cc, s_("pe1"))
                C.wait(nc.tensor, s_("pe1"))
                C.wait(nc.tensor, s_("vev"), C.cnt.get(s_("vev"), 0))
                C.wait(nc.tensor, s_("cev"), C.cnt.get(s_("cev"), 0))
                nc.tensor.matmul(pctx[:], probsT[:, 0:128],
                                 vt[vslot(qb)][:, 128 * hh:128 * hh + 128],
                                 start=True, stop=False)
                C.inc(nc.tensor.matmul(pctx[:], probsT[:, 128:256],
                                       vt[vslot(qb + 1)][:, 128 * hh:128 * hh + 128],
                                       start=False, stop=True), s_("pve"))
                C.wait(nc.vector, s_("pve"))
                C.wait(nc.vector, s_("lrc"))
                cst = ctx_st[qb % 2]
                if hh == 0 and C.cnt.get(s_("cso"), 0) >= 32:
                    C.wait(nc.vector, s_("cso"), C.cnt[s_("cso")] - 16)
                C.inc(nc.vector.tensor_scalar(cst[:, 128 * hh:128 * hh + 128],
                                              pctx[:], rsum, None, AX.mult),
                      s_("cev"))
            C.wait(nc.scalar, s_("cev"))
            C.inc(nc.scalar.dma_start(
                h["ctxh"][qb, 0:128, 1024 * hg:1024 * hg + 1024],
                ctx_st[qb % 2][:]), s_("cso"), 16)

        def flash_step(fc, use_kt0=False):
            p = fcnt[0] % 2
            for hh in range(8):
                hd = 8 * hg + hh
                ten = kt0 if use_kt0 else kt[fc % 2]
                C.wait(nc.tensor, s_("gex"), C.cnt.get(s_("gex"), 0))
                C.wait(nc.tensor, s_("lex"), C.cnt.get(s_("lex"), 0))
                C.wait(nc.tensor, s_("kev"), C.cnt.get(s_("kev"), 0))
                C.inc(nc.tensor.matmul(psc[0:TOPK, 0:512],
                                       P["qgt"][:, TOPK * hd:TOPK * hd + TOPK],
                                       ten[:, 576 * hh:576 * hh + 512],
                                       start=True, stop=True), s_("gsc"))
                C.wait(nc.vector, s_("gsc"))
                C.inc(nc.vector.reduce_max(gmc, psc[0:TOPK, 0:512], AXL.X),
                      s_("gmx"))
                C.wait(nc.vector, s_("gmx"))
                C.inc(nc.vector.tensor_max(rm[1 - p][:, hh:hh + 1],
                                           rm[p][:, hh:hh + 1], gmc), s_("gnm"))
                C.wait(nc.vector, s_("gnm"))
                C.inc(nc.vector.tensor_sub(gda, rm[p][:, hh:hh + 1],
                                           rm[1 - p][:, hh:hh + 1]), s_("gda"))
                C.inc(nc.vector.tensor_scalar(gnn, rm[1 - p][:, hh:hh + 1],
                                              -1.0, None, AX.mult), s_("gnn"))
                C.wait(nc.scalar, s_("gda"))
                C.inc(nc.scalar.activation(gal, gda, AF.Exp), s_("gal"))
                C.wait(nc.scalar, s_("gnn"))
                C.wait(nc.scalar, s_("gtp"), C.cnt.get(s_("gtp"), 0))
                e = nc.scalar.activation(pgb[:], psc[0:TOPK, 0:512], AF.Exp,
                                         bias=gnn)
                C.inc(e, s_("gex"))
                C.wait(nc.vector, s_("gex"))
                C.inc(nc.vector.reduce_sum(gcs, pgb[:], AXL.X), s_("gcr"))
                C.wait(nc.vector, s_("gcr"))
                C.wait(nc.vector, s_("gal"))
                C.inc(nc.vector.scalar_tensor_tensor(
                    rs[1 - p][:, hh:hh + 1], rs[p][:, hh:hh + 1], gal, gcs,
                    AX.mult, AX.add), s_("grs"))
                prs = [(ptrb[1][:, 64 * i2:64 * i2 + 64],
                        pgb[:, 128 * i2:128 * i2 + 128]) for i2 in range(4)]
                C.wait(nc.tensor, s_("gex"))
                ins = tp_into(1, prs, P["idb"][0:TOPK, 0:TOPK])
                C.inc(ins, s_("gtp"))
                C.wait(nc.vector, s_("gtp"))
                cc = nc.vector.tensor_copy(pgbT[:], ptrb[1][:, 0:256])
                C.inc(cc, s_("pe1"))
                C.wait(nc.tensor, s_("pe1"))
                C.wait(nc.tensor, s_("vev"), C.cnt.get(s_("vev"), 0))
                for i2 in range(4):
                    m = 4 * fc + i2
                    ins = nc.tensor.matmul(ppv[:], pgbT[:, 64 * i2:64 * i2 + 64],
                                           vt[vslot(m)][:, 128 * hh:128 * hh + 128],
                                           start=(i2 == 0), stop=(i2 == 3))
                C.inc(ins, s_("gpv"))
                C.wait(nc.vector, s_("gpv"))
                C.inc(nc.vector.scalar_tensor_tensor(
                    gc[1 - p][:, 128 * hh:128 * hh + 128],
                    gc[p][:, 128 * hh:128 * hh + 128], gal, ppv[:],
                    AX.mult, AX.add), s_("ggc"))
                C.wait(nc.tensor, s_("ggc"))
            fcnt[0] += 1

        ready = {c: [] for c in range(NCH)}
        for qb in range(1, NSUB):
            ready[(128 * qb + 191) // 512].append(qb)

        for c in range(NCH):
            xn_rebuild(c)
            k_proj(c)
            v_proj(c)
            if c >= 1:
                flash_step(c)
            for qb in ready[c]:
                local_qb(qb)
        flash_step(0, use_kt0=True)
        local_qb(0, use_kt0=True)

        pfin = fcnt[0] % 2
        for hh in range(8):
            hd = 8 * hg + hh
            C.inc(nc.vector.reciprocal(grr, rs[pfin][:, hh:hh + 1]), s_("gfr"))
            C.wait(nc.vector, s_("gfr"))
            C.inc(nc.vector.tensor_scalar(P["gfin"][:, 128 * hd:128 * hd + 128],
                                          gc[pfin][:, 128 * hh:128 * hh + 128],
                                          grr, None, AX.mult), s_("gfw"))
        C.wait(nc.sync, s_("cso"))
        C.wait(nc.sync, s_("gfw"))
        if hg == 0 and "ktz" in C.dbg:
            C.wait(nc.gpsimd, s_("kev"), C.cnt[s_("kev")])
            C.wait(nc.gpsimd, s_("vev"), C.cnt[s_("vev")])
            C.dbg_dump("ktz", kt0[:], [128, 8 * 576], BF16)
            C.dbg_dump("kta", kt[0][:], [128, 8 * 576], BF16)
            C.dbg_dump("ktb", kt[1][:], [128, 8 * 576], BF16)
            for i in range(4):
                C.dbg_dump(f"vt{i}", vt[i][:], [128, 1024], BF16)


def phase_c(C, h, P):
    nc = C.nc
    s_ = lambda n: "c_" + n
    SC = 1.0 / float(DM)
    with ExitStack() as st:
        sb = lambda n, sh, dt=F32: st.enter_context(nc.sbuf_tensor(s_(n), sh, dt))
        ps = lambda n, sh, dt=F32: st.enter_context(nc.psum_tensor(s_(n), sh, dt))
        r_sb = sb("r", [128, DFF])
        xqt = sb("xqt", [128, 16 * 256], BF16)
        x2qt = sb("x2qt", [128, 64 * 256], BF16)
        w1sb = sb("w1sb", [128, DFF], BF16)
        w2sb = sb("w2sb", [128, DFF], BF16)
        g2b = sb("g2b", [128, DFF], BF16)
        g1b = sb("g1b", [128, DM], BF16)
        xsub = [sb(f"xs{i}", [128, DM]) for i in range(2)]
        ctxb = [sb(f"cb{i}", [128, DM], BF16) for i in range(2)]
        accp = [sb(f"acc{i}", [128, DM]) for i in range(2)]
        yst = [sb(f"yst{i}", [128, DM]) for i in range(2)]
        scrA = sb("scrA", [128, 512])
        scrB = sb("scrB", [128, 512])
        scrC = sb("scrC", [128, 512])
        scrI = sb("scrI", [128, 512], I32)
        qs_bf = sb("qs_bf", [128, 512], BF16)
        mtb = sb("mtb", [128, TOPK], BF16)
        mbf = sb("mbf", [TOPK, 128], BF16)
        mxp = sb("mxp", [128, 16])
        s4p = sb("s4p", [128, 16])
        selbar = sb("selbar", [128, NH])
        n_ = {}
        for nm in ("u1 u2 u3 rms1 rr1 mxa mxn mxn2 mxc rmx s1 cA colsc d1 rdc c1 "
                   "c2 c4 tt tt2 rms2 rr2 m2a m2b m2c rm2 s2 cs2a cs2b d2 rd2 c2f "
                   "mxu s4").split():
            n_[nm] = sb("n_" + nm, [128, 1])
        c1c = sb("c1c", [128, 2])
        c2fc = sb("c2fc", [128, 2])
        ptrb = ps("ptrb", [128, 1024], BF16)
        ptrf = ps("ptrf", [128, 512])
        g = [ps(f"g{i}", [128, 512]) for i in range(6)]

        C.inc(nc.gpsimd.dma_start(g2b[:], h["g2"][0:128, :]), s_("gld"), 16)
        C.inc(nc.gpsimd.dma_start(g1b[:], h["g1"][0:128, :]), s_("gld"), 16)
        C.wait(nc.vector, s_("gld"))

        def gate(ins, tag, eng):
            C.inc(ins, s_(tag))
            C.wait(eng, s_(tag))

        def tp_into(pairs, ident, pbank):
            C.wait(nc.tensor, s_("pe"), C.cnt.get(s_("pe"), 0))
            ins = None
            for dst, srcp in pairs:
                ins = nc.tensor.transpose(dst, srcp, ident)
            return ins

        for T in range(NCH):
            for ls in range(2):
                sub = 2 * T + ls
                if T > 0:
                    C.wait(nc.sync, s_("acc"), C.cnt.get(s_("acc"), 0))
                    C.wait(nc.sync, s_("bl"), C.cnt.get(s_("bl"), 0))
                C.inc(nc.sync.dma_start(xsub[ls][:],
                                        h["x"][128 * sub:128 * sub + 128, :]),
                      s_("xd"), 16)
                C.inc(nc.sync.dma_start(ctxb[ls][:], h["ctxh"][sub]), s_("cd"), 16)
            for ls in range(2):
                sub = 2 * T + ls
                C.wait(nc.vector, s_("xd"), 16 * (2 * T + ls + 1))
                for sl in range(4):
                    c0 = 512 * sl
                    gate(nc.vector.scalar_tensor_tensor(
                        scrA[:], xsub[ls][:, c0:c0 + 512],
                        P["stats_m"][:, sub:sub + 1], g1b[:, c0:c0 + 512],
                        AX.subtract, AX.mult), "t1", nc.vector)
                    C.inc(nc.vector.reduce_max(mxp[:, sl:sl + 1], scrA[:], AXL.X,
                                               apply_absolute_value=True), s_("t2p"))
                C.wait(nc.vector, s_("t2p"))
                gate(nc.vector.reduce_max(n_["mxa"][:], mxp[:, 0:4], AXL.X),
                     "nx", nc.vector)
                isd = P["stats_isd"][:, sub:sub + 1]
                gate(nc.vector.tensor_mul(n_["u1"][:], isd, isd), "nx", nc.vector)
                gate(nc.vector.tensor_scalar(n_["u2"][:], n_["u1"][:], -1.0e-5,
                                             None, AX.mult), "nx", nc.vector)
                gate(nc.vector.tensor_scalar(n_["u3"][:], n_["u2"][:], 1.000001,
                                             None, AX.add), "nx", nc.vector)
                C.wait(nc.scalar, s_("nx"))
                gate(nc.scalar.activation(n_["rms1"][:], n_["u3"][:], AF.Sqrt),
                     "ns", nc.vector)
                gate(nc.vector.reciprocal(n_["rr1"][:], n_["rms1"][:]), "nx", nc.vector)
                gate(nc.vector.tensor_mul(n_["mxn"][:], n_["mxa"][:], isd),
                     "nx", nc.vector)
                gate(nc.vector.tensor_mul(n_["mxn2"][:], n_["mxn"][:], n_["rr1"][:]),
                     "nx", nc.vector)
                gate(nc.vector.tensor_scalar(n_["mxc"][:], n_["mxn2"][:], 1.0e-5,
                                             None, AX.max), "nx", nc.vector)
                gate(nc.vector.reciprocal(n_["rmx"][:], n_["mxc"][:]), "nx", nc.vector)
                gate(nc.vector.tensor_scalar(n_["s1"][:], n_["rmx"][:], 127.0,
                                             None, AX.mult), "nx", nc.vector)
                gate(nc.vector.tensor_mul(n_["cA"][:], isd, n_["rr1"][:]),
                     "nx", nc.vector)
                gate(nc.vector.tensor_mul(n_["colsc"][:], n_["cA"][:], n_["s1"][:]),
                     "nx", nc.vector)
                gate(nc.vector.tensor_mul(n_["d1"][:], n_["s1"][:],
                                          P["swv"][:, 0:1]), "nx", nc.vector)
                gate(nc.vector.reciprocal(n_["rdc"][:], n_["d1"][:]), "nx", nc.vector)
                gate(nc.vector.tensor_scalar(c1c[:, ls:ls + 1], n_["rdc"][:], 0.5,
                                             None, AX.mult), "nx", nc.vector)
                for sl in range(4):
                    c0 = 512 * sl
                    gate(nc.vector.scalar_tensor_tensor(
                        scrB[:], xsub[ls][:, c0:c0 + 512],
                        P["stats_m"][:, sub:sub + 1], g1b[:, c0:c0 + 512],
                        AX.subtract, AX.mult), "t1b", nc.vector)
                    gate(nc.vector.tensor_scalar(scrA[:], scrB[:],
                                                 n_["colsc"][:], MAGIC,
                                                 AX.mult, AX.add), "qv", nc.vector)
                    gate(nc.vector.tensor_scalar(scrI[:], scrA[:], MAGIC, None,
                                                 AX.subtract), "qv", nc.vector)
                    C.wait(nc.scalar, s_("qv"))
                    if C.cnt.get(s_("qtp"), 0) > 0:
                        C.wait(nc.scalar, s_("qtp"))
                    C.inc(nc.scalar.copy(qs_bf[:], scrI[:]), s_("qb"))
                    C.wait(nc.tensor, s_("qb"))
                    prs = [(ptrb[:, 128 * j:128 * j + 128],
                            qs_bf[:, 128 * j:128 * j + 128]) for j in range(4)]
                    ins = tp_into(prs, P["idb"][:], ptrb)
                    C.inc(ins, s_("qtp"))
                    C.wait(nc.vector, s_("qtp"))
                    for j in range(4):
                        k = 4 * sl + j
                        cc = nc.vector.tensor_copy(
                            xqt[:, 256 * k + 128 * ls:256 * k + 128 * ls + 128],
                            ptrb[:, 128 * j:128 * j + 128])
                        C.inc(cc, s_("pe"))
                C.wait(nc.tensor, s_("pe"))
                C.wait(nc.scalar, s_("x2a"), C.cnt.get(s_("x2a"), 0))
                for ffs in range(16):
                    wb = w1sb if ffs % 2 == 0 else w2sb
                    if ffs == 0 and C.cnt.get(s_("f2m"), 0) > 0:
                        C.wait(nc.gpsimd, s_("f2m"))
                    if C.cnt.get(s_("f1m"), 0) > 1:
                        C.wait(nc.gpsimd, s_("f1m"), C.cnt[s_("f1m")] - 1)
                    C.inc(nc.gpsimd.dma_start(wb[:], h["w1t"][ffs]), s_("w1d"), 16)
                    C.wait(nc.tensor, s_("w1d"))
                    pb = g[ffs % 2]
                    if C.cnt.get(s_("rel"), 0) > 1:
                        C.wait(nc.tensor, s_("rel"), C.cnt[s_("rel")] - 1)
                    for k in range(16):
                        ins = nc.tensor.matmul(
                            pb[:], xqt[:, 256 * k + 128 * ls:256 * k + 128 * ls + 128],
                            wb[:, 512 * k:512 * k + 512],
                            start=(k == 0), stop=(k == 15))
                    C.inc(ins, s_("f1m"))
                    C.wait(nc.scalar, s_("f1m"))
                    C.inc(nc.scalar.activation(r_sb[:, 512 * ffs:512 * ffs + 512],
                                               pb[:], AF.Relu), s_("rel"))
                C.wait(nc.vector, s_("rel"), C.cnt[s_("rel")])
                if T == 0 and ls == 0 and "r0" in C.dbg:
                    C.wait(nc.gpsimd, s_("rel"), C.cnt[s_("rel")])
                    C.dbg_dump("r0", r_sb[:], [128, DFF])
                C.wait(nc.scalar, s_("ytp"), C.cnt.get(s_("ytp"), 0))
                for sl in range(16):
                    c0 = 512 * sl
                    nc.vector.tensor_mul(scrA[:], r_sb[:, c0:c0 + 512], g2b[:, c0:c0 + 512])
                    gate(nc.vector.tensor_mul(scrB[:], scrA[:], r_sb[:, c0:c0 + 512]),
                         "st1", nc.vector)
                    C.inc(nc.vector.reduce_max(mxp[:, sl:sl + 1], scrB[:], AXL.X,
                                               apply_absolute_value=True), s_("st2"))
                    C.wait(nc.scalar, s_("rel"), C.cnt[s_("rel")])
                    C.inc(nc.scalar.square(scrC[:], r_sb[:, c0:c0 + 512]), s_("sq1"))
                    C.wait(nc.scalar, s_("sq1"))
                    C.wait(nc.scalar, s_("st1"), C.cnt.get(s_("st1"), 0))
                    C.inc(nc.scalar.square(scrA[:], scrC[:]), s_("sq2"))
                    C.wait(nc.vector, s_("sq2"))
                    C.inc(nc.vector.reduce_sum(s4p[:, sl:sl + 1], scrA[:], AXL.X),
                          s_("st3"))
                C.wait(nc.vector, s_("st2"))
                C.wait(nc.vector, s_("st3"))
                gate(nc.vector.reduce_max(n_["mxu"][:], mxp[:], AXL.X), "nx", nc.vector)
                gate(nc.vector.reduce_sum(n_["s4"][:], s4p[:], AXL.X), "nx", nc.vector)
                c1 = c1c[:, ls:ls + 1]
                gate(nc.vector.tensor_mul(n_["c2"][:], c1, c1), "nx", nc.vector)
                gate(nc.vector.tensor_mul(n_["c4"][:], n_["c2"][:], n_["c2"][:]),
                     "nx", nc.vector)
                gate(nc.vector.tensor_mul(n_["tt"][:], n_["s4"][:], n_["c4"][:]),
                     "nx", nc.vector)
                gate(nc.vector.tensor_scalar(n_["tt2"][:], n_["tt"][:], 1.0 / DFF,
                                             None, AX.mult), "nx", nc.vector)
                C.wait(nc.scalar, s_("nx"))
                gate(nc.scalar.activation(n_["rms2"][:], n_["tt2"][:], AF.Sqrt,
                                          bias=P["eps6"][:]), "ns", nc.vector)
                gate(nc.vector.reciprocal(n_["rr2"][:], n_["rms2"][:]), "nx", nc.vector)
                gate(nc.vector.tensor_mul(n_["m2a"][:], n_["mxu"][:], n_["c2"][:]),
                     "nx", nc.vector)
                gate(nc.vector.tensor_mul(n_["m2b"][:], n_["m2a"][:], n_["rr2"][:]),
                     "nx", nc.vector)
                gate(nc.vector.tensor_scalar(n_["m2c"][:], n_["m2b"][:], 1.0e-5,
                                             None, AX.max), "nx", nc.vector)
                gate(nc.vector.reciprocal(n_["rm2"][:], n_["m2c"][:]), "nx", nc.vector)
                gate(nc.vector.tensor_scalar(n_["s2"][:], n_["rm2"][:], 127.0,
                                             None, AX.mult), "nx", nc.vector)
                gate(nc.vector.tensor_mul(n_["cs2a"][:], n_["c2"][:], n_["rr2"][:]),
                     "nx", nc.vector)
                gate(nc.vector.tensor_mul(n_["cs2b"][:], n_["cs2a"][:], n_["s2"][:]),
                     "nx", nc.vector)
                gate(nc.vector.tensor_mul(n_["d2"][:], n_["s2"][:], P["swv"][:, 1:2]),
                     "nx", nc.vector)
                gate(nc.vector.reciprocal(n_["rd2"][:], n_["d2"][:]), "nx", nc.vector)
                gate(nc.vector.tensor_scalar(c2fc[:, ls:ls + 1], n_["rd2"][:], 0.5,
                                             None, AX.mult), "nx", nc.vector)
                for sl in range(16):
                    c0 = 512 * sl
                    nc.vector.tensor_mul(scrA[:], r_sb[:, c0:c0 + 512],
                                         g2b[:, c0:c0 + 512])
                    gate(nc.vector.tensor_mul(scrB[:], scrA[:], r_sb[:, c0:c0 + 512]),
                         "x2a", nc.vector)
                    gate(nc.vector.tensor_scalar(scrA[:], scrB[:], n_["cs2b"][:],
                                                 MAGIC, AX.mult, AX.add),
                         "x2b", nc.vector)
                    gate(nc.vector.tensor_scalar(scrI[:], scrA[:], MAGIC, None,
                                                 AX.subtract), "x2c", nc.vector)
                    C.wait(nc.scalar, s_("x2c"))
                    if C.cnt.get(s_("x2tp"), 0) > 0:
                        C.wait(nc.scalar, s_("x2tp"))
                    C.inc(nc.scalar.copy(qs_bf[:], scrI[:]), s_("x2q"))
                    C.wait(nc.tensor, s_("x2q"))
                    prs = [(ptrb[:, 128 * j:128 * j + 128],
                            qs_bf[:, 128 * j:128 * j + 128]) for j in range(4)]
                    ins = tp_into(prs, P["idb"][:], ptrb)
                    C.inc(ins, s_("x2tp"))
                    C.wait(nc.vector, s_("x2tp"))
                    for j in range(4):
                        f = 4 * sl + j
                        cc = nc.vector.tensor_copy(
                            x2qt[:, 256 * f + 128 * ls:256 * f + 128 * ls + 128],
                            ptrb[:, 128 * j:128 * j + 128])
                        C.inc(cc, s_("pe"))
            C.wait(nc.vector, s_("yo"), C.cnt.get(s_("yo"), 0))
            for ls in range(2):
                sub = 2 * T + ls
                gate(nc.vector.tensor_scalar(selbar[:], P["sel_tm"][:, 16 * sub:16 * sub + 16],
                                             -1.0, 1.0, AX.mult, AX.add),
                     "sbv", nc.vector)
                C.wait(nc.vector, s_("cd"), 16 * (2 * T + ls + 1))
                for hd in range(NH):
                    gate(nc.vector.tensor_scalar(
                        mtb[:], P["iota64"][:],
                        P["rank_tm"][:, 16 * sub + hd:16 * sub + hd + 1],
                        P["sel_tm"][:, 16 * sub + hd:16 * sub + hd + 1],
                        AX.is_equal, AX.mult), "mtv", nc.tensor)
                    ins = tp_into([(ptrb[0:TOPK, 0:128], mtb[:])], P["idb"][:], ptrb)
                    C.inc(ins, s_("mtp"))
                    C.wait(nc.vector, s_("mtp"))
                    cc = nc.vector.tensor_copy(mbf[:], ptrb[0:TOPK, 0:128])
                    C.inc(cc, s_("pe"))
                    C.wait(nc.tensor, s_("pe"))
                    if hd == 0:
                        C.wait(nc.tensor, s_("bl"), C.cnt.get(s_("bl"), 0))
                    C.inc(nc.tensor.matmul(
                        g[2 + hd // 4][:, 128 * (hd % 4):128 * (hd % 4) + 128],
                        mbf[:], P["gfin"][:, 128 * hd:128 * hd + 128],
                        start=True, stop=True), s_("cgm"))
                C.wait(nc.vector, s_("cgm"))
                for hd in range(NH):
                    C.inc(nc.vector.scalar_tensor_tensor(
                        yst[ls][:, 128 * hd:128 * hd + 128],
                        ctxb[ls][:, 128 * hd:128 * hd + 128],
                        selbar[:, hd:hd + 1],
                        g[2 + hd // 4][:, 128 * (hd % 4):128 * (hd % 4) + 128],
                        AX.mult, AX.add), s_("bl"))
                C.wait(nc.vector, s_("bl"))
                C.inc(nc.vector.tensor_add(accp[ls][:], yst[ls][:], xsub[ls][:]),
                      s_("acc"))
            C.wait(nc.tensor, s_("pe"))
            if T == 0 and "acc0" in C.dbg:
                C.wait(nc.gpsimd, s_("acc"), C.cnt[s_("acc")])
                C.wait(nc.gpsimd, s_("pe"), C.cnt[s_("pe")])
                C.wait(nc.gpsimd, s_("nx"), C.cnt[s_("nx")])
                C.dbg_dump("acc0", accp[0][:], [128, DM])
                C.dbg_dump("acc1", accp[1][:], [128, DM])
                C.dbg_dump("x2qt0", x2qt[:], [128, 64 * 256], BF16)
                C.dbg_dump("xqt0", xqt[:], [128, 16 * 256], BF16)
                C.dbg_dump("c1c", c1c[:], [128, 2])
                C.dbg_dump("c2fc", c2fc[:], [128, 2])
            C.wait(nc.vector, s_("acc"))
            for b in range(16):
                if b == 0:
                    C.wait(nc.gpsimd, s_("f1m"), C.cnt[s_("f1m")])
                if C.cnt.get(s_("f2m"), 0) > 0:
                    C.wait(nc.gpsimd, s_("f2m"))
                C.inc(nc.gpsimd.dma_start(w2sb[:], h["w2t"][b]), s_("w2d"), 16)
                C.wait(nc.tensor, s_("w2d"))
                pb = g[b % 2]
                if C.cnt.get(s_("o2e"), 0) > 1:
                    C.wait(nc.tensor, s_("o2e"), C.cnt[s_("o2e")] - 1)
                for k in range(64):
                    ins = nc.tensor.matmul(pb[:, 0:256], w2sb[:, 128 * k:128 * k + 128],
                                           x2qt[:, 256 * k:256 * k + 256],
                                           start=(k == 0), stop=(k == 63))
                C.inc(ins, s_("f2m"))
                C.wait(nc.vector, s_("f2m"))
                C.inc(nc.vector.tensor_copy(scrC[:, 0:256], pb[:, 0:256]), s_("o2e"))
                C.wait(nc.tensor, s_("o2e"))
                C.wait(nc.tensor, s_("yw"), C.cnt.get(s_("yw"), 0))
                for ls in range(2):
                    ins = nc.tensor.transpose(ptrf[:, 128 * ls:128 * ls + 128],
                                              scrC[:, 128 * ls:128 * ls + 128],
                                              P["idf"][:])
                C.inc(ins, s_("ytp"))
                C.wait(nc.vector, s_("ytp"))
                for ls in range(2):
                    cc = nc.vector.scalar_tensor_tensor(
                        yst[ls][:, 128 * b:128 * b + 128],
                        ptrf[:, 128 * ls:128 * ls + 128], c2fc[:, ls:ls + 1],
                        accp[ls][:, 128 * b:128 * b + 128], AX.mult, AX.add)
                    C.inc(cc, s_("yw"))
            C.wait(nc.scalar, s_("yw"))
            for ls in range(2):
                sub = 2 * T + ls
                C.inc(nc.scalar.dma_start(h["y"][128 * sub:128 * sub + 128, :],
                                          yst[ls][:]), s_("yo"), 16)
        C.wait(nc.sync, s_("yo"))


def _consts():
    c = {}
    c["ident_bf"] = np.eye(128, dtype=ml_dtypes.bfloat16)
    c["ident_f32"] = np.eye(128, dtype=np.float32)
    c["triu_bf"] = np.triu(np.ones((128, 128))).astype(ml_dtypes.bfloat16)
    c["ones_col_bf"] = np.ones((128, 1), dtype=ml_dtypes.bfloat16)
    c["ones_row_bf"] = np.ones((1, 128), dtype=ml_dtypes.bfloat16)
    c["ones_col_f"] = np.ones((128, 1), dtype=np.float32)
    c["ones_row_f"] = np.ones((1, 128), dtype=np.float32)
    c["iota64_rep"] = np.ascontiguousarray(
        np.tile(np.arange(1, TOPK + 1, dtype=np.float32), (128, 1)))
    return c


def _stage_core(inputs, core):
    seq = core // 2
    half = core % 2
    off = HALF * half
    x = np.asarray(inputs["L0_attn_hidden_states"][seq])
    m = {}
    m["x_seq"] = np.ascontiguousarray(np.roll(x, -off, axis=0))
    m["Wq"] = np.asarray(inputs["L0_attn_Wq"])
    m["Wk"] = np.asarray(inputs["L0_attn_Wk"])
    m["Wv"] = np.asarray(inputs["L0_attn_Wv"])
    m["w1"] = np.asarray(inputs["L0_ffn_w1"])
    m["w2"] = np.asarray(inputs["L0_ffn_w2"])
    m["bq_rep"] = np.ascontiguousarray(
        np.tile(np.asarray(inputs["L0_attn_bq"])[None, :], (128, 1)))
    m["bv_rep"] = np.ascontiguousarray(
        np.tile(np.asarray(inputs["L0_attn_bv"])[None, :], (128, 1)))
    m["bk_hm"] = np.ascontiguousarray(
        np.asarray(inputs["L0_attn_bk"]).reshape(NH, DH).T)
    m["g1_rep"] = np.ascontiguousarray(
        np.tile(np.asarray(inputs["L0_ffn_g1"])[None, :], (128, 1)))
    m["g2_rep"] = np.ascontiguousarray(
        np.tile(np.asarray(inputs["L0_ffn_g2"])[None, :], (128, 1)))
    m.update(_consts())
    mask = np.zeros((2, 128, 256), dtype=np.float32)
    if half == 0:
        mask[0, :, 0:64] = NEG
    else:
        mask[1, :, 192:256] = NEG
    m["maskadd"] = mask
    be = np.zeros((NH, SEQ), dtype=np.float32)
    be[:, (0 - off) % SEQ] = 1.0
    be[:, (SEQ - 1 - off) % SEQ] = 1.0
    m["boseos"] = be
    return m


def kernel(**inputs):
    nc = build_program(dbg=_DBG, phases=_PHASES)
    in_maps = [_stage_core(inputs, c) for c in range(8)]
    res = run_bass_kernel_spmd(nc, in_maps, core_ids=list(range(8)))
    out = np.empty((4, SEQ, DM), dtype=np.float32)
    for c in range(8):
        seq, half = c // 2, c % 2
        out[seq, HALF * half:HALF * half + HALF] = res.results[c]["y"]
    return out


# version 03 (final)
# speedup vs baseline: 1.2198x; 1.0843x over previous
import numpy as np
import ml_dtypes
from contextlib import ExitStack

import concourse.bass as bass
import concourse.mybir as mybir
from concourse.bass_utils import run_bass_kernel_spmd

F32 = mybir.dt.float32
BF16 = mybir.dt.bfloat16
I32 = mybir.dt.int32
FP8 = mybir.dt.float8e4
AX = mybir.AluOpType
AF = mybir.ActivationFunctionType
AXL = mybir.AxisListType

MAGIC = 12582912.0
NEG = -1.0e9
SEQ = 4096
DM = 2048
DFF = 8192
NH = 16
DH = 128
HALF = 2048
NSUB = 16
NSUBF = 32
NCH = 8
TOPK = 64

_PHASES = "wabc"
_DBG = ()


class Ctx:

    def __init__(self, nc, stack, dbg):
        self.nc = nc
        self.stack = stack
        self.dbg = set(dbg)
        self.dbg_dram = {}
        self.sems = {}
        self.cnt = {}
        self.sem("gbar")

    def sem(self, name):
        if name not in self.sems:
            self.sems[name] = self.stack.enter_context(self.nc.semaphore(name))
            self.cnt[name] = 0
        return self.sems[name]

    def inc(self, ins, name, n=1):
        ins.then_inc(self.sem(name), n)
        self.cnt[name] += n
        return self.cnt[name]

    def wait(self, eng, name, val=None):
        eng.wait_ge(self.sem(name), self.cnt[name] if val is None else val)

    def edge(self, eng, tag):
        name = "e_" + tag
        self.inc(eng.nop(), name)
        self.wait(eng, name)

    def barrier_all(self, name):
        nc = self.nc
        engines = [nc.sync, nc.scalar, nc.vector, nc.tensor, nc.gpsimd]
        s = self.sem(name)
        base = self.cnt[name]
        for e in engines:
            e.sem_inc(s, 1)
        self.cnt[name] = base + 5
        for e in engines:
            e.wait_ge(s, base + 5)

    def free_scope(self, prefixes):
        victims = [n for n in self.sems if any(n.startswith(p) for p in prefixes)]
        if not victims:
            return
        self.nc.clear_and_free_semaphores([self.sems[n] for n in victims])
        for n in victims:
            del self.sems[n]
            del self.cnt[n]
        self.barrier_all("gbar")

    def sb(self, name, shape, dt=F32):
        return self.stack.enter_context(self.nc.sbuf_tensor(name, shape, dt))

    def ps(self, name, shape, dt=F32):
        return self.stack.enter_context(self.nc.psum_tensor(name, shape, dt))

    def dbg_out(self, name, shape, dt=F32):
        if name not in self.dbg_dram:
            self.dbg_dram[name] = self.nc.dram_tensor(
                "dbg_" + name, shape, dt, kind="ExternalOutput").ap()
        return self.dbg_dram[name]

    def dbg_dump(self, name, src_ap, shape, dt=F32):
        if name not in self.dbg:
            return
        d = self.dbg_out(name, shape, dt)
        self.inc(self.nc.gpsimd.dma_start(d, src_ap), "dbgo_" + name, 16)
        self.wait(self.nc.gpsimd, "dbgo_" + name)


def build_program(dbg=(), phases=_PHASES):
    nc = bass.Bass("TRN2", target_bir_lowering=False)
    h = {}
    h["x"] = nc.dram_tensor("x_seq", [SEQ, DM], F32, kind="ExternalInput").ap()
    h["wq"] = nc.dram_tensor("Wq", [DM, DM], F32, kind="ExternalInput").ap()
    h["wk"] = nc.dram_tensor("Wk", [DM, DM], F32, kind="ExternalInput").ap()
    h["wv"] = nc.dram_tensor("Wv", [DM, DM], F32, kind="ExternalInput").ap()
    h["w1"] = nc.dram_tensor("w1", [DFF, DM], F32, kind="ExternalInput").ap()
    h["w2"] = nc.dram_tensor("w2", [DM, DFF], F32, kind="ExternalInput").ap()
    h["bq"] = nc.dram_tensor("bq_rep", [128, DM], F32, kind="ExternalInput").ap()
    h["bkh"] = nc.dram_tensor("bk_hm", [128, NH], F32, kind="ExternalInput").ap()
    h["bv"] = nc.dram_tensor("bv_rep", [128, DM], F32, kind="ExternalInput").ap()
    h["g1"] = nc.dram_tensor("g1_rep", [128, DM], F32, kind="ExternalInput").ap()
    h["g2"] = nc.dram_tensor("g2_rep", [128, DFF], F32, kind="ExternalInput").ap()
    h["idb"] = nc.dram_tensor("ident_bf", [128, 128], BF16, kind="ExternalInput").ap()
    h["idf"] = nc.dram_tensor("ident_f32", [128, 128], F32, kind="ExternalInput").ap()
    h["triu"] = nc.dram_tensor("triu_bf", [128, 128], BF16, kind="ExternalInput").ap()
    h["onescb"] = nc.dram_tensor("ones_col_bf", [128, 1], BF16, kind="ExternalInput").ap()
    h["onesrb"] = nc.dram_tensor("ones_row_bf", [1, 128], BF16, kind="ExternalInput").ap()
    h["onescf"] = nc.dram_tensor("ones_col_f", [128, 1], F32, kind="ExternalInput").ap()
    h["onesrf"] = nc.dram_tensor("ones_row_f", [1, 128], F32, kind="ExternalInput").ap()
    h["iota64"] = nc.dram_tensor("iota64_rep", [128, TOPK], F32, kind="ExternalInput").ap()
    h["maskadd"] = nc.dram_tensor("maskadd", [2, 128, 256], F32, kind="ExternalInput").ap()
    h["boseos"] = nc.dram_tensor("boseos", [NH, SEQ], F32, kind="ExternalInput").ap()
    h["y"] = nc.dram_tensor("y", [HALF, DM], F32, kind="ExternalOutput").ap()
    h["qsp"] = nc.dram_tensor("qspill", [NSUB, 128, DM], BF16, kind="Internal").ap()
    h["ctxh"] = nc.dram_tensor("ctxhbm", [NSUB, 128, DM], BF16, kind="Internal").ap()
    h["w1t"] = nc.dram_tensor("w1t_cache", [16, 128, DFF], FP8, kind="Internal").ap()
    h["w2t"] = nc.dram_tensor("w2t_cache", [16, 128, DFF], FP8, kind="Internal").ap()

    with ExitStack() as stack:
        C = Ctx(nc, stack, dbg)
        P = {}
        P["stats_m"] = C.sb("stats_m", [128, NSUBF], F32)
        P["stats_isd"] = C.sb("stats_isd", [128, NSUBF], F32)
        P["stats_nmi"] = C.sb("stats_nmi", [128, NSUBF], F32)
        P["sel_tm"] = C.sb("sel_tm", [128, NSUB * NH], F32)
        P["selb_tm"] = C.sb("selb_tm", [128, NSUB * NH], BF16)
        P["rank_tm"] = C.sb("rank_tm", [128, NSUB * NH], F32)
        P["qgt"] = C.sb("qgt", [128, NH * TOPK], BF16)
        P["gfin"] = C.sb("gfin", [TOPK, NH * DH], BF16)
        P["idb"] = C.sb("c_idb", [128, 128], BF16)
        P["idf"] = C.sb("c_idf", [128, 128], F32)
        P["triu"] = C.sb("c_triu", [128, 128], BF16)
        P["onescb"] = C.sb("c_onescb", [128, 1], BF16)
        P["onesrb"] = C.sb("c_onesrb", [1, 128], BF16)
        P["onescf"] = C.sb("c_onescf", [128, 1], F32)
        P["onesrf"] = C.sb("c_onesrf", [1, 128], F32)
        P["iota64"] = C.sb("c_iota64", [128, TOPK], F32)
        P["eps5"] = C.sb("c_eps5", [128, 1], F32)
        P["eps6"] = C.sb("c_eps6", [128, 1], F32)
        P["swv"] = C.sb("swv", [128, 2], F32)
        P["maskadd"] = C.sb("maskadd_sb", [128, 2 * 256], F32)
        P["bkh"] = C.sb("bkh_sb", [128, NH], F32)

        for nm, hap in (("idb", "idb"), ("idf", "idf"), ("triu", "triu"),
                        ("onescb", "onescb"), ("onesrb", "onesrb"),
                        ("onescf", "onescf"), ("onesrf", "onesrf"),
                        ("iota64", "iota64")):
            C.inc(nc.sync.dma_start(P[nm][:], h[hap]), "cld", 16)
        C.inc(nc.sync.dma_start(P["maskadd"][:, 0:256], h["maskadd"][0]), "cld", 16)
        C.inc(nc.sync.dma_start(P["maskadd"][:, 256:512], h["maskadd"][1]), "cld", 16)
        C.inc(nc.sync.dma_start(P["bkh"][:], h["bkh"]), "cld", 16)
        for eng in (nc.vector, nc.scalar, nc.tensor, nc.gpsimd, nc.sync):
            C.wait(eng, "cld")
        nc.vector.memset(P["eps5"][:], 1.0e-5)
        C.inc(nc.vector.memset(P["eps6"][:], 1.0e-6), "init")
        C.wait(nc.vector, "init")
        C.wait(nc.scalar, "init")

        if "w" in phases:
            phase_w(C, h, P)
            C.barrier_all("pw")
            C.free_scope(("w0", "w1", "winit", "e_w"))
        if "a" in phases:
            phase_a(C, h, P)
            C.barrier_all("pa")
            C.free_scope(("a_", "a2", "a3", "e_a", "pa1", "pa2", "pa3", "dbgo"))
        if "b" in phases:
            phase_b(C, h, P, 0)
            C.barrier_all("pb0")
            C.free_scope(("b0_",))
            phase_b(C, h, P, 1)
            C.barrier_all("pb1")
            C.free_scope(("b1_",))
            C.dbg_dump("qsp", h["qsp"], [NSUB, 128, DM], BF16)
            C.dbg_dump("ctxh", h["ctxh"], [NSUB, 128, DM], BF16)
            C.dbg_dump("gfin", P["gfin"][:], [TOPK, NH * DH], BF16)
        if "c" in phases:
            phase_c(C, h, P)
        else:
            zz = C.sb("zz", [128, DM], F32)
            nc.vector.memset(zz[:], 0.0)
            C.inc(nc.vector.nop(), "zzv")
            C.wait(nc.sync, "zzv")
            for s in range(NSUB):
                C.inc(nc.sync.dma_start(h["y"][128 * s:128 * s + 128, :], zz[:]),
                      "zzo", 16)
            C.wait(nc.sync, "zzo")
    return nc


def phase_w(C, h, P):
    nc = C.nc
    with ExitStack() as st:
        sb = lambda n, s, dt=F32: st.enter_context(nc.sbuf_tensor(n, s, dt))
        blk = [sb(f"w_blk{i}", [128, DFF]) for i in range(2)]
        acc = sb("w_acc", [128, 1])
        red = sb("w_red", [128, 1])
        tsum = sb("w_tsum", [1, 1])
        asb = sb("w_a", [128, DFF], BF16)
        bsb = sb("w_b", [128, DFF], BF16)
        tern = sb("w_tern", [128, DFF], BF16)
        stage = sb("w_stage", [128, DFF], BF16)
        stage8 = sb("w_stage8", [128, DFF], FP8)
        halfp = sb("w_halfp", [128, 1])
        halfn = sb("w_halfn", [128, 1])
        sum_ps = st.enter_context(nc.psum_tensor("w_sum", [1, 1], F32))
        bc_ps = st.enter_context(nc.psum_tensor("w_bc", [128, 1], F32))
        tr_ps = [st.enter_context(nc.psum_tensor(f"w_tr{i}", [128, 1024], BF16))
                 for i in range(2)]

        nc.vector.memset(halfp[:], 0.5)
        C.inc(nc.vector.memset(halfn[:], -0.5), "winit")
        C.wait(nc.scalar, "winit")

        specs = [(h["w1"], DFF // 128, DM, 0), (h["w2"], DM // 128, DFF, 1)]
        parts = sb("w_parts", [128, 64])
        for src, nb, ncol, wi in specs:
            if wi == 1:
                C.wait(nc.sync, "w0v")
                C.wait(nc.vector, "w0t1")
            for b in range(nb):
                buf = blk[b % 2]
                if b >= 2:
                    C.wait(nc.sync, "w0v", C.cnt["w0v"] - 1)
                C.inc(nc.sync.dma_start(buf[:, 0:ncol],
                                        src[128 * b:128 * b + 128, :]), "w0d", 16)
                C.wait(nc.vector, "w0d")
                C.inc(nc.vector.tensor_reduce(parts[:, b:b + 1], buf[:, 0:ncol],
                                              AXL.X, AX.add,
                                              apply_absolute_value=True), "w0v")
            C.wait(nc.vector, "w0v")
            C.inc(nc.vector.tensor_reduce(acc[:], parts[:, 0:nb], AXL.X, AX.add),
                  "w0r")
            C.wait(nc.tensor, "w0r")
            C.inc(nc.tensor.matmul(sum_ps[:], P["onescf"][:], acc[:],
                                   start=True, stop=True), "w0t1")
            C.wait(nc.vector, "w0t1")
            C.inc(nc.vector.tensor_copy(tsum[:], sum_ps[:]), "w0ts")
            C.wait(nc.tensor, "w0ts")
            C.inc(nc.tensor.matmul(bc_ps[:], P["onesrf"][:], tsum[:],
                                   start=True, stop=True), "w0t2")
            C.wait(nc.vector, "w0t2")
            ntot = float(nb * 128 * ncol)
            C.inc(nc.vector.tensor_scalar(red[:], bc_ps[:], 1.0 / ntot, 1.0e-5,
                                          AX.mult, AX.max), "w0m")
            C.wait(nc.vector, "w0m")
            C.inc(nc.vector.reciprocal(P["swv"][:, wi:wi + 1], red[:]), "w0sw")
        C.wait(nc.scalar, "w0sw")

        C.wait(nc.sync, "w0v")
        for src, nb, ncol, wi in specs:
            if wi == 1:
                C.wait(nc.sync, "w1s")
            nkc = ncol // 128
            group = 4 if wi == 0 else 1
            cache = h["w1t"] if wi == 0 else h["w2t"]
            sw_col = P["swv"][:, wi:wi + 1]
            for b in range(nb):
                buf = blk[b % 2]
                if b >= 2:
                    C.wait(nc.sync, "w1s", C.cnt["w1s"] - 1)
                C.inc(nc.sync.dma_start(buf[:, 0:ncol],
                                        src[128 * b:128 * b + 128, :]), "w1d", 16)
                C.wait(nc.scalar, "w1d")
                if C.cnt.get("w1v", 0) > 0:
                    C.wait(nc.scalar, "w1v")
                nc.scalar.activation(asb[:, 0:ncol], buf[:, 0:ncol], AF.Sign,
                                     bias=halfn[:], scale=sw_col)
                C.inc(nc.scalar.activation(bsb[:, 0:ncol], buf[:, 0:ncol], AF.Sign,
                                           bias=halfp[:], scale=sw_col), "w1s")
                C.wait(nc.vector, "w1s")
                if C.cnt.get("w1tp", 0) > 0:
                    C.wait(nc.vector, "w1tp")
                C.inc(nc.vector.tensor_add(tern[:, 0:ncol], asb[:, 0:ncol],
                                           bsb[:, 0:ncol]), "w1v")
                C.wait(nc.tensor, "w1v")
                nhalf = nkc // 8
                for half in range(nhalf):
                    pb = tr_ps[half % 2]
                    hglob = C.cnt.get("w1tp", 0)
                    if hglob >= 2:
                        C.wait(nc.tensor, "w1ev", 8 * (hglob - 1))
                    for t in range(8):
                        k = 8 * half + t
                        ins = nc.tensor.transpose(pb[:, 128 * t:128 * t + 128],
                                                  tern[:, 128 * k:128 * k + 128],
                                                  P["idb"][:])
                    C.inc(ins, "w1tp")
                    C.wait(nc.vector, "w1tp")
                    if C.cnt.get("w1c8", 0) > 0:
                        C.wait(nc.vector, "w1c8")
                    for t in range(8):
                        k = 8 * half + t
                        dstc = (512 * k + 128 * (b % group)) if wi == 0 else 128 * k
                        C.inc(nc.vector.tensor_copy(stage[:, dstc:dstc + 128],
                                                    pb[:, 128 * t:128 * t + 128]),
                              "w1ev")
                if b % group == group - 1:
                    C.wait(nc.vector, "w1ev")
                    if C.cnt.get("w1o", 0) > 0:
                        C.wait(nc.vector, "w1o")
                    C.inc(nc.vector.tensor_copy(stage8[:], stage[:]), "w1c8")
                    C.wait(nc.scalar, "w1c8")
                    C.inc(nc.scalar.dma_start(cache[b // group], stage8[:]),
                          "w1o", 16)
        C.wait(nc.sync, "w1o")
        C.barrier_all("pwd")
        C.dbg_dump("swv", P["swv"][:], [128, 2])

        if "w1t" in C.dbg or "w2t" in C.dbg:
            dsb = blk[0]
            dsb8 = stage8
            for nm, cache in (("w1t", h["w1t"]), ("w2t", h["w2t"])):
                if nm not in C.dbg:
                    continue
                d = C.dbg_out(nm, [16, 128, DFF], F32)
                for i in range(16):
                    C.inc(nc.sync.dma_start(dsb8[:], cache[i]), "wdbd" + nm, 16)
                    C.wait(nc.vector, "wdbd" + nm)
                    C.inc(nc.vector.tensor_copy(dsb[:], dsb8[:]), "wdbv" + nm)
                    C.wait(nc.sync, "wdbv" + nm)
                    C.inc(nc.sync.dma_start(d[i], dsb[:]), "wdbo" + nm, 16)
                    C.wait(nc.sync, "wdbo" + nm)


def phase_a(C, h, P):
    nc = C.nc
    with ExitStack() as st:
        sb = lambda n, s, dt=F32: st.enter_context(nc.sbuf_tensor(n, s, dt))
        P["norms"] = sb("norms_h", [NH, SEQ], F32)
        P["sel_h"] = sb("sel_h", [NH, SEQ], F32)
        P["boseos"] = sb("boseos_sb", [NH, SEQ], F32)
        C.inc(nc.sync.dma_start(P["boseos"][:], h["boseos"]), "a_bos", 16)
        C.wait(nc.vector, "a_bos")
        xs = [sb(f"a_x{i}", [128, DM]) for i in range(2)]
        sq = sb("a_sq", [128, DM])
        tq = sb("a_tq", [128, 512])
        usq = sb("a_usq", [128, 512])
        xnb = sb("a_xnb", [128, DM], BF16)
        xnt = sb("a_xnt", [128, DM], BF16)
        qsub = sb("a_qsub", [128, DM], BF16)
        nrm = sb("a_nrm", [128, NH])
        wq_sb = [sb(f"a_wq{k}", [128, DM], BF16) for k in range(16)]
        bq_sb = sb("a_bq", [128, DM])
        lnvar = sb("a_lnvar", [128, NSUBF])
        t1 = sb("a_t1", [128, 1])
        t2 = sb("a_t2", [128, 1])
        t3 = sb("a_t3", [128, 1])
        t4 = sb("a_t4", [128, 1])
        t5 = sb("a_t5", [128, 1])
        t6 = sb("a_t6", [128, 1])
        st1 = ExitStack()
        psq = [st1.enter_context(nc.psum_tensor(f"a_q{i}", [128, 512], F32))
               for i in range(4)]
        ptr = st1.enter_context(nc.psum_tensor("a_tr", [128, 1024], BF16))
        pnrm = st1.enter_context(nc.psum_tensor("a_pnrm", [16, 128], F32))

        for k in range(16):
            C.inc(nc.gpsimd.dma_start(wq_sb[k][:], h["wq"][128 * k:128 * k + 128, :]),
                  "a_wqd", 16)
        C.inc(nc.sync.dma_start(bq_sb[:], h["bq"]), "a_bqd", 16)
        C.wait(nc.tensor, "a_wqd")
        C.wait(nc.vector, "a_bqd")

        for sub in range(NSUBF):
            buf = xs[sub % 2]
            if sub >= 2:
                C.wait(nc.sync, "a_xs", C.cnt["a_xs"] - 1)
                C.wait(nc.sync, "a_xv", C.cnt["a_xv"] - 1)
            C.inc(nc.sync.dma_start(buf[:], h["x"][128 * sub:128 * sub + 128, :]),
                  "a_xd", 16)
            C.wait(nc.vector, "a_xd")
            C.inc(nc.vector.reduce_sum(t1[:], buf[:], AXL.X), "a_xv")
            C.wait(nc.scalar, "a_xd")
            if sub > 0:
                C.wait(nc.scalar, "a_t2r")
                C.wait(nc.scalar, "a_tp")
            C.inc(nc.scalar.square(sq[:], buf[:]), "a_sqs")
            C.wait(nc.vector, "a_xv")
            C.inc(nc.vector.tensor_scalar(P["stats_m"][:, sub:sub + 1], t1[:],
                                          1.0 / DM, None, AX.mult), "a_m")
            C.wait(nc.vector, "a_sqs")
            C.inc(nc.vector.reduce_sum(t2[:], sq[:], AXL.X), "a_t2r")
            C.wait(nc.vector, "a_m")
            m_col = P["stats_m"][:, sub:sub + 1]
            C.inc(nc.vector.tensor_mul(t3[:], m_col, m_col), "a_msq")
            C.wait(nc.vector, "a_t2r")
            C.inc(nc.vector.tensor_scalar(t4[:], t2[:], 1.0 / DM, None, AX.mult),
                  "a_s2m")
            C.wait(nc.vector, "a_msq")
            C.wait(nc.vector, "a_s2m")
            C.inc(nc.vector.tensor_sub(t5[:], t4[:], t3[:]), "a_var")
            C.wait(nc.vector, "a_var")
            nc.vector.tensor_copy(lnvar[:, sub:sub + 1], t5[:])
            C.wait(nc.scalar, "a_var")
            C.inc(nc.scalar.activation(t3[:], t5[:], AF.Sqrt, bias=P["eps5"][:]),
                  "a_sd")
            C.wait(nc.vector, "a_sd")
            C.inc(nc.vector.reciprocal(P["stats_isd"][:, sub:sub + 1], t3[:]),
                  "a_isd")
            C.wait(nc.vector, "a_isd")
            isd_col = P["stats_isd"][:, sub:sub + 1]
            C.inc(nc.vector.tensor_mul(t6[:], m_col, isd_col), "a_mi")
            C.wait(nc.vector, "a_mi")
            C.inc(nc.vector.tensor_scalar(P["stats_nmi"][:, sub:sub + 1], t6[:],
                                          -1.0, None, AX.mult), "a_nmi")
            C.wait(nc.scalar, "a_nmi")
            C.inc(nc.scalar.activation(xnb[:], buf[:], AF.Identity,
                                       bias=P["stats_nmi"][:, sub:sub + 1],
                                       scale=isd_col), "a_xs")
            C.wait(nc.tensor, "a_xs")
            if sub > 0:
                C.wait(nc.tensor, "a_te")
            for hh in range(2):
                if hh == 1:
                    C.wait(nc.tensor, "a_te", C.cnt["a_te"] - 0)
                for t in range(8):
                    k = 8 * hh + t
                    ins = nc.tensor.transpose(ptr[:, 128 * t:128 * t + 128],
                                              xnb[:, 128 * k:128 * k + 128],
                                              P["idb"][:])
                C.inc(ins, "a_tp")
                C.wait(nc.vector, "a_tp")
                for t in range(8):
                    k = 8 * hh + t
                    C.inc(nc.vector.tensor_copy(xnt[:, 128 * k:128 * k + 128],
                                                ptr[:, 128 * t:128 * t + 128]),
                          "a_te")
            C.wait(nc.tensor, "a_te")
            if sub == 0 and "e3" in C.dbg:
                C.barrier_all("e3b")
            if sub > 0:
                C.wait(nc.tensor, "a_tqe")
            for k in range(16):
                for n in range(4):
                    ins = nc.tensor.matmul(psq[n][:], xnt[:, 128 * k:128 * k + 128],
                                           wq_sb[k][:, 512 * n:512 * n + 512],
                                           start=(k == 0), stop=(k == 15))
            C.inc(ins, "a_mm")
            if sub == 0 and "e2" in C.dbg:
                C.barrier_all("e2b")
            C.wait(nc.vector, "a_mm")
            if sub >= 1 and sub <= NSUB:
                C.wait(nc.vector, "a_qo", 16 * min(sub, NSUB))
            for n in range(4):
                if n > 0:
                    C.wait(nc.vector, "a_us", C.cnt["a_us"])
                C.inc(nc.vector.tensor_add(
                    tq[:], psq[n][:], bq_sb[:, 512 * n:512 * n + 512]), "a_tqe")
                C.inc(nc.vector.tensor_copy(qsub[:, 512 * n:512 * n + 512], tq[:]),
                      "a_q2s")
                C.wait(nc.scalar, "a_q2s")
                if C.cnt.get("a_nr", 0) > 0:
                    C.wait(nc.scalar, "a_nr")
                C.inc(nc.scalar.square(usq[:], tq[:]), "a_us")
                C.wait(nc.vector, "a_us")
                if n == 0:
                    C.wait(nc.vector, "a_ntp", C.cnt.get("a_ntp", 0))
                u3 = usq[:].rearrange("p (g d) -> p g d", g=4)
                C.inc(nc.vector.reduce_sum(nrm[:, 4 * n:4 * n + 4], u3, AXL.X),
                      "a_nr")
            if sub < NSUB:
                C.wait(nc.scalar, "a_q2s")
                C.inc(nc.scalar.dma_start(h["qsp"][sub], qsub[:]), "a_qo", 16)
            if sub == 0 and "qsub0" in C.dbg:
                C.wait(nc.gpsimd, "a_q2s", C.cnt["a_q2s"])
                C.dbg_dump("qsub0", qsub[:], [128, DM], BF16)
                C.dbg_dump("xnt0", xnt[:], [128, DM], BF16)
            C.wait(nc.tensor, "a_nr")
            if sub > 0:
                C.wait(nc.tensor, "a_nev", C.cnt["a_nev"])
            C.inc(nc.tensor.transpose(pnrm[:], nrm[:], P["idf"][:]), "a_ntp")
            C.wait(nc.vector, "a_ntp")
            C.inc(nc.vector.tensor_copy(P["norms"][0:16, 128 * sub:128 * sub + 128],
                                        pnrm[0:16, :]), "a_nev")
        C.wait(nc.sync, "a_qo", C.cnt.get("a_qo", 0))
        st1.close()
        C.barrier_all("pa1")
        C.dbg_dump("qspA", h["qsp"], [NSUB, 128, DM], BF16)
        C.dbg_dump("stats_m", P["stats_m"][:], [128, NSUBF])
        C.dbg_dump("lnvar", lnvar[:], [128, NSUBF])
        C.dbg_dump("stats_isd", P["stats_isd"][:], [128, NSUBF])
        C.dbg_dump("norms", P["norms"][:], [NH, SEQ])

        wrk = sb("a_wrk", [NH, SEQ])
        top8 = sb("a_top8", [NH, 8])
        topv = sb("a_topv", [NH, TOPK])
        nc.vector.tensor_copy(wrk[:], P["norms"][:])
        for r in range(8):
            C.inc(nc.vector.max(top8[:], wrk[:]), "a2m")
            C.wait(nc.vector, "a2m")
            C.inc(nc.vector.tensor_copy(topv[:, 8 * r:8 * r + 8], top8[:]), "a2tv")
            if r < 7:
                C.inc(nc.vector.match_replace(wrk[:], top8[:], wrk[:], -1.0e30),
                      "a2m")
                C.wait(nc.vector, "a2m")
        C.wait(nc.vector, "a2tv")
        C.inc(nc.vector.tensor_single_scalar(wrk[:], P["norms"][:],
                                              topv[:, 61:62], AX.is_ge), "a2sr")
        C.wait(nc.vector, "a2sr")
        C.inc(nc.vector.tensor_max(P["sel_h"][:], wrk[:], P["boseos"][:]), "a2s")
        C.wait(nc.vector, "a2s")
        C.wait(nc.tensor, "a2s")

        st2 = ExitStack()
        selp = st2.enter_context(nc.psum_tensor("a_selp", [128, NH], F32))
        rankp = st2.enter_context(nc.psum_tensor("a_rankp", [128, NH], F32))
        totp = st2.enter_context(nc.psum_tensor("a_totp", [1, NH], F32))
        cumA = sb("a_cumA", [1, NH])
        cumB = sb("a_cumB", [1, NH])
        cumb = sb("a_cumb", [1, NH], BF16)
        C.inc(nc.vector.memset(cumA[:], 0.0), "a2ci")
        C.wait(nc.vector, "a2ci")
        C.inc(nc.vector.tensor_copy(cumb[:], cumA[:]), "a2cu")
        C.wait(nc.tensor, "a2cu")
        for s in range(NSUB):
            if s > 0:
                C.wait(nc.tensor, "a2se", C.cnt["a2se"])
            C.inc(nc.tensor.transpose(selp[:], P["sel_h"][0:16, 128 * s:128 * s + 128],
                                      P["idf"][0:16, 0:16]), "a2tp")
            C.wait(nc.vector, "a2tp")
            nc.vector.tensor_copy(P["sel_tm"][:, 16 * s:16 * s + 16], selp[:])
            C.inc(nc.vector.tensor_copy(P["selb_tm"][:, 16 * s:16 * s + 16],
                                        selp[:]), "a2se")
            C.wait(nc.tensor, "a2se")
            if s > 0:
                C.wait(nc.tensor, "a2rv", C.cnt["a2rv"])
            nc.tensor.matmul(rankp[:], P["triu"][:],
                             P["selb_tm"][:, 16 * s:16 * s + 16],
                             start=True, stop=False)
            C.inc(nc.tensor.matmul(rankp[:], P["onesrb"][:], cumb[:],
                                   start=False, stop=True), "a2rk")
            C.inc(nc.tensor.matmul(totp[:], P["onescb"][:],
                                   P["selb_tm"][:, 16 * s:16 * s + 16],
                                   start=True, stop=True), "a2tt")
            C.wait(nc.vector, "a2rk")
            C.wait(nc.vector, "a2tt")
            C.inc(nc.vector.tensor_copy(P["rank_tm"][:, 16 * s:16 * s + 16],
                                        rankp[:]), "a2rv")
            src_c, dst_c = (cumA, cumB) if s % 2 == 0 else (cumB, cumA)
            C.inc(nc.vector.tensor_add(dst_c[:], src_c[:], totp[0:1, :]), "a2ca")
            C.wait(nc.vector, "a2ca")
            C.inc(nc.vector.tensor_copy(cumb[:], dst_c[:]), "a2cu")
            C.wait(nc.tensor, "a2cu")
        st2.close()
        C.barrier_all("pa2")
        C.dbg_dump("sel_tm", P["sel_tm"][:], [128, NSUB * NH])
        C.dbg_dump("rank_tm", P["rank_tm"][:], [128, NSUB * NH])

        C.inc(nc.vector.memset(P["qgt"][:], 0.0), "a3qz")
        C.wait(nc.vector, "a3qz")
        mtb = sb("a_mtb", [128, TOPK], BF16)
        qgs = sb("a_qgs", [TOPK, DH], BF16)
        st3 = ExitStack()
        qg_ps = [st3.enter_context(nc.psum_tensor(f"a_qg{i}", [TOPK, DH], F32))
                 for i in range(4)]
        qgt_ps = st3.enter_context(nc.psum_tensor("a_qgt", [128, TOPK], BF16))
        for sweep in range(4):
            for s in range(NSUB):
                C.wait(nc.sync, "a3fr", C.cnt.get("a3fr", 0))
                C.inc(nc.sync.dma_start(qsub[:], h["qsp"][s]), "a3qd", 16)
                C.wait(nc.vector, "a3qd")
                for hh in range(4):
                    hd = 4 * sweep + hh
                    C.inc(nc.vector.tensor_scalar(
                        mtb[:], P["iota64"][:],
                        P["rank_tm"][:, 16 * s + hd:16 * s + hd + 1],
                        P["sel_tm"][:, 16 * s + hd:16 * s + hd + 1],
                        AX.is_equal, AX.mult), "a3mt")
                    C.wait(nc.tensor, "a3mt")
                    ins = nc.tensor.matmul(qg_ps[hh][:], mtb[:],
                                           qsub[:, 128 * hd:128 * hd + 128],
                                           start=(s == 0), stop=(s == NSUB - 1))
                    C.inc(ins, "a3mb")
                    C.wait(nc.vector, "a3mb")
                C.inc(nc.tensor.nop(), "a3fr")
            C.inc(nc.tensor.nop(), "a3mm")
            C.wait(nc.vector, "a3mm")
            for hh in range(4):
                hd = 4 * sweep + hh
                C.inc(nc.vector.tensor_copy(qgs[:], qg_ps[hh][:]), "a3ev")
                C.wait(nc.tensor, "a3ev")
                if C.cnt.get("a3qg", 0) > 0:
                    C.wait(nc.tensor, "a3qg")
                C.inc(nc.tensor.transpose(qgt_ps[0:128, 0:TOPK], qgs[:],
                                          P["idb"][0:TOPK, 0:TOPK]), "a3tp")
                C.wait(nc.vector, "a3tp")
                C.inc(nc.vector.tensor_scalar(P["qgt"][:, 64 * hd:64 * hd + 64],
                                              qgt_ps[0:128, 0:64],
                                              0.08838834764831845, None, AX.mult),
                      "a3qg")
        st3.close()
        C.barrier_all("pa3")
        C.dbg_dump("qgt", P["qgt"][:], [128, NH * TOPK])


def phase_b(C, h, P, hg):
    nc = C.nc
    s_ = lambda n: f"b{hg}_" + n
    with ExitStack() as st:
        sb = lambda n, sh, dt=F32: st.enter_context(nc.sbuf_tensor(s_(n), sh, dt))
        ps = lambda n, sh, dt=F32: st.enter_context(nc.psum_tensor(s_(n), sh, dt))
        wk_sb = [sb(f"wk{k}", [128, 1024], BF16) for k in range(16)]
        wv_sb = [sb(f"wv{k}", [128, 1024], BF16) for k in range(16)]
        bv_sb = sb("bv", [128, 1024])
        qsb = sb("qsb", [128, 1024], BF16)
        xs = [sb(f"x{i}", [128, DM]) for i in range(2)]
        xnb = sb("xnb", [128, DM], BF16)
        xnt = sb("xnt", [128, 16 * 576], BF16)
        stash = sb("stash", [128, 16 * 64], BF16)
        kt = [sb(f"kt{i}", [128, 8 * 576], BF16) for i in range(2)]
        kt0 = sb("ktz", [128, 8 * 576], BF16)
        vt = [sb(f"vt{i}", [128, 1024], BF16) for i in range(9)]
        qt_t = sb("qt_t", [128, 128], BF16)
        probs = sb("probs", [128, 256], BF16)
        probsT = sb("probsT", [128, 256], BF16)
        pgb = sb("pgb", [TOPK, 512], BF16)
        pgbT = sb("pgbT", [128, 256], BF16)
        ctx_st = [sb(f"cst{i}", [128, 1024], BF16) for i in range(2)]
        maskmul = sb("mskm", [128, 2 * 256], BF16)
        probs2 = sb("probs2", [128, 256], BF16)
        ncolt = sb("ncolt", [128, 12])
        nmx = ncolt[:, 0:1]
        ssum = ncolt[:, 1:2]
        rsum = ncolt[:, 2:3]
        gmc = ncolt[0:TOPK, 3:4]
        gda = ncolt[0:TOPK, 4:5]
        gal = ncolt[0:TOPK, 5:6]
        gnn = ncolt[0:TOPK, 6:7]
        gcs = ncolt[0:TOPK, 7:8]
        grr = ncolt[0:TOPK, 8:9]
        rm = [sb(f"rm{i}", [TOPK, 8]) for i in range(2)]
        rs = [sb(f"rs{i}", [TOPK, 8]) for i in range(2)]
        gc = [sb(f"gc{i}", [TOPK, 1024]) for i in range(2)]
        ptrb = [ps(f"ptr{i}", [128, 1024], BF16) for i in range(2)]
        pk = ps("pk", [128, 512])
        pv0 = ps("pv0", [128, 512])
        pv1 = ps("pv1", [128, 512])
        psc = ps("psc", [128, 512])
        pctx = ps("pctx", [128, 128])
        ppv = ps("ppv", [TOPK, DH])

        def tp_into(i, pairs, ident):
            C.wait(nc.tensor, s_(f"pe{i}"), C.cnt.get(s_(f"pe{i}"), 0))
            ins = None
            for dst, srcp in pairs:
                ins = nc.tensor.transpose(dst, srcp, ident)
            return ins

        C.inc(nc.vector.tensor_scalar(maskmul[:], P["maskadd"][:], -0.5e9, None,
                                      AX.is_ge), s_("mm"))
        nc.vector.memset(rm[0][:], -1.0e30)
        nc.vector.memset(rs[0][:], 0.0)
        C.inc(nc.vector.memset(gc[0][:], 0.0), s_("fi"))
        C.wait(nc.vector, s_("fi"))
        C.wait(nc.scalar, s_("fi"))

        for k in range(16):
            C.inc(nc.gpsimd.dma_start(
                wk_sb[k][:], h["wk"][128 * k:128 * k + 128,
                                     1024 * hg:1024 * hg + 1024]), s_("wd"), 16)
            C.inc(nc.gpsimd.dma_start(
                wv_sb[k][:], h["wv"][128 * k:128 * k + 128,
                                     1024 * hg:1024 * hg + 1024]), s_("wd"), 16)
        C.inc(nc.sync.dma_start(bv_sb[:], h["bv"][0:128, 1024 * hg:1024 * hg + 1024]),
              s_("bvd"), 16)
        C.wait(nc.tensor, s_("wd"))
        C.wait(nc.vector, s_("bvd"))

        fcnt = [0]
        vslot = lambda m: m if m <= 3 else 4 + ((m - 4) % 5)

        def xn_rebuild(c):
            if c > 0:
                C.wait(nc.vector, s_("kmm"), C.cnt.get(s_("kmm"), 0))
                C.wait(nc.vector, s_("vmm"), C.cnt.get(s_("vmm"), 0))
            if c == 0:
                for k in range(16):
                    C.inc(nc.vector.memset(xnt[:, 576 * k:576 * k + 64], 0.0),
                          s_("xe"))
            else:
                for k in range(16):
                    C.inc(nc.vector.tensor_copy(xnt[:, 576 * k:576 * k + 64],
                                                stash[:, 64 * k:64 * k + 64]),
                          s_("xe"))
            for ls in range(4):
                sub = 4 * c + ls
                buf = xs[ls % 2]
                if 4 * c + ls >= 2:
                    C.wait(nc.sync, s_("xs"), C.cnt[s_("xs")] - 1)
                C.inc(nc.sync.dma_start(buf[:], h["x"][128 * sub:128 * sub + 128, :]),
                      s_("xd"), 16)
                C.wait(nc.scalar, s_("xd"))
                if C.cnt.get(s_("xtp"), 0) > 0:
                    C.wait(nc.scalar, s_("xtp"))
                C.inc(nc.scalar.activation(xnb[:], buf[:], AF.Identity,
                                           bias=P["stats_nmi"][:, sub:sub + 1],
                                           scale=P["stats_isd"][:, sub:sub + 1]),
                      s_("xs"))
                C.wait(nc.tensor, s_("xs"))
                for hh2 in range(2):
                    prs = [(ptrb[hh2][:, 128 * t:128 * t + 128],
                            xnb[:, 128 * (8 * hh2 + t):128 * (8 * hh2 + t) + 128])
                           for t in range(8)]
                    ins = tp_into(hh2, prs, P["idb"][:])
                    C.inc(ins, s_("xtp"))
                    C.wait(nc.vector, s_("xtp"))
                    for t in range(8):
                        k = 8 * hh2 + t
                        cc = nc.vector.tensor_copy(
                            xnt[:, 576 * k + 64 + 128 * ls:
                                576 * k + 64 + 128 * ls + 128],
                            ptrb[hh2][:, 128 * t:128 * t + 128])
                        C.inc(cc, s_(f"pe{hh2}"))
            C.wait(nc.vector, s_("xe"))
            C.wait(nc.vector, s_("pe0"))
            C.wait(nc.vector, s_("pe1"))
            for k in range(16):
                C.inc(nc.vector.tensor_copy(stash[:, 64 * k:64 * k + 64],
                                            xnt[:, 576 * k + 512:576 * k + 576]),
                      s_("sst"))
            C.inc(nc.vector.nop(), s_("xrd"))
            C.wait(nc.tensor, s_("xrd"))

        def k_proj(c):
            slot = kt[c % 2]
            if c >= 2:
                C.wait(nc.scalar, s_("lsc"), C.cnt.get(s_("lsc"), 0))
                C.wait(nc.scalar, s_("gsc"), C.cnt.get(s_("gsc"), 0))
            for hh in range(8):
                hd = 8 * hg + hh
                if C.cnt.get(s_("kev"), 0) > 0:
                    C.wait(nc.tensor, s_("kev"), C.cnt[s_("kev")])
                C.wait(nc.tensor, s_("lex"), C.cnt.get(s_("lex"), 0))
                C.wait(nc.tensor, s_("gex"), C.cnt.get(s_("gex"), 0))
                for k in range(16):
                    nc.tensor.matmul(pk[:], wk_sb[k][:, 128 * hh:128 * hh + 128],
                                     xnt[:, 576 * k + 64:576 * k + 576],
                                     start=(k == 0), stop=(k == 15))
                    ins = nc.tensor.matmul(psc[0:128, 0:64],
                                           wk_sb[k][:, 128 * hh:128 * hh + 128],
                                           xnt[:, 576 * k:576 * k + 64],
                                           start=(k == 0), stop=(k == 15))
                C.inc(ins, s_("kmm"))
                C.wait(nc.scalar, s_("kmm"))
                nc.scalar.activation(slot[:, 576 * hh + 64:576 * hh + 576],
                                     pk[:], AF.Identity,
                                     bias=P["bkh"][:, hd:hd + 1])
                a2 = nc.scalar.activation(slot[:, 576 * hh:576 * hh + 64],
                                          psc[0:128, 0:64], AF.Identity,
                                          bias=P["bkh"][:, hd:hd + 1])
                C.inc(a2, s_("kev"))
                if c == 0:
                    C.inc(nc.scalar.copy(kt0[:, 576 * hh:576 * hh + 576],
                                         slot[:, 576 * hh:576 * hh + 576]),
                          s_("kev"))

        def v_proj(c):
            if c >= 2:
                C.wait(nc.vector, s_("pve"), C.cnt.get(s_("pve"), 0))
                C.wait(nc.vector, s_("gpv"), C.cnt.get(s_("gpv"), 0))
            for i in range(4):
                m = 4 * c + i
                slot = vt[vslot(m)]
                if C.cnt.get(s_("vev"), 0) > 0:
                    C.wait(nc.tensor, s_("vev"), C.cnt[s_("vev")])
                for k in range(16):
                    nc.tensor.matmul(pv0[:],
                                     xnt[:, 576 * k + 128 * i:576 * k + 128 * i + 128],
                                     wv_sb[k][:, 0:512], start=(k == 0), stop=(k == 15))
                    ins = nc.tensor.matmul(pv1[:],
                                           xnt[:, 576 * k + 128 * i:576 * k + 128 * i + 128],
                                           wv_sb[k][:, 512:1024],
                                           start=(k == 0), stop=(k == 15))
                C.inc(ins, s_("vmm"))
                C.wait(nc.vector, s_("vmm"))
                nc.vector.tensor_add(slot[:, 0:512], pv0[:], bv_sb[:, 0:512])
                C.inc(nc.vector.tensor_add(slot[:, 512:1024], pv1[:],
                                           bv_sb[:, 512:1024]), s_("vev"))
            if c == 7:
                C.wait(nc.tensor, s_("vev"), C.cnt[s_("vev")])
                for k in range(16):
                    nc.tensor.matmul(pv0[0:64, :], xnt[:, 576 * k + 512:576 * k + 576],
                                     wv_sb[k][:, 0:512], start=(k == 0), stop=(k == 15))
                    ins = nc.tensor.matmul(pv1[0:64, :],
                                           xnt[:, 576 * k + 512:576 * k + 576],
                                           wv_sb[k][:, 512:1024],
                                           start=(k == 0), stop=(k == 15))
                C.inc(ins, s_("vmm"))
                C.wait(nc.vector, s_("vmm"))
                nc.vector.tensor_add(vt[0][0:64, 0:512], pv0[0:64, :],
                                     bv_sb[0:64, 0:512])
                C.inc(nc.vector.tensor_add(vt[0][0:64, 512:1024], pv1[0:64, :],
                                           bv_sb[0:64, 512:1024]), s_("vev"))
                for hh in range(8):
                    hd = 8 * hg + hh
                    C.wait(nc.tensor, s_("kev"), C.cnt[s_("kev")])
                    for k in range(16):
                        ins = nc.tensor.matmul(
                            psc[0:128, 0:64],
                            wk_sb[k][:, 128 * hh:128 * hh + 128],
                            xnt[:, 576 * k + 512:576 * k + 576],
                            start=(k == 0), stop=(k == 15))
                    C.inc(ins, s_("kmm"))
                    C.wait(nc.scalar, s_("kmm"))
                    C.inc(nc.scalar.activation(kt0[:, 576 * hh:576 * hh + 64],
                                               psc[0:128, 0:64], AF.Identity,
                                               bias=P["bkh"][:, hd:hd + 1]),
                          s_("kev"))

        def kt_cols(tok0, tok1, hh, use_kt0):
            pieces = []
            t = tok0
            while t < tok1:
                cprime = (t + 64) // 512
                ten = kt0 if (cprime == 0 and use_kt0) else kt[cprime % 2]
                base = 512 * cprime - 64
                hi = min(tok1, base + 576)
                pieces.append((ten, 576 * hh + (t - base), 576 * hh + (hi - base),
                               t - tok0))
                t = hi
            return pieces

        def local_qb(qb, use_kt0=False):
            if C.cnt.get(s_("qtp"), 0) > 0:
                C.wait(nc.sync, s_("qtp"))
            C.inc(nc.sync.dma_start(qsb[:],
                                    h["qsp"][qb, 0:128, 1024 * hg:1024 * hg + 1024]),
                  s_("qd"), 16)
            C.wait(nc.tensor, s_("qd"))
            for hh in range(8):
                hd = 8 * hg + hh
                prs = [(ptrb[0][:, 0:128],
                        qsb[:, 128 * hh:128 * hh + 128])]
                ins = tp_into(0, prs, P["idb"][:])
                C.inc(ins, s_("qtp"))
                C.wait(nc.vector, s_("qtp"))
                cc = nc.vector.tensor_scalar(qt_t[:], ptrb[0][:, 0:128],
                                             0.08838834764831845, None, AX.mult)
                C.inc(cc, s_("pe0"))
                C.wait(nc.tensor, s_("pe0"))
                C.wait(nc.tensor, s_("kev"), C.cnt.get(s_("kev"), 0))
                C.wait(nc.tensor, s_("lex"), C.cnt.get(s_("lex"), 0))
                C.wait(nc.tensor, s_("gex"), C.cnt.get(s_("gex"), 0))
                qs = 128 * qb
                for ten, c0, c1, d0 in kt_cols(qs - 64, qs + 192, hh, use_kt0):
                    ins = nc.tensor.matmul(psc[0:128, d0:d0 + (c1 - c0)], qt_t[:],
                                           ten[:, c0:c1], start=True, stop=True)
                C.inc(ins, s_("lsc"))
                C.wait(nc.vector, s_("lsc"))
                C.inc(nc.vector.reduce_max(nmx, psc[0:128, 0:256], AXL.X,
                                           negate=True), s_("lmx"))
                C.wait(nc.scalar, s_("lmx"))
                C.wait(nc.scalar, s_("ptp"), C.cnt.get(s_("ptp"), 0))
                masked = qb in (0, 15)
                e = nc.scalar.activation(probs[:], psc[0:128, 0:256], AF.Exp,
                                         bias=nmx)
                C.inc(e, s_("lex"))
                C.wait(nc.vector, s_("lex"))
                psrc = probs
                if masked:
                    moff = 0 if qb == 0 else 256
                    C.inc(nc.vector.tensor_mul(probs2[:], probs[:],
                                               maskmul[:, moff:moff + 256]),
                          s_("lpm"))
                    C.wait(nc.vector, s_("lpm"))
                    psrc = probs2
                C.inc(nc.vector.reduce_sum(ssum, psrc[:], AXL.X), s_("lsm"))
                C.wait(nc.vector, s_("lsm"))
                C.inc(nc.vector.reciprocal(rsum, ssum), s_("lrc"))
                C.wait(nc.tensor, s_("lex"))
                if masked:
                    C.wait(nc.tensor, s_("lpm"))
                prs = [(ptrb[1][:, 128 * i2:128 * i2 + 128],
                        psrc[:, 128 * i2:128 * i2 + 128]) for i2 in range(2)]
                ins = tp_into(1, prs, P["idb"][:])
                C.inc(ins, s_("ptp"))
                C.wait(nc.vector, s_("ptp"))
                cc = nc.vector.tensor_copy(probsT[:], ptrb[1][:, 0:256])
                C.inc(cc, s_("pe1"))
                C.wait(nc.tensor, s_("pe1"))
                C.wait(nc.tensor, s_("vev"), C.cnt.get(s_("vev"), 0))
                C.wait(nc.tensor, s_("cev"), C.cnt.get(s_("cev"), 0))
                nc.tensor.matmul(pctx[:], probsT[:, 0:128],
                                 vt[vslot(qb)][:, 128 * hh:128 * hh + 128],
                                 start=True, stop=False)
                C.inc(nc.tensor.matmul(pctx[:], probsT[:, 128:256],
                                       vt[vslot(qb + 1)][:, 128 * hh:128 * hh + 128],
                                       start=False, stop=True), s_("pve"))
                C.wait(nc.vector, s_("pve"))
                C.wait(nc.vector, s_("lrc"))
                cst = ctx_st[qb % 2]
                if hh == 0 and C.cnt.get(s_("cso"), 0) >= 32:
                    C.wait(nc.vector, s_("cso"), C.cnt[s_("cso")] - 16)
                C.inc(nc.vector.tensor_scalar(cst[:, 128 * hh:128 * hh + 128],
                                              pctx[:], rsum, None, AX.mult),
                      s_("cev"))
            C.wait(nc.scalar, s_("cev"))
            C.inc(nc.scalar.dma_start(
                h["ctxh"][qb, 0:128, 1024 * hg:1024 * hg + 1024],
                ctx_st[qb % 2][:]), s_("cso"), 16)

        def flash_step(fc, use_kt0=False):
            p = fcnt[0] % 2
            for hh in range(8):
                hd = 8 * hg + hh
                ten = kt0 if use_kt0 else kt[fc % 2]
                C.wait(nc.tensor, s_("gex"), C.cnt.get(s_("gex"), 0))
                C.wait(nc.tensor, s_("lex"), C.cnt.get(s_("lex"), 0))
                C.wait(nc.tensor, s_("kev"), C.cnt.get(s_("kev"), 0))
                C.inc(nc.tensor.matmul(psc[0:TOPK, 0:512],
                                       P["qgt"][:, TOPK * hd:TOPK * hd + TOPK],
                                       ten[:, 576 * hh:576 * hh + 512],
                                       start=True, stop=True), s_("gsc"))
                C.wait(nc.vector, s_("gsc"))
                C.inc(nc.vector.reduce_max(gmc, psc[0:TOPK, 0:512], AXL.X),
                      s_("gmx"))
                C.wait(nc.vector, s_("gmx"))
                C.inc(nc.vector.tensor_max(rm[1 - p][:, hh:hh + 1],
                                           rm[p][:, hh:hh + 1], gmc), s_("gnm"))
                C.wait(nc.vector, s_("gnm"))
                C.inc(nc.vector.tensor_sub(gda, rm[p][:, hh:hh + 1],
                                           rm[1 - p][:, hh:hh + 1]), s_("gda"))
                C.inc(nc.vector.tensor_scalar(gnn, rm[1 - p][:, hh:hh + 1],
                                              -1.0, None, AX.mult), s_("gnn"))
                C.wait(nc.scalar, s_("gda"))
                C.inc(nc.scalar.activation(gal, gda, AF.Exp), s_("gal"))
                C.wait(nc.scalar, s_("gnn"))
                C.wait(nc.scalar, s_("gtp"), C.cnt.get(s_("gtp"), 0))
                e = nc.scalar.activation(pgb[:], psc[0:TOPK, 0:512], AF.Exp,
                                         bias=gnn)
                C.inc(e, s_("gex"))
                C.wait(nc.vector, s_("gex"))
                C.inc(nc.vector.reduce_sum(gcs, pgb[:], AXL.X), s_("gcr"))
                C.wait(nc.vector, s_("gcr"))
                C.wait(nc.vector, s_("gal"))
                C.inc(nc.vector.scalar_tensor_tensor(
                    rs[1 - p][:, hh:hh + 1], rs[p][:, hh:hh + 1], gal, gcs,
                    AX.mult, AX.add), s_("grs"))
                prs = [(ptrb[1][:, 64 * i2:64 * i2 + 64],
                        pgb[:, 128 * i2:128 * i2 + 128]) for i2 in range(4)]
                C.wait(nc.tensor, s_("gex"))
                ins = tp_into(1, prs, P["idb"][0:TOPK, 0:TOPK])
                C.inc(ins, s_("gtp"))
                C.wait(nc.vector, s_("gtp"))
                cc = nc.vector.tensor_copy(pgbT[:], ptrb[1][:, 0:256])
                C.inc(cc, s_("pe1"))
                C.wait(nc.tensor, s_("pe1"))
                C.wait(nc.tensor, s_("vev"), C.cnt.get(s_("vev"), 0))
                for i2 in range(4):
                    m = 4 * fc + i2
                    ins = nc.tensor.matmul(ppv[:], pgbT[:, 64 * i2:64 * i2 + 64],
                                           vt[vslot(m)][:, 128 * hh:128 * hh + 128],
                                           start=(i2 == 0), stop=(i2 == 3))
                C.inc(ins, s_("gpv"))
                C.wait(nc.vector, s_("gpv"))
                C.inc(nc.vector.scalar_tensor_tensor(
                    gc[1 - p][:, 128 * hh:128 * hh + 128],
                    gc[p][:, 128 * hh:128 * hh + 128], gal, ppv[:],
                    AX.mult, AX.add), s_("ggc"))
                C.wait(nc.tensor, s_("ggc"))
            fcnt[0] += 1

        ready = {c: [] for c in range(NCH)}
        for qb in range(1, NSUB):
            ready[(128 * qb + 191) // 512].append(qb)

        for c in range(NCH):
            xn_rebuild(c)
            k_proj(c)
            v_proj(c)
            if c >= 1:
                flash_step(c)
            for qb in ready[c]:
                local_qb(qb)
        flash_step(0, use_kt0=True)
        local_qb(0, use_kt0=True)

        pfin = fcnt[0] % 2
        for hh in range(8):
            hd = 8 * hg + hh
            C.inc(nc.vector.reciprocal(grr, rs[pfin][:, hh:hh + 1]), s_("gfr"))
            C.wait(nc.vector, s_("gfr"))
            C.inc(nc.vector.tensor_scalar(P["gfin"][:, 128 * hd:128 * hd + 128],
                                          gc[pfin][:, 128 * hh:128 * hh + 128],
                                          grr, None, AX.mult), s_("gfw"))
        C.wait(nc.sync, s_("cso"))
        C.wait(nc.sync, s_("gfw"))
        if hg == 0 and "ktz" in C.dbg:
            C.wait(nc.gpsimd, s_("kev"), C.cnt[s_("kev")])
            C.wait(nc.gpsimd, s_("vev"), C.cnt[s_("vev")])
            C.dbg_dump("ktz", kt0[:], [128, 8 * 576], BF16)
            C.dbg_dump("kta", kt[0][:], [128, 8 * 576], BF16)
            C.dbg_dump("ktb", kt[1][:], [128, 8 * 576], BF16)
            for i in range(4):
                C.dbg_dump(f"vt{i}", vt[i][:], [128, 1024], BF16)


def phase_c(C, h, P):
    nc = C.nc
    s_ = lambda n: "c_" + n
    SC = 1.0 / float(DM)
    with ExitStack() as st:
        sb = lambda n, sh, dt=F32: st.enter_context(nc.sbuf_tensor(s_(n), sh, dt))
        ps = lambda n, sh, dt=F32: st.enter_context(nc.psum_tensor(s_(n), sh, dt))
        r_sb = sb("r", [128, DFF])
        xqt = sb("xqt", [128, 16 * 256], BF16)
        x2qt = sb("x2qt", [128, 64 * 256], BF16)
        w1sb = sb("w1sb", [128, DFF], BF16)
        w2sb = sb("w2sb", [128, DFF], BF16)
        g2b = sb("g2b", [128, DFF], BF16)
        g1b = sb("g1b", [128, DM], BF16)
        xsub = [sb(f"xs{i}", [128, DM]) for i in range(2)]
        ctxb = [sb(f"cb{i}", [128, DM], BF16) for i in range(2)]
        accp = [sb(f"acc{i}", [128, DM]) for i in range(2)]
        yst = [sb(f"yst{i}", [128, DM]) for i in range(2)]
        scrA = sb("scrA", [128, 512])
        scrB = sb("scrB", [128, 512])
        scrC = sb("scrC", [128, 512])
        scrI = sb("scrI", [128, 512], I32)
        qs_bf = sb("qs_bf", [128, 512], BF16)
        mtb = sb("mtb", [128, TOPK], BF16)
        mbf = sb("mbf", [TOPK, 128], BF16)
        mxp = sb("mxp", [128, 16])
        s4p = sb("s4p", [128, 16])
        selbar = sb("selbar", [128, NH])
        n_ = {}
        for nm in ("u1 u2 u3 rms1 rr1 mxa mxn mxn2 mxc rmx s1 cA colsc d1 rdc c1 "
                   "c2 c4 tt tt2 rms2 rr2 m2a m2b m2c rm2 s2 cs2a cs2b d2 rd2 c2f "
                   "mxu s4").split():
            n_[nm] = sb("n_" + nm, [128, 1])
        c1c = sb("c1c", [128, 2])
        c2fc = sb("c2fc", [128, 2])
        ptrb = ps("ptrb", [128, 1024], BF16)
        ptrf = ps("ptrf", [128, 512])
        g = [ps(f"g{i}", [128, 512]) for i in range(6)]

        C.inc(nc.gpsimd.dma_start(g2b[:], h["g2"][0:128, :]), s_("gld"), 16)
        C.inc(nc.gpsimd.dma_start(g1b[:], h["g1"][0:128, :]), s_("gld"), 16)
        C.wait(nc.vector, s_("gld"))

        def gate(ins, tag, eng):
            C.inc(ins, s_(tag))
            C.wait(eng, s_(tag))

        def tp_into(pairs, ident, pbank):
            C.wait(nc.tensor, s_("pe"), C.cnt.get(s_("pe"), 0))
            ins = None
            for dst, srcp in pairs:
                ins = nc.tensor.transpose(dst, srcp, ident)
            return ins

        for T in range(NCH):
            for ls in range(2):
                sub = 2 * T + ls
                if T > 0:
                    C.wait(nc.sync, s_("acc"), C.cnt.get(s_("acc"), 0))
                    C.wait(nc.sync, s_("bl"), C.cnt.get(s_("bl"), 0))
                C.inc(nc.sync.dma_start(xsub[ls][:],
                                        h["x"][128 * sub:128 * sub + 128, :]),
                      s_("xd"), 16)
                C.inc(nc.sync.dma_start(ctxb[ls][:], h["ctxh"][sub]), s_("cd"), 16)
            for ls in range(2):
                sub = 2 * T + ls
                C.wait(nc.vector, s_("xd"), 16 * (2 * T + ls + 1))
                for sl in range(4):
                    c0 = 512 * sl
                    gate(nc.vector.scalar_tensor_tensor(
                        scrA[:], xsub[ls][:, c0:c0 + 512],
                        P["stats_m"][:, sub:sub + 1], g1b[:, c0:c0 + 512],
                        AX.subtract, AX.mult), "t1", nc.vector)
                    C.inc(nc.vector.reduce_max(mxp[:, sl:sl + 1], scrA[:], AXL.X,
                                               apply_absolute_value=True), s_("t2p"))
                C.wait(nc.vector, s_("t2p"))
                gate(nc.vector.reduce_max(n_["mxa"][:], mxp[:, 0:4], AXL.X),
                     "nx", nc.vector)
                isd = P["stats_isd"][:, sub:sub + 1]
                gate(nc.vector.tensor_mul(n_["u1"][:], isd, isd), "nx", nc.vector)
                gate(nc.vector.tensor_scalar(n_["u2"][:], n_["u1"][:], -1.0e-5,
                                             None, AX.mult), "nx", nc.vector)
                gate(nc.vector.tensor_scalar(n_["u3"][:], n_["u2"][:], 1.000001,
                                             None, AX.add), "nx", nc.vector)
                C.wait(nc.scalar, s_("nx"))
                gate(nc.scalar.activation(n_["rms1"][:], n_["u3"][:], AF.Sqrt),
                     "ns", nc.vector)
                gate(nc.vector.reciprocal(n_["rr1"][:], n_["rms1"][:]), "nx", nc.vector)
                gate(nc.vector.tensor_mul(n_["mxn"][:], n_["mxa"][:], isd),
                     "nx", nc.vector)
                gate(nc.vector.tensor_mul(n_["mxn2"][:], n_["mxn"][:], n_["rr1"][:]),
                     "nx", nc.vector)
                gate(nc.vector.tensor_scalar(n_["mxc"][:], n_["mxn2"][:], 1.0e-5,
                                             None, AX.max), "nx", nc.vector)
                gate(nc.vector.reciprocal(n_["rmx"][:], n_["mxc"][:]), "nx", nc.vector)
                gate(nc.vector.tensor_scalar(n_["s1"][:], n_["rmx"][:], 127.0,
                                             None, AX.mult), "nx", nc.vector)
                gate(nc.vector.tensor_mul(n_["cA"][:], isd, n_["rr1"][:]),
                     "nx", nc.vector)
                gate(nc.vector.tensor_mul(n_["colsc"][:], n_["cA"][:], n_["s1"][:]),
                     "nx", nc.vector)
                gate(nc.vector.tensor_mul(n_["d1"][:], n_["s1"][:],
                                          P["swv"][:, 0:1]), "nx", nc.vector)
                gate(nc.vector.reciprocal(n_["rdc"][:], n_["d1"][:]), "nx", nc.vector)
                gate(nc.vector.tensor_scalar(c1c[:, ls:ls + 1], n_["rdc"][:], 0.5,
                                             None, AX.mult), "nx", nc.vector)
                for sl in range(4):
                    c0 = 512 * sl
                    gate(nc.vector.scalar_tensor_tensor(
                        scrB[:], xsub[ls][:, c0:c0 + 512],
                        P["stats_m"][:, sub:sub + 1], g1b[:, c0:c0 + 512],
                        AX.subtract, AX.mult), "t1b", nc.vector)
                    gate(nc.vector.tensor_scalar(scrA[:], scrB[:],
                                                 n_["colsc"][:], MAGIC,
                                                 AX.mult, AX.add), "qv", nc.vector)
                    gate(nc.vector.tensor_scalar(scrI[:], scrA[:], MAGIC, None,
                                                 AX.subtract), "qv", nc.vector)
                    C.wait(nc.scalar, s_("qv"))
                    if C.cnt.get(s_("qtp"), 0) > 0:
                        C.wait(nc.scalar, s_("qtp"))
                    C.inc(nc.scalar.copy(qs_bf[:], scrI[:]), s_("qb"))
                    C.wait(nc.tensor, s_("qb"))
                    prs = [(ptrb[:, 128 * j:128 * j + 128],
                            qs_bf[:, 128 * j:128 * j + 128]) for j in range(4)]
                    ins = tp_into(prs, P["idb"][:], ptrb)
                    C.inc(ins, s_("qtp"))
                    C.wait(nc.vector, s_("qtp"))
                    for j in range(4):
                        k = 4 * sl + j
                        cc = nc.vector.tensor_copy(
                            xqt[:, 256 * k + 128 * ls:256 * k + 128 * ls + 128],
                            ptrb[:, 128 * j:128 * j + 128])
                        C.inc(cc, s_("pe"))
                C.wait(nc.tensor, s_("pe"))
                C.wait(nc.scalar, s_("x2a"), C.cnt.get(s_("x2a"), 0))
                for ffs in range(16):
                    wb = w1sb if ffs % 2 == 0 else w2sb
                    if ffs == 0 and C.cnt.get(s_("f2m"), 0) > 0:
                        C.wait(nc.gpsimd, s_("f2m"))
                    if C.cnt.get(s_("f1m"), 0) > 1:
                        C.wait(nc.gpsimd, s_("f1m"), C.cnt[s_("f1m")] - 1)
                    C.inc(nc.gpsimd.dma_start(wb[:], h["w1t"][ffs]), s_("w1d"), 16)
                    C.wait(nc.tensor, s_("w1d"))
                    pb = g[ffs % 2]
                    if C.cnt.get(s_("rel"), 0) > 1:
                        C.wait(nc.tensor, s_("rel"), C.cnt[s_("rel")] - 1)
                    for k in range(16):
                        ins = nc.tensor.matmul(
                            pb[:], xqt[:, 256 * k + 128 * ls:256 * k + 128 * ls + 128],
                            wb[:, 512 * k:512 * k + 512],
                            start=(k == 0), stop=(k == 15))
                    C.inc(ins, s_("f1m"))
                    C.wait(nc.scalar, s_("f1m"))
                    C.inc(nc.scalar.activation(r_sb[:, 512 * ffs:512 * ffs + 512],
                                               pb[:], AF.Relu), s_("rel"))
                C.wait(nc.vector, s_("rel"), C.cnt[s_("rel")])
                if T == 0 and ls == 0 and "r0" in C.dbg:
                    C.wait(nc.gpsimd, s_("rel"), C.cnt[s_("rel")])
                    C.dbg_dump("r0", r_sb[:], [128, DFF])
                C.wait(nc.scalar, s_("ytp"), C.cnt.get(s_("ytp"), 0))
                for sl in range(16):
                    c0 = 512 * sl
                    nc.vector.tensor_mul(scrA[:], r_sb[:, c0:c0 + 512], g2b[:, c0:c0 + 512])
                    gate(nc.vector.tensor_mul(scrB[:], scrA[:], r_sb[:, c0:c0 + 512]),
                         "st1", nc.vector)
                    C.inc(nc.vector.reduce_max(mxp[:, sl:sl + 1], scrB[:], AXL.X,
                                               apply_absolute_value=True), s_("st2"))
                    C.wait(nc.scalar, s_("rel"), C.cnt[s_("rel")])
                    C.inc(nc.scalar.square(scrC[:], r_sb[:, c0:c0 + 512]), s_("sq1"))
                    C.wait(nc.scalar, s_("sq1"))
                    C.wait(nc.scalar, s_("st1"), C.cnt.get(s_("st1"), 0))
                    C.inc(nc.scalar.square(scrA[:], scrC[:]), s_("sq2"))
                    C.wait(nc.vector, s_("sq2"))
                    C.inc(nc.vector.reduce_sum(s4p[:, sl:sl + 1], scrA[:], AXL.X),
                          s_("st3"))
                C.wait(nc.vector, s_("st2"))
                C.wait(nc.vector, s_("st3"))
                gate(nc.vector.reduce_max(n_["mxu"][:], mxp[:], AXL.X), "nx", nc.vector)
                gate(nc.vector.reduce_sum(n_["s4"][:], s4p[:], AXL.X), "nx", nc.vector)
                c1 = c1c[:, ls:ls + 1]
                gate(nc.vector.tensor_mul(n_["c2"][:], c1, c1), "nx", nc.vector)
                gate(nc.vector.tensor_mul(n_["c4"][:], n_["c2"][:], n_["c2"][:]),
                     "nx", nc.vector)
                gate(nc.vector.tensor_mul(n_["tt"][:], n_["s4"][:], n_["c4"][:]),
                     "nx", nc.vector)
                gate(nc.vector.tensor_scalar(n_["tt2"][:], n_["tt"][:], 1.0 / DFF,
                                             None, AX.mult), "nx", nc.vector)
                C.wait(nc.scalar, s_("nx"))
                gate(nc.scalar.activation(n_["rms2"][:], n_["tt2"][:], AF.Sqrt,
                                          bias=P["eps6"][:]), "ns", nc.vector)
                gate(nc.vector.reciprocal(n_["rr2"][:], n_["rms2"][:]), "nx", nc.vector)
                gate(nc.vector.tensor_mul(n_["m2a"][:], n_["mxu"][:], n_["c2"][:]),
                     "nx", nc.vector)
                gate(nc.vector.tensor_mul(n_["m2b"][:], n_["m2a"][:], n_["rr2"][:]),
                     "nx", nc.vector)
                gate(nc.vector.tensor_scalar(n_["m2c"][:], n_["m2b"][:], 1.0e-5,
                                             None, AX.max), "nx", nc.vector)
                gate(nc.vector.reciprocal(n_["rm2"][:], n_["m2c"][:]), "nx", nc.vector)
                gate(nc.vector.tensor_scalar(n_["s2"][:], n_["rm2"][:], 127.0,
                                             None, AX.mult), "nx", nc.vector)
                gate(nc.vector.tensor_mul(n_["cs2a"][:], n_["c2"][:], n_["rr2"][:]),
                     "nx", nc.vector)
                gate(nc.vector.tensor_mul(n_["cs2b"][:], n_["cs2a"][:], n_["s2"][:]),
                     "nx", nc.vector)
                gate(nc.vector.tensor_mul(n_["d2"][:], n_["s2"][:], P["swv"][:, 1:2]),
                     "nx", nc.vector)
                gate(nc.vector.reciprocal(n_["rd2"][:], n_["d2"][:]), "nx", nc.vector)
                gate(nc.vector.tensor_scalar(c2fc[:, ls:ls + 1], n_["rd2"][:], 0.5,
                                             None, AX.mult), "nx", nc.vector)
                for sl in range(16):
                    c0 = 512 * sl
                    nc.vector.tensor_mul(scrA[:], r_sb[:, c0:c0 + 512],
                                         g2b[:, c0:c0 + 512])
                    gate(nc.vector.tensor_mul(scrB[:], scrA[:], r_sb[:, c0:c0 + 512]),
                         "x2a", nc.vector)
                    gate(nc.vector.tensor_scalar(scrA[:], scrB[:], n_["cs2b"][:],
                                                 MAGIC, AX.mult, AX.add),
                         "x2b", nc.vector)
                    gate(nc.vector.tensor_scalar(scrI[:], scrA[:], MAGIC, None,
                                                 AX.subtract), "x2c", nc.vector)
                    C.wait(nc.scalar, s_("x2c"))
                    if C.cnt.get(s_("x2tp"), 0) > 0:
                        C.wait(nc.scalar, s_("x2tp"))
                    C.inc(nc.scalar.copy(qs_bf[:], scrI[:]), s_("x2q"))
                    C.wait(nc.tensor, s_("x2q"))
                    prs = [(ptrb[:, 128 * j:128 * j + 128],
                            qs_bf[:, 128 * j:128 * j + 128]) for j in range(4)]
                    ins = tp_into(prs, P["idb"][:], ptrb)
                    C.inc(ins, s_("x2tp"))
                    C.wait(nc.vector, s_("x2tp"))
                    for j in range(4):
                        f = 4 * sl + j
                        cc = nc.vector.tensor_copy(
                            x2qt[:, 256 * f + 128 * ls:256 * f + 128 * ls + 128],
                            ptrb[:, 128 * j:128 * j + 128])
                        C.inc(cc, s_("pe"))
            C.wait(nc.vector, s_("yo"), C.cnt.get(s_("yo"), 0))
            for ls in range(2):
                sub = 2 * T + ls
                gate(nc.vector.tensor_scalar(selbar[:], P["sel_tm"][:, 16 * sub:16 * sub + 16],
                                             -1.0, 1.0, AX.mult, AX.add),
                     "sbv", nc.vector)
                C.wait(nc.vector, s_("cd"), 16 * (2 * T + ls + 1))
                for hd in range(NH):
                    gate(nc.vector.tensor_scalar(
                        mtb[:], P["iota64"][:],
                        P["rank_tm"][:, 16 * sub + hd:16 * sub + hd + 1],
                        P["sel_tm"][:, 16 * sub + hd:16 * sub + hd + 1],
                        AX.is_equal, AX.mult), "mtv", nc.tensor)
                    ins = tp_into([(ptrb[0:TOPK, 0:128], mtb[:])], P["idb"][:], ptrb)
                    C.inc(ins, s_("mtp"))
                    C.wait(nc.vector, s_("mtp"))
                    cc = nc.vector.tensor_copy(mbf[:], ptrb[0:TOPK, 0:128])
                    C.inc(cc, s_("pe"))
                    C.wait(nc.tensor, s_("pe"))
                    if hd == 0:
                        C.wait(nc.tensor, s_("bl"), C.cnt.get(s_("bl"), 0))
                    C.inc(nc.tensor.matmul(
                        g[2 + hd // 4][:, 128 * (hd % 4):128 * (hd % 4) + 128],
                        mbf[:], P["gfin"][:, 128 * hd:128 * hd + 128],
                        start=True, stop=True), s_("cgm"))
                C.wait(nc.vector, s_("cgm"))
                for hd in range(NH):
                    C.inc(nc.vector.scalar_tensor_tensor(
                        yst[ls][:, 128 * hd:128 * hd + 128],
                        ctxb[ls][:, 128 * hd:128 * hd + 128],
                        selbar[:, hd:hd + 1],
                        g[2 + hd // 4][:, 128 * (hd % 4):128 * (hd % 4) + 128],
                        AX.mult, AX.add), s_("bl"))
                C.wait(nc.vector, s_("bl"))
                C.inc(nc.vector.tensor_add(accp[ls][:], yst[ls][:], xsub[ls][:]),
                      s_("acc"))
            C.wait(nc.tensor, s_("pe"))
            if T == 0 and "acc0" in C.dbg:
                C.wait(nc.gpsimd, s_("acc"), C.cnt[s_("acc")])
                C.wait(nc.gpsimd, s_("pe"), C.cnt[s_("pe")])
                C.wait(nc.gpsimd, s_("nx"), C.cnt[s_("nx")])
                C.dbg_dump("acc0", accp[0][:], [128, DM])
                C.dbg_dump("acc1", accp[1][:], [128, DM])
                C.dbg_dump("x2qt0", x2qt[:], [128, 64 * 256], BF16)
                C.dbg_dump("xqt0", xqt[:], [128, 16 * 256], BF16)
                C.dbg_dump("c1c", c1c[:], [128, 2])
                C.dbg_dump("c2fc", c2fc[:], [128, 2])
            C.wait(nc.vector, s_("acc"))
            for b in range(16):
                wb2 = w2sb if b % 2 == 0 else w1sb
                if b == 0:
                    C.wait(nc.gpsimd, s_("f1m"), C.cnt[s_("f1m")])
                if C.cnt.get(s_("f2m"), 0) > 1:
                    C.wait(nc.gpsimd, s_("f2m"), C.cnt[s_("f2m")] - 1)
                C.inc(nc.gpsimd.dma_start(wb2[:], h["w2t"][b]), s_("w2d"), 16)
                C.wait(nc.tensor, s_("w2d"))
                pb = g[b % 2]
                if C.cnt.get(s_("o2e"), 0) > 1:
                    C.wait(nc.tensor, s_("o2e"), C.cnt[s_("o2e")] - 1)
                for k in range(64):
                    ins = nc.tensor.matmul(pb[:, 0:256], wb2[:, 128 * k:128 * k + 128],
                                           x2qt[:, 256 * k:256 * k + 256],
                                           start=(k == 0), stop=(k == 63))
                C.inc(ins, s_("f2m"))
                C.wait(nc.vector, s_("f2m"))
                C.inc(nc.vector.tensor_copy(scrC[:, 0:256], pb[:, 0:256]), s_("o2e"))
                C.wait(nc.tensor, s_("o2e"))
                C.wait(nc.tensor, s_("yw"), C.cnt.get(s_("yw"), 0))
                for ls in range(2):
                    ins = nc.tensor.transpose(ptrf[:, 128 * ls:128 * ls + 128],
                                              scrC[:, 128 * ls:128 * ls + 128],
                                              P["idf"][:])
                C.inc(ins, s_("ytp"))
                C.wait(nc.vector, s_("ytp"))
                for ls in range(2):
                    cc = nc.vector.scalar_tensor_tensor(
                        yst[ls][:, 128 * b:128 * b + 128],
                        ptrf[:, 128 * ls:128 * ls + 128], c2fc[:, ls:ls + 1],
                        accp[ls][:, 128 * b:128 * b + 128], AX.mult, AX.add)
                    C.inc(cc, s_("yw"))
            C.wait(nc.scalar, s_("yw"))
            for ls in range(2):
                sub = 2 * T + ls
                C.inc(nc.scalar.dma_start(h["y"][128 * sub:128 * sub + 128, :],
                                          yst[ls][:]), s_("yo"), 16)
        C.wait(nc.sync, s_("yo"))


def _consts():
    c = {}
    c["ident_bf"] = np.eye(128, dtype=ml_dtypes.bfloat16)
    c["ident_f32"] = np.eye(128, dtype=np.float32)
    c["triu_bf"] = np.triu(np.ones((128, 128))).astype(ml_dtypes.bfloat16)
    c["ones_col_bf"] = np.ones((128, 1), dtype=ml_dtypes.bfloat16)
    c["ones_row_bf"] = np.ones((1, 128), dtype=ml_dtypes.bfloat16)
    c["ones_col_f"] = np.ones((128, 1), dtype=np.float32)
    c["ones_row_f"] = np.ones((1, 128), dtype=np.float32)
    c["iota64_rep"] = np.ascontiguousarray(
        np.tile(np.arange(1, TOPK + 1, dtype=np.float32), (128, 1)))
    return c


def _stage_core(inputs, core):
    seq = core // 2
    half = core % 2
    off = HALF * half
    x = np.asarray(inputs["L0_attn_hidden_states"][seq])
    m = {}
    m["x_seq"] = np.ascontiguousarray(np.roll(x, -off, axis=0))
    m["Wq"] = np.asarray(inputs["L0_attn_Wq"])
    m["Wk"] = np.asarray(inputs["L0_attn_Wk"])
    m["Wv"] = np.asarray(inputs["L0_attn_Wv"])
    m["w1"] = np.asarray(inputs["L0_ffn_w1"])
    m["w2"] = np.asarray(inputs["L0_ffn_w2"])
    m["bq_rep"] = np.ascontiguousarray(
        np.tile(np.asarray(inputs["L0_attn_bq"])[None, :], (128, 1)))
    m["bv_rep"] = np.ascontiguousarray(
        np.tile(np.asarray(inputs["L0_attn_bv"])[None, :], (128, 1)))
    m["bk_hm"] = np.ascontiguousarray(
        np.asarray(inputs["L0_attn_bk"]).reshape(NH, DH).T)
    m["g1_rep"] = np.ascontiguousarray(
        np.tile(np.asarray(inputs["L0_ffn_g1"])[None, :], (128, 1)))
    m["g2_rep"] = np.ascontiguousarray(
        np.tile(np.asarray(inputs["L0_ffn_g2"])[None, :], (128, 1)))
    m.update(_consts())
    mask = np.zeros((2, 128, 256), dtype=np.float32)
    if half == 0:
        mask[0, :, 0:64] = NEG
    else:
        mask[1, :, 192:256] = NEG
    m["maskadd"] = mask
    be = np.zeros((NH, SEQ), dtype=np.float32)
    be[:, (0 - off) % SEQ] = 1.0
    be[:, (SEQ - 1 - off) % SEQ] = 1.0
    m["boseos"] = be
    return m


def kernel(**inputs):
    nc = build_program(dbg=_DBG, phases=_PHASES)
    in_maps = [_stage_core(inputs, c) for c in range(8)]
    res = run_bass_kernel_spmd(nc, in_maps, core_ids=list(range(8)))
    out = np.empty((4, SEQ, DM), dtype=np.float32)
    for c in range(8):
        seq, half = c // 2, c % 2
        out[seq, HALF * half:HALF * half + HALF] = res.results[c]["y"]
    return out


# version 04
# speedup vs baseline: 1.2250x; 1.0042x over previous
import numpy as np
import ml_dtypes
from contextlib import ExitStack

import concourse.bass as bass
import concourse.mybir as mybir
from concourse.bass_utils import run_bass_kernel_spmd

F32 = mybir.dt.float32
BF16 = mybir.dt.bfloat16
I32 = mybir.dt.int32
FP8 = mybir.dt.float8e4
AX = mybir.AluOpType
AF = mybir.ActivationFunctionType
AXL = mybir.AxisListType

MAGIC = 12582912.0
NEG = -1.0e9
SEQ = 4096
DM = 2048
DFF = 8192
NH = 16
DH = 128
HALF = 2048
NSUB = 16
NSUBF = 32
NCH = 8
TOPK = 64

_PHASES = "wabc"
_DBG = ()


class Ctx:

    def __init__(self, nc, stack, dbg):
        self.nc = nc
        self.stack = stack
        self.dbg = set(dbg)
        self.dbg_dram = {}
        self.sems = {}
        self.cnt = {}
        self.sem("gbar")

    def sem(self, name):
        if name not in self.sems:
            self.sems[name] = self.stack.enter_context(self.nc.semaphore(name))
            self.cnt[name] = 0
        return self.sems[name]

    def inc(self, ins, name, n=1):
        ins.then_inc(self.sem(name), n)
        self.cnt[name] += n
        return self.cnt[name]

    def wait(self, eng, name, val=None):
        eng.wait_ge(self.sem(name), self.cnt[name] if val is None else val)

    def edge(self, eng, tag):
        name = "e_" + tag
        self.inc(eng.nop(), name)
        self.wait(eng, name)

    def barrier_all(self, name):
        nc = self.nc
        engines = [nc.sync, nc.scalar, nc.vector, nc.tensor, nc.gpsimd]
        s = self.sem(name)
        base = self.cnt[name]
        for e in engines:
            e.sem_inc(s, 1)
        self.cnt[name] = base + 5
        for e in engines:
            e.wait_ge(s, base + 5)

    def free_scope(self, prefixes):
        victims = [n for n in self.sems if any(n.startswith(p) for p in prefixes)]
        if not victims:
            return
        self.nc.clear_and_free_semaphores([self.sems[n] for n in victims])
        for n in victims:
            del self.sems[n]
            del self.cnt[n]
        self.barrier_all("gbar")

    def sb(self, name, shape, dt=F32):
        return self.stack.enter_context(self.nc.sbuf_tensor(name, shape, dt))

    def ps(self, name, shape, dt=F32):
        return self.stack.enter_context(self.nc.psum_tensor(name, shape, dt))

    def dbg_out(self, name, shape, dt=F32):
        if name not in self.dbg_dram:
            self.dbg_dram[name] = self.nc.dram_tensor(
                "dbg_" + name, shape, dt, kind="ExternalOutput").ap()
        return self.dbg_dram[name]

    def dbg_dump(self, name, src_ap, shape, dt=F32):
        if name not in self.dbg:
            return
        d = self.dbg_out(name, shape, dt)
        self.inc(self.nc.gpsimd.dma_start(d, src_ap), "dbgo_" + name, 16)
        self.wait(self.nc.gpsimd, "dbgo_" + name)


def build_program(dbg=(), phases=_PHASES):
    nc = bass.Bass("TRN2", target_bir_lowering=False)
    h = {}
    h["x"] = nc.dram_tensor("x_seq", [SEQ, DM], F32, kind="ExternalInput").ap()
    h["wq"] = nc.dram_tensor("Wq", [DM, DM], F32, kind="ExternalInput").ap()
    h["wk"] = nc.dram_tensor("Wk", [DM, DM], F32, kind="ExternalInput").ap()
    h["wv"] = nc.dram_tensor("Wv", [DM, DM], F32, kind="ExternalInput").ap()
    h["w1"] = nc.dram_tensor("w1", [DFF, DM], F32, kind="ExternalInput").ap()
    h["w2"] = nc.dram_tensor("w2", [DM, DFF], F32, kind="ExternalInput").ap()
    h["bq"] = nc.dram_tensor("bq_rep", [128, DM], F32, kind="ExternalInput").ap()
    h["bkh"] = nc.dram_tensor("bk_hm", [128, NH], F32, kind="ExternalInput").ap()
    h["bv"] = nc.dram_tensor("bv_rep", [128, DM], F32, kind="ExternalInput").ap()
    h["g1"] = nc.dram_tensor("g1_rep", [128, DM], F32, kind="ExternalInput").ap()
    h["g2"] = nc.dram_tensor("g2_rep", [128, DFF], F32, kind="ExternalInput").ap()
    h["idb"] = nc.dram_tensor("ident_bf", [128, 128], BF16, kind="ExternalInput").ap()
    h["idf"] = nc.dram_tensor("ident_f32", [128, 128], F32, kind="ExternalInput").ap()
    h["triu"] = nc.dram_tensor("triu_bf", [128, 128], BF16, kind="ExternalInput").ap()
    h["onescb"] = nc.dram_tensor("ones_col_bf", [128, 1], BF16, kind="ExternalInput").ap()
    h["onesrb"] = nc.dram_tensor("ones_row_bf", [1, 128], BF16, kind="ExternalInput").ap()
    h["onescf"] = nc.dram_tensor("ones_col_f", [128, 1], F32, kind="ExternalInput").ap()
    h["onesrf"] = nc.dram_tensor("ones_row_f", [1, 128], F32, kind="ExternalInput").ap()
    h["iota64"] = nc.dram_tensor("iota64_rep", [128, TOPK], F32, kind="ExternalInput").ap()
    h["maskadd"] = nc.dram_tensor("maskadd", [2, 128, 256], F32, kind="ExternalInput").ap()
    h["boseos"] = nc.dram_tensor("boseos", [NH, SEQ], F32, kind="ExternalInput").ap()
    h["y"] = nc.dram_tensor("y", [HALF, DM], F32, kind="ExternalOutput").ap()
    h["qsp"] = nc.dram_tensor("qspill", [NSUB, 128, DM], BF16, kind="Internal").ap()
    h["ctxh"] = nc.dram_tensor("ctxhbm", [NSUB, 128, DM], BF16, kind="Internal").ap()
    h["w1t"] = nc.dram_tensor("w1t_cache", [16, 128, DFF], FP8, kind="Internal").ap()
    h["w2t"] = nc.dram_tensor("w2t_cache", [16, 128, DFF], FP8, kind="Internal").ap()

    with ExitStack() as stack:
        C = Ctx(nc, stack, dbg)
        P = {}
        P["stats_m"] = C.sb("stats_m", [128, NSUBF], F32)
        P["stats_isd"] = C.sb("stats_isd", [128, NSUBF], F32)
        P["stats_nmi"] = C.sb("stats_nmi", [128, NSUBF], F32)
        P["sel_tm"] = C.sb("sel_tm", [128, NSUB * NH], F32)
        P["selb_tm"] = C.sb("selb_tm", [128, NSUB * NH], BF16)
        P["rank_tm"] = C.sb("rank_tm", [128, NSUB * NH], F32)
        P["qgt"] = C.sb("qgt", [128, NH * TOPK], BF16)
        P["gfin"] = C.sb("gfin", [TOPK, NH * DH], BF16)
        P["idb"] = C.sb("c_idb", [128, 128], BF16)
        P["idf"] = C.sb("c_idf", [128, 128], F32)
        P["triu"] = C.sb("c_triu", [128, 128], BF16)
        P["onescb"] = C.sb("c_onescb", [128, 1], BF16)
        P["onesrb"] = C.sb("c_onesrb", [1, 128], BF16)
        P["onescf"] = C.sb("c_onescf", [128, 1], F32)
        P["onesrf"] = C.sb("c_onesrf", [1, 128], F32)
        P["iota64"] = C.sb("c_iota64", [128, TOPK], F32)
        P["eps5"] = C.sb("c_eps5", [128, 1], F32)
        P["eps6"] = C.sb("c_eps6", [128, 1], F32)
        P["swv"] = C.sb("swv", [128, 2], F32)
        P["maskadd"] = C.sb("maskadd_sb", [128, 2 * 256], F32)
        P["bkh"] = C.sb("bkh_sb", [128, NH], F32)

        for nm, hap in (("idb", "idb"), ("idf", "idf"), ("triu", "triu"),
                        ("onescb", "onescb"), ("onesrb", "onesrb"),
                        ("onescf", "onescf"), ("onesrf", "onesrf"),
                        ("iota64", "iota64")):
            C.inc(nc.sync.dma_start(P[nm][:], h[hap]), "cld", 16)
        C.inc(nc.sync.dma_start(P["maskadd"][:, 0:256], h["maskadd"][0]), "cld", 16)
        C.inc(nc.sync.dma_start(P["maskadd"][:, 256:512], h["maskadd"][1]), "cld", 16)
        C.inc(nc.sync.dma_start(P["bkh"][:], h["bkh"]), "cld", 16)
        for eng in (nc.vector, nc.scalar, nc.tensor, nc.gpsimd, nc.sync):
            C.wait(eng, "cld")
        nc.vector.memset(P["eps5"][:], 1.0e-5)
        C.inc(nc.vector.memset(P["eps6"][:], 1.0e-6), "init")
        C.wait(nc.vector, "init")
        C.wait(nc.scalar, "init")

        if "w" in phases:
            phase_w(C, h, P)
            C.barrier_all("pw")
            C.free_scope(("w0", "w1", "winit", "e_w"))
        if "a" in phases:
            phase_a(C, h, P)
            C.barrier_all("pa")
            C.free_scope(("a_", "a2", "a3", "e_a", "pa1", "pa2", "pa3", "dbgo"))
        if "b" in phases:
            phase_b(C, h, P, 0)
            C.barrier_all("pb0")
            C.free_scope(("b0_",))
            phase_b(C, h, P, 1)
            C.barrier_all("pb1")
            C.free_scope(("b1_",))
            C.dbg_dump("qsp", h["qsp"], [NSUB, 128, DM], BF16)
            C.dbg_dump("ctxh", h["ctxh"], [NSUB, 128, DM], BF16)
            C.dbg_dump("gfin", P["gfin"][:], [TOPK, NH * DH], BF16)
        if "c" in phases:
            phase_c(C, h, P)
        else:
            zz = C.sb("zz", [128, DM], F32)
            nc.vector.memset(zz[:], 0.0)
            C.inc(nc.vector.nop(), "zzv")
            C.wait(nc.sync, "zzv")
            for s in range(NSUB):
                C.inc(nc.sync.dma_start(h["y"][128 * s:128 * s + 128, :], zz[:]),
                      "zzo", 16)
            C.wait(nc.sync, "zzo")
    return nc


def phase_w(C, h, P):
    nc = C.nc
    with ExitStack() as st:
        sb = lambda n, s, dt=F32: st.enter_context(nc.sbuf_tensor(n, s, dt))
        blk = [sb(f"w_blk{i}", [128, DFF]) for i in range(2)]
        acc = sb("w_acc", [128, 1])
        red = sb("w_red", [128, 1])
        tsum = sb("w_tsum", [1, 1])
        asb = sb("w_a", [128, DFF], BF16)
        bsb = sb("w_b", [128, DFF], BF16)
        tern = sb("w_tern", [128, DFF], BF16)
        stage = sb("w_stage", [128, DFF], BF16)
        stage8 = sb("w_stage8", [128, DFF], FP8)
        halfp = sb("w_halfp", [128, 1])
        halfn = sb("w_halfn", [128, 1])
        sum_ps = st.enter_context(nc.psum_tensor("w_sum", [1, 1], F32))
        bc_ps = st.enter_context(nc.psum_tensor("w_bc", [128, 1], F32))
        tr_ps = [st.enter_context(nc.psum_tensor(f"w_tr{i}", [128, 1024], BF16))
                 for i in range(2)]

        nc.vector.memset(halfp[:], 0.5)
        C.inc(nc.vector.memset(halfn[:], -0.5), "winit")
        C.wait(nc.scalar, "winit")

        specs = [(h["w1"], DFF // 128, DM, 0), (h["w2"], DM // 128, DFF, 1)]
        parts = sb("w_parts", [128, 64])
        for src, nb, ncol, wi in specs:
            if wi == 1:
                C.wait(nc.sync, "w0v")
                C.wait(nc.vector, "w0t1")
            for b in range(nb):
                buf = blk[b % 2]
                if b >= 2:
                    C.wait(nc.sync, "w0v", C.cnt["w0v"] - 1)
                C.inc(nc.sync.dma_start(buf[:, 0:ncol],
                                        src[128 * b:128 * b + 128, :]), "w0d", 16)
                C.wait(nc.vector, "w0d")
                C.inc(nc.vector.tensor_reduce(parts[:, b:b + 1], buf[:, 0:ncol],
                                              AXL.X, AX.add,
                                              apply_absolute_value=True), "w0v")
            C.wait(nc.vector, "w0v")
            C.inc(nc.vector.tensor_reduce(acc[:], parts[:, 0:nb], AXL.X, AX.add),
                  "w0r")
            C.wait(nc.tensor, "w0r")
            C.inc(nc.tensor.matmul(sum_ps[:], P["onescf"][:], acc[:],
                                   start=True, stop=True), "w0t1")
            C.wait(nc.vector, "w0t1")
            C.inc(nc.vector.tensor_copy(tsum[:], sum_ps[:]), "w0ts")
            C.wait(nc.tensor, "w0ts")
            C.inc(nc.tensor.matmul(bc_ps[:], P["onesrf"][:], tsum[:],
                                   start=True, stop=True), "w0t2")
            C.wait(nc.vector, "w0t2")
            ntot = float(nb * 128 * ncol)
            C.inc(nc.vector.tensor_scalar(red[:], bc_ps[:], 1.0 / ntot, 1.0e-5,
                                          AX.mult, AX.max), "w0m")
            C.wait(nc.vector, "w0m")
            C.inc(nc.vector.reciprocal(P["swv"][:, wi:wi + 1], red[:]), "w0sw")
        C.wait(nc.scalar, "w0sw")

        C.wait(nc.sync, "w0v")
        for src, nb, ncol, wi in specs:
            if wi == 1:
                C.wait(nc.sync, "w1s")
            nkc = ncol // 128
            group = 4 if wi == 0 else 1
            cache = h["w1t"] if wi == 0 else h["w2t"]
            sw_col = P["swv"][:, wi:wi + 1]
            for b in range(nb):
                buf = blk[b % 2]
                if b >= 2:
                    C.wait(nc.sync, "w1s", C.cnt["w1s"] - 1)
                C.inc(nc.sync.dma_start(buf[:, 0:ncol],
                                        src[128 * b:128 * b + 128, :]), "w1d", 16)
                C.wait(nc.scalar, "w1d")
                if C.cnt.get("w1v", 0) > 0:
                    C.wait(nc.scalar, "w1v")
                nc.scalar.activation(asb[:, 0:ncol], buf[:, 0:ncol], AF.Sign,
                                     bias=halfn[:], scale=sw_col)
                C.inc(nc.scalar.activation(bsb[:, 0:ncol], buf[:, 0:ncol], AF.Sign,
                                           bias=halfp[:], scale=sw_col), "w1s")
                C.wait(nc.vector, "w1s")
                if C.cnt.get("w1tp", 0) > 0:
                    C.wait(nc.vector, "w1tp")
                C.inc(nc.vector.tensor_add(tern[:, 0:ncol], asb[:, 0:ncol],
                                           bsb[:, 0:ncol]), "w1v")
                C.wait(nc.tensor, "w1v")
                nhalf = nkc // 8
                for half in range(nhalf):
                    pb = tr_ps[half % 2]
                    hglob = C.cnt.get("w1tp", 0)
                    if hglob >= 2:
                        C.wait(nc.tensor, "w1ev", 8 * (hglob - 1))
                    for t in range(8):
                        k = 8 * half + t
                        ins = nc.tensor.transpose(pb[:, 128 * t:128 * t + 128],
                                                  tern[:, 128 * k:128 * k + 128],
                                                  P["idb"][:])
                    C.inc(ins, "w1tp")
                    C.wait(nc.vector, "w1tp")
                    if C.cnt.get("w1c8", 0) > 0:
                        C.wait(nc.vector, "w1c8")
                    for t in range(8):
                        k = 8 * half + t
                        dstc = (512 * k + 128 * (b % group)) if wi == 0 else 128 * k
                        C.inc(nc.vector.tensor_copy(stage[:, dstc:dstc + 128],
                                                    pb[:, 128 * t:128 * t + 128]),
                              "w1ev")
                if b % group == group - 1:
                    C.wait(nc.vector, "w1ev")
                    if C.cnt.get("w1o", 0) > 0:
                        C.wait(nc.vector, "w1o")
                    C.inc(nc.vector.tensor_copy(stage8[:], stage[:]), "w1c8")
                    C.wait(nc.scalar, "w1c8")
                    C.inc(nc.scalar.dma_start(cache[b // group], stage8[:]),
                          "w1o", 16)
        C.wait(nc.sync, "w1o")
        C.barrier_all("pwd")
        C.dbg_dump("swv", P["swv"][:], [128, 2])

        if "w1t" in C.dbg or "w2t" in C.dbg:
            dsb = blk[0]
            dsb8 = stage8
            for nm, cache in (("w1t", h["w1t"]), ("w2t", h["w2t"])):
                if nm not in C.dbg:
                    continue
                d = C.dbg_out(nm, [16, 128, DFF], F32)
                for i in range(16):
                    C.inc(nc.sync.dma_start(dsb8[:], cache[i]), "wdbd" + nm, 16)
                    C.wait(nc.vector, "wdbd" + nm)
                    C.inc(nc.vector.tensor_copy(dsb[:], dsb8[:]), "wdbv" + nm)
                    C.wait(nc.sync, "wdbv" + nm)
                    C.inc(nc.sync.dma_start(d[i], dsb[:]), "wdbo" + nm, 16)
                    C.wait(nc.sync, "wdbo" + nm)


def phase_a(C, h, P):
    nc = C.nc
    with ExitStack() as st:
        sb = lambda n, s, dt=F32: st.enter_context(nc.sbuf_tensor(n, s, dt))
        P["norms"] = sb("norms_h", [NH, SEQ], F32)
        P["sel_h"] = sb("sel_h", [NH, SEQ], F32)
        P["boseos"] = sb("boseos_sb", [NH, SEQ], F32)
        C.inc(nc.sync.dma_start(P["boseos"][:], h["boseos"]), "a_bos", 16)
        C.wait(nc.vector, "a_bos")
        xs = [sb(f"a_x{i}", [128, DM]) for i in range(2)]
        sq = sb("a_sq", [128, DM])
        tq = sb("a_tq", [128, 512])
        usq = sb("a_usq", [128, 512])
        xnb = sb("a_xnb", [128, DM], BF16)
        xnt = sb("a_xnt", [128, DM], BF16)
        qsub = sb("a_qsub", [128, DM], BF16)
        nrm = sb("a_nrm", [128, NH])
        wq_sb = [sb(f"a_wq{k}", [128, DM], BF16) for k in range(16)]
        bq_sb = sb("a_bq", [128, DM])
        lnvar = sb("a_lnvar", [128, NSUBF])
        t1 = sb("a_t1", [128, 1])
        t2 = sb("a_t2", [128, 1])
        t3 = sb("a_t3", [128, 1])
        t4 = sb("a_t4", [128, 1])
        t5 = sb("a_t5", [128, 1])
        t6 = sb("a_t6", [128, 1])
        st1 = ExitStack()
        psq = [st1.enter_context(nc.psum_tensor(f"a_q{i}", [128, 512], F32))
               for i in range(4)]
        ptr = st1.enter_context(nc.psum_tensor("a_tr", [128, 1024], BF16))
        pnrm = st1.enter_context(nc.psum_tensor("a_pnrm", [16, 128], F32))

        for k in range(16):
            C.inc(nc.gpsimd.dma_start(wq_sb[k][:], h["wq"][128 * k:128 * k + 128, :]),
                  "a_wqd", 16)
        C.inc(nc.sync.dma_start(bq_sb[:], h["bq"]), "a_bqd", 16)
        C.wait(nc.vector, "a_bqd")

        for sub in range(NSUBF):
            buf = xs[sub % 2]
            if sub >= 2:
                C.wait(nc.sync, "a_xs", C.cnt["a_xs"] - 1)
                C.wait(nc.sync, "a_xv", C.cnt["a_xv"] - 1)
            C.inc(nc.sync.dma_start(buf[:], h["x"][128 * sub:128 * sub + 128, :]),
                  "a_xd", 16)
            C.wait(nc.vector, "a_xd")
            C.inc(nc.vector.reduce_sum(t1[:], buf[:], AXL.X), "a_xv")
            C.wait(nc.scalar, "a_xd")
            if sub > 0:
                C.wait(nc.scalar, "a_t2r")
                C.wait(nc.scalar, "a_tp")
            C.inc(nc.scalar.square(sq[:], buf[:]), "a_sqs")
            C.wait(nc.vector, "a_xv")
            C.inc(nc.vector.tensor_scalar(P["stats_m"][:, sub:sub + 1], t1[:],
                                          1.0 / DM, None, AX.mult), "a_m")
            C.wait(nc.vector, "a_sqs")
            C.inc(nc.vector.reduce_sum(t2[:], sq[:], AXL.X), "a_t2r")
            C.wait(nc.vector, "a_m")
            m_col = P["stats_m"][:, sub:sub + 1]
            C.inc(nc.vector.tensor_mul(t3[:], m_col, m_col), "a_msq")
            C.wait(nc.vector, "a_t2r")
            C.inc(nc.vector.tensor_scalar(t4[:], t2[:], 1.0 / DM, None, AX.mult),
                  "a_s2m")
            C.wait(nc.vector, "a_msq")
            C.wait(nc.vector, "a_s2m")
            C.inc(nc.vector.tensor_sub(t5[:], t4[:], t3[:]), "a_var")
            C.wait(nc.vector, "a_var")
            nc.vector.tensor_copy(lnvar[:, sub:sub + 1], t5[:])
            C.wait(nc.scalar, "a_var")
            C.inc(nc.scalar.activation(t3[:], t5[:], AF.Sqrt, bias=P["eps5"][:]),
                  "a_sd")
            C.wait(nc.vector, "a_sd")
            C.inc(nc.vector.reciprocal(P["stats_isd"][:, sub:sub + 1], t3[:]),
                  "a_isd")
            C.wait(nc.vector, "a_isd")
            isd_col = P["stats_isd"][:, sub:sub + 1]
            C.inc(nc.vector.tensor_mul(t6[:], m_col, isd_col), "a_mi")
            C.wait(nc.vector, "a_mi")
            C.inc(nc.vector.tensor_scalar(P["stats_nmi"][:, sub:sub + 1], t6[:],
                                          -1.0, None, AX.mult), "a_nmi")
            C.wait(nc.scalar, "a_nmi")
            C.inc(nc.scalar.activation(xnb[:], buf[:], AF.Identity,
                                       bias=P["stats_nmi"][:, sub:sub + 1],
                                       scale=isd_col), "a_xs")
            C.wait(nc.tensor, "a_xs")
            if sub > 0:
                C.wait(nc.tensor, "a_te")
            for hh in range(2):
                if hh == 1:
                    C.wait(nc.tensor, "a_te", C.cnt["a_te"] - 0)
                for t in range(8):
                    k = 8 * hh + t
                    ins = nc.tensor.transpose(ptr[:, 128 * t:128 * t + 128],
                                              xnb[:, 128 * k:128 * k + 128],
                                              P["idb"][:])
                C.inc(ins, "a_tp")
                C.wait(nc.vector, "a_tp")
                for t in range(8):
                    k = 8 * hh + t
                    C.inc(nc.vector.tensor_copy(xnt[:, 128 * k:128 * k + 128],
                                                ptr[:, 128 * t:128 * t + 128]),
                          "a_te")
            C.wait(nc.tensor, "a_te")
            if sub == 0:
                C.wait(nc.tensor, "a_wqd")
            if sub == 0 and "e3" in C.dbg:
                C.barrier_all("e3b")
            if sub > 0:
                C.wait(nc.tensor, "a_tqe")
            for k in range(16):
                for n in range(4):
                    ins = nc.tensor.matmul(psq[n][:], xnt[:, 128 * k:128 * k + 128],
                                           wq_sb[k][:, 512 * n:512 * n + 512],
                                           start=(k == 0), stop=(k == 15))
            C.inc(ins, "a_mm")
            if sub == 0 and "e2" in C.dbg:
                C.barrier_all("e2b")
            C.wait(nc.vector, "a_mm")
            if sub >= 1 and sub <= NSUB:
                C.wait(nc.vector, "a_qo", 16 * min(sub, NSUB))
            for n in range(4):
                if n > 0:
                    C.wait(nc.vector, "a_us", C.cnt["a_us"])
                C.inc(nc.vector.tensor_add(
                    tq[:], psq[n][:], bq_sb[:, 512 * n:512 * n + 512]), "a_tqe")
                C.inc(nc.vector.tensor_copy(qsub[:, 512 * n:512 * n + 512], tq[:]),
                      "a_q2s")
                C.wait(nc.scalar, "a_q2s")
                if C.cnt.get("a_nr", 0) > 0:
                    C.wait(nc.scalar, "a_nr")
                C.inc(nc.scalar.square(usq[:], tq[:]), "a_us")
                C.wait(nc.vector, "a_us")
                if n == 0:
                    C.wait(nc.vector, "a_ntp", C.cnt.get("a_ntp", 0))
                u3 = usq[:].rearrange("p (g d) -> p g d", g=4)
                C.inc(nc.vector.reduce_sum(nrm[:, 4 * n:4 * n + 4], u3, AXL.X),
                      "a_nr")
            if sub < NSUB:
                C.wait(nc.scalar, "a_q2s")
                C.inc(nc.scalar.dma_start(h["qsp"][sub], qsub[:]), "a_qo", 16)
            if sub == 0 and "qsub0" in C.dbg:
                C.wait(nc.gpsimd, "a_q2s", C.cnt["a_q2s"])
                C.dbg_dump("qsub0", qsub[:], [128, DM], BF16)
                C.dbg_dump("xnt0", xnt[:], [128, DM], BF16)
            C.wait(nc.tensor, "a_nr")
            if sub > 0:
                C.wait(nc.tensor, "a_nev", C.cnt["a_nev"])
            C.inc(nc.tensor.transpose(pnrm[:], nrm[:], P["idf"][:]), "a_ntp")
            C.wait(nc.vector, "a_ntp")
            C.inc(nc.vector.tensor_copy(P["norms"][0:16, 128 * sub:128 * sub + 128],
                                        pnrm[0:16, :]), "a_nev")
        C.wait(nc.sync, "a_qo", C.cnt.get("a_qo", 0))
        st1.close()
        C.barrier_all("pa1")
        C.dbg_dump("qspA", h["qsp"], [NSUB, 128, DM], BF16)
        C.dbg_dump("stats_m", P["stats_m"][:], [128, NSUBF])
        C.dbg_dump("lnvar", lnvar[:], [128, NSUBF])
        C.dbg_dump("stats_isd", P["stats_isd"][:], [128, NSUBF])
        C.dbg_dump("norms", P["norms"][:], [NH, SEQ])

        wrk = sb("a_wrk", [NH, SEQ])
        top8 = sb("a_top8", [NH, 8])
        topv = sb("a_topv", [NH, TOPK])
        nc.vector.tensor_copy(wrk[:], P["norms"][:])
        for r in range(8):
            C.inc(nc.vector.max(top8[:], wrk[:]), "a2m")
            C.wait(nc.vector, "a2m")
            C.inc(nc.vector.tensor_copy(topv[:, 8 * r:8 * r + 8], top8[:]), "a2tv")
            if r < 7:
                C.inc(nc.vector.match_replace(wrk[:], top8[:], wrk[:], -1.0e30),
                      "a2m")
                C.wait(nc.vector, "a2m")
        C.wait(nc.vector, "a2tv")
        C.inc(nc.vector.tensor_single_scalar(wrk[:], P["norms"][:],
                                              topv[:, 61:62], AX.is_ge), "a2sr")
        C.wait(nc.vector, "a2sr")
        C.inc(nc.vector.tensor_max(P["sel_h"][:], wrk[:], P["boseos"][:]), "a2s")
        C.wait(nc.vector, "a2s")
        C.wait(nc.tensor, "a2s")

        st2 = ExitStack()
        selp = st2.enter_context(nc.psum_tensor("a_selp", [128, NH], F32))
        rankp = st2.enter_context(nc.psum_tensor("a_rankp", [128, NH], F32))
        totp = st2.enter_context(nc.psum_tensor("a_totp", [1, NH], F32))
        cumA = sb("a_cumA", [1, NH])
        cumB = sb("a_cumB", [1, NH])
        cumb = sb("a_cumb", [1, NH], BF16)
        C.inc(nc.vector.memset(cumA[:], 0.0), "a2ci")
        C.wait(nc.vector, "a2ci")
        C.inc(nc.vector.tensor_copy(cumb[:], cumA[:]), "a2cu")
        C.wait(nc.tensor, "a2cu")
        for s in range(NSUB):
            if s > 0:
                C.wait(nc.tensor, "a2se", C.cnt["a2se"])
            C.inc(nc.tensor.transpose(selp[:], P["sel_h"][0:16, 128 * s:128 * s + 128],
                                      P["idf"][0:16, 0:16]), "a2tp")
            C.wait(nc.vector, "a2tp")
            nc.vector.tensor_copy(P["sel_tm"][:, 16 * s:16 * s + 16], selp[:])
            C.inc(nc.vector.tensor_copy(P["selb_tm"][:, 16 * s:16 * s + 16],
                                        selp[:]), "a2se")
            C.wait(nc.tensor, "a2se")
            if s > 0:
                C.wait(nc.tensor, "a2rv", C.cnt["a2rv"])
            nc.tensor.matmul(rankp[:], P["triu"][:],
                             P["selb_tm"][:, 16 * s:16 * s + 16],
                             start=True, stop=False)
            C.inc(nc.tensor.matmul(rankp[:], P["onesrb"][:], cumb[:],
                                   start=False, stop=True), "a2rk")
            C.inc(nc.tensor.matmul(totp[:], P["onescb"][:],
                                   P["selb_tm"][:, 16 * s:16 * s + 16],
                                   start=True, stop=True), "a2tt")
            C.wait(nc.vector, "a2rk")
            C.wait(nc.vector, "a2tt")
            C.inc(nc.vector.tensor_copy(P["rank_tm"][:, 16 * s:16 * s + 16],
                                        rankp[:]), "a2rv")
            src_c, dst_c = (cumA, cumB) if s % 2 == 0 else (cumB, cumA)
            C.inc(nc.vector.tensor_add(dst_c[:], src_c[:], totp[0:1, :]), "a2ca")
            C.wait(nc.vector, "a2ca")
            C.inc(nc.vector.tensor_copy(cumb[:], dst_c[:]), "a2cu")
            C.wait(nc.tensor, "a2cu")
        st2.close()
        C.barrier_all("pa2")
        C.dbg_dump("sel_tm", P["sel_tm"][:], [128, NSUB * NH])
        C.dbg_dump("rank_tm", P["rank_tm"][:], [128, NSUB * NH])

        C.inc(nc.vector.memset(P["qgt"][:], 0.0), "a3qz")
        C.wait(nc.vector, "a3qz")
        mtb = sb("a_mtb", [128, TOPK], BF16)
        qgs = sb("a_qgs", [TOPK, DH], BF16)
        st3 = ExitStack()
        qg_ps = [st3.enter_context(nc.psum_tensor(f"a_qg{i}", [TOPK, DH], F32))
                 for i in range(4)]
        qgt_ps = st3.enter_context(nc.psum_tensor("a_qgt", [128, TOPK], BF16))
        for sweep in range(4):
            for s in range(NSUB):
                C.wait(nc.sync, "a3fr", C.cnt.get("a3fr", 0))
                C.inc(nc.sync.dma_start(qsub[:], h["qsp"][s]), "a3qd", 16)
                C.wait(nc.vector, "a3qd")
                for hh in range(4):
                    hd = 4 * sweep + hh
                    C.inc(nc.vector.tensor_scalar(
                        mtb[:], P["iota64"][:],
                        P["rank_tm"][:, 16 * s + hd:16 * s + hd + 1],
                        P["sel_tm"][:, 16 * s + hd:16 * s + hd + 1],
                        AX.is_equal, AX.mult), "a3mt")
                    C.wait(nc.tensor, "a3mt")
                    ins = nc.tensor.matmul(qg_ps[hh][:], mtb[:],
                                           qsub[:, 128 * hd:128 * hd + 128],
                                           start=(s == 0), stop=(s == NSUB - 1))
                    C.inc(ins, "a3mb")
                    C.wait(nc.vector, "a3mb")
                C.inc(nc.tensor.nop(), "a3fr")
            C.inc(nc.tensor.nop(), "a3mm")
            C.wait(nc.vector, "a3mm")
            for hh in range(4):
                hd = 4 * sweep + hh
                C.inc(nc.vector.tensor_copy(qgs[:], qg_ps[hh][:]), "a3ev")
                C.wait(nc.tensor, "a3ev")
                if C.cnt.get("a3qg", 0) > 0:
                    C.wait(nc.tensor, "a3qg")
                C.inc(nc.tensor.transpose(qgt_ps[0:128, 0:TOPK], qgs[:],
                                          P["idb"][0:TOPK, 0:TOPK]), "a3tp")
                C.wait(nc.vector, "a3tp")
                C.inc(nc.vector.tensor_scalar(P["qgt"][:, 64 * hd:64 * hd + 64],
                                              qgt_ps[0:128, 0:64],
                                              0.08838834764831845, None, AX.mult),
                      "a3qg")
        st3.close()
        C.barrier_all("pa3")
        C.dbg_dump("qgt", P["qgt"][:], [128, NH * TOPK])


def phase_b(C, h, P, hg):
    nc = C.nc
    s_ = lambda n: f"b{hg}_" + n
    with ExitStack() as st:
        sb = lambda n, sh, dt=F32: st.enter_context(nc.sbuf_tensor(s_(n), sh, dt))
        ps = lambda n, sh, dt=F32: st.enter_context(nc.psum_tensor(s_(n), sh, dt))
        wk_sb = [sb(f"wk{k}", [128, 1024], BF16) for k in range(16)]
        wv_sb = [sb(f"wv{k}", [128, 1024], BF16) for k in range(16)]
        bv_sb = sb("bv", [128, 1024])
        qsb = sb("qsb", [128, 1024], BF16)
        xs = [sb(f"x{i}", [128, DM]) for i in range(2)]
        xnb = sb("xnb", [128, DM], BF16)
        xnt = sb("xnt", [128, 16 * 576], BF16)
        stash = sb("stash", [128, 16 * 64], BF16)
        kt = [sb(f"kt{i}", [128, 8 * 576], BF16) for i in range(2)]
        kt0 = sb("ktz", [128, 8 * 576], BF16)
        vt = [sb(f"vt{i}", [128, 1024], BF16) for i in range(9)]
        qt_t = sb("qt_t", [128, 128], BF16)
        probs = sb("probs", [128, 256], BF16)
        probsT = sb("probsT", [128, 256], BF16)
        pgb = sb("pgb", [TOPK, 512], BF16)
        pgbT = sb("pgbT", [128, 256], BF16)
        ctx_st = [sb(f"cst{i}", [128, 1024], BF16) for i in range(2)]
        maskmul = sb("mskm", [128, 2 * 256], BF16)
        probs2 = sb("probs2", [128, 256], BF16)
        ncolt = sb("ncolt", [128, 12])
        nmx = ncolt[:, 0:1]
        ssum = ncolt[:, 1:2]
        rsum = ncolt[:, 2:3]
        gmc = ncolt[0:TOPK, 3:4]
        gda = ncolt[0:TOPK, 4:5]
        gal = ncolt[0:TOPK, 5:6]
        gnn = ncolt[0:TOPK, 6:7]
        gcs = ncolt[0:TOPK, 7:8]
        grr = ncolt[0:TOPK, 8:9]
        rm = [sb(f"rm{i}", [TOPK, 8]) for i in range(2)]
        rs = [sb(f"rs{i}", [TOPK, 8]) for i in range(2)]
        gc = [sb(f"gc{i}", [TOPK, 1024]) for i in range(2)]
        ptrb = [ps(f"ptr{i}", [128, 1024], BF16) for i in range(2)]
        pk = ps("pk", [128, 512])
        pv0 = ps("pv0", [128, 512])
        pv1 = ps("pv1", [128, 512])
        psc = ps("psc", [128, 512])
        pctx = ps("pctx", [128, 128])
        ppv = ps("ppv", [TOPK, DH])

        def tp_into(i, pairs, ident):
            C.wait(nc.tensor, s_(f"pe{i}"), C.cnt.get(s_(f"pe{i}"), 0))
            ins = None
            for dst, srcp in pairs:
                ins = nc.tensor.transpose(dst, srcp, ident)
            return ins

        C.inc(nc.vector.tensor_scalar(maskmul[:], P["maskadd"][:], -0.5e9, None,
                                      AX.is_ge), s_("mm"))
        nc.vector.memset(rm[0][:], -1.0e30)
        nc.vector.memset(rs[0][:], 0.0)
        C.inc(nc.vector.memset(gc[0][:], 0.0), s_("fi"))
        C.wait(nc.vector, s_("fi"))
        C.wait(nc.scalar, s_("fi"))

        for k in range(16):
            C.inc(nc.gpsimd.dma_start(
                wk_sb[k][:], h["wk"][128 * k:128 * k + 128,
                                     1024 * hg:1024 * hg + 1024]), s_("wd"), 16)
            C.inc(nc.gpsimd.dma_start(
                wv_sb[k][:], h["wv"][128 * k:128 * k + 128,
                                     1024 * hg:1024 * hg + 1024]), s_("wd"), 16)
        C.inc(nc.sync.dma_start(bv_sb[:], h["bv"][0:128, 1024 * hg:1024 * hg + 1024]),
              s_("bvd"), 16)
        C.wait(nc.vector, s_("bvd"))

        fcnt = [0]
        vslot = lambda m: m if m <= 3 else 4 + ((m - 4) % 5)

        def xn_rebuild(c):
            if c > 0:
                C.wait(nc.vector, s_("kmm"), C.cnt.get(s_("kmm"), 0))
                C.wait(nc.vector, s_("vmm"), C.cnt.get(s_("vmm"), 0))
            if c == 0:
                for k in range(16):
                    C.inc(nc.vector.memset(xnt[:, 576 * k:576 * k + 64], 0.0),
                          s_("xe"))
            else:
                for k in range(16):
                    C.inc(nc.vector.tensor_copy(xnt[:, 576 * k:576 * k + 64],
                                                stash[:, 64 * k:64 * k + 64]),
                          s_("xe"))
            for ls in range(4):
                sub = 4 * c + ls
                buf = xs[ls % 2]
                if 4 * c + ls >= 2:
                    C.wait(nc.sync, s_("xs"), C.cnt[s_("xs")] - 1)
                C.inc(nc.sync.dma_start(buf[:], h["x"][128 * sub:128 * sub + 128, :]),
                      s_("xd"), 16)
                C.wait(nc.scalar, s_("xd"))
                if C.cnt.get(s_("xtp"), 0) > 0:
                    C.wait(nc.scalar, s_("xtp"))
                C.inc(nc.scalar.activation(xnb[:], buf[:], AF.Identity,
                                           bias=P["stats_nmi"][:, sub:sub + 1],
                                           scale=P["stats_isd"][:, sub:sub + 1]),
                      s_("xs"))
                C.wait(nc.tensor, s_("xs"))
                for hh2 in range(2):
                    prs = [(ptrb[hh2][:, 128 * t:128 * t + 128],
                            xnb[:, 128 * (8 * hh2 + t):128 * (8 * hh2 + t) + 128])
                           for t in range(8)]
                    ins = tp_into(hh2, prs, P["idb"][:])
                    C.inc(ins, s_("xtp"))
                    C.wait(nc.vector, s_("xtp"))
                    for t in range(8):
                        k = 8 * hh2 + t
                        cc = nc.vector.tensor_copy(
                            xnt[:, 576 * k + 64 + 128 * ls:
                                576 * k + 64 + 128 * ls + 128],
                            ptrb[hh2][:, 128 * t:128 * t + 128])
                        C.inc(cc, s_(f"pe{hh2}"))
            C.wait(nc.vector, s_("xe"))
            C.wait(nc.vector, s_("pe0"))
            C.wait(nc.vector, s_("pe1"))
            for k in range(16):
                C.inc(nc.vector.tensor_copy(stash[:, 64 * k:64 * k + 64],
                                            xnt[:, 576 * k + 512:576 * k + 576]),
                      s_("sst"))
            C.inc(nc.vector.nop(), s_("xrd"))
            C.wait(nc.tensor, s_("xrd"))

        def k_proj(c):
            if c == 0:
                C.wait(nc.tensor, s_("wd"))
            slot = kt[c % 2]
            if c >= 2:
                C.wait(nc.scalar, s_("lsc"), C.cnt.get(s_("lsc"), 0))
                C.wait(nc.scalar, s_("gsc"), C.cnt.get(s_("gsc"), 0))
            for hh in range(8):
                hd = 8 * hg + hh
                if C.cnt.get(s_("kev"), 0) > 0:
                    C.wait(nc.tensor, s_("kev"), C.cnt[s_("kev")])
                C.wait(nc.tensor, s_("lex"), C.cnt.get(s_("lex"), 0))
                C.wait(nc.tensor, s_("gex"), C.cnt.get(s_("gex"), 0))
                for k in range(16):
                    nc.tensor.matmul(pk[:], wk_sb[k][:, 128 * hh:128 * hh + 128],
                                     xnt[:, 576 * k + 64:576 * k + 576],
                                     start=(k == 0), stop=(k == 15))
                    ins = nc.tensor.matmul(psc[0:128, 0:64],
                                           wk_sb[k][:, 128 * hh:128 * hh + 128],
                                           xnt[:, 576 * k:576 * k + 64],
                                           start=(k == 0), stop=(k == 15))
                C.inc(ins, s_("kmm"))
                C.wait(nc.scalar, s_("kmm"))
                nc.scalar.activation(slot[:, 576 * hh + 64:576 * hh + 576],
                                     pk[:], AF.Identity,
                                     bias=P["bkh"][:, hd:hd + 1])
                a2 = nc.scalar.activation(slot[:, 576 * hh:576 * hh + 64],
                                          psc[0:128, 0:64], AF.Identity,
                                          bias=P["bkh"][:, hd:hd + 1])
                C.inc(a2, s_("kev"))
                if c == 0:
                    C.inc(nc.scalar.copy(kt0[:, 576 * hh:576 * hh + 576],
                                         slot[:, 576 * hh:576 * hh + 576]),
                          s_("kev"))

        def v_proj(c):
            if c >= 2:
                C.wait(nc.vector, s_("pve"), C.cnt.get(s_("pve"), 0))
                C.wait(nc.vector, s_("gpv"), C.cnt.get(s_("gpv"), 0))
            for i in range(4):
                m = 4 * c + i
                slot = vt[vslot(m)]
                if C.cnt.get(s_("vev"), 0) > 0:
                    C.wait(nc.tensor, s_("vev"), C.cnt[s_("vev")])
                for k in range(16):
                    nc.tensor.matmul(pv0[:],
                                     xnt[:, 576 * k + 128 * i:576 * k + 128 * i + 128],
                                     wv_sb[k][:, 0:512], start=(k == 0), stop=(k == 15))
                    ins = nc.tensor.matmul(pv1[:],
                                           xnt[:, 576 * k + 128 * i:576 * k + 128 * i + 128],
                                           wv_sb[k][:, 512:1024],
                                           start=(k == 0), stop=(k == 15))
                C.inc(ins, s_("vmm"))
                C.wait(nc.vector, s_("vmm"))
                nc.vector.tensor_add(slot[:, 0:512], pv0[:], bv_sb[:, 0:512])
                C.inc(nc.vector.tensor_add(slot[:, 512:1024], pv1[:],
                                           bv_sb[:, 512:1024]), s_("vev"))
            if c == 7:
                C.wait(nc.tensor, s_("vev"), C.cnt[s_("vev")])
                for k in range(16):
                    nc.tensor.matmul(pv0[0:64, :], xnt[:, 576 * k + 512:576 * k + 576],
                                     wv_sb[k][:, 0:512], start=(k == 0), stop=(k == 15))
                    ins = nc.tensor.matmul(pv1[0:64, :],
                                           xnt[:, 576 * k + 512:576 * k + 576],
                                           wv_sb[k][:, 512:1024],
                                           start=(k == 0), stop=(k == 15))
                C.inc(ins, s_("vmm"))
                C.wait(nc.vector, s_("vmm"))
                nc.vector.tensor_add(vt[0][0:64, 0:512], pv0[0:64, :],
                                     bv_sb[0:64, 0:512])
                C.inc(nc.vector.tensor_add(vt[0][0:64, 512:1024], pv1[0:64, :],
                                           bv_sb[0:64, 512:1024]), s_("vev"))
                for hh in range(8):
                    hd = 8 * hg + hh
                    C.wait(nc.tensor, s_("kev"), C.cnt[s_("kev")])
                    for k in range(16):
                        ins = nc.tensor.matmul(
                            psc[0:128, 0:64],
                            wk_sb[k][:, 128 * hh:128 * hh + 128],
                            xnt[:, 576 * k + 512:576 * k + 576],
                            start=(k == 0), stop=(k == 15))
                    C.inc(ins, s_("kmm"))
                    C.wait(nc.scalar, s_("kmm"))
                    C.inc(nc.scalar.activation(kt0[:, 576 * hh:576 * hh + 64],
                                               psc[0:128, 0:64], AF.Identity,
                                               bias=P["bkh"][:, hd:hd + 1]),
                          s_("kev"))

        def kt_cols(tok0, tok1, hh, use_kt0):
            pieces = []
            t = tok0
            while t < tok1:
                cprime = (t + 64) // 512
                ten = kt0 if (cprime == 0 and use_kt0) else kt[cprime % 2]
                base = 512 * cprime - 64
                hi = min(tok1, base + 576)
                pieces.append((ten, 576 * hh + (t - base), 576 * hh + (hi - base),
                               t - tok0))
                t = hi
            return pieces

        def local_qb(qb, use_kt0=False):
            if C.cnt.get(s_("qtp"), 0) > 0:
                C.wait(nc.sync, s_("qtp"))
            C.inc(nc.sync.dma_start(qsb[:],
                                    h["qsp"][qb, 0:128, 1024 * hg:1024 * hg + 1024]),
                  s_("qd"), 16)
            C.wait(nc.tensor, s_("qd"))
            for hh in range(8):
                hd = 8 * hg + hh
                prs = [(ptrb[0][:, 0:128],
                        qsb[:, 128 * hh:128 * hh + 128])]
                ins = tp_into(0, prs, P["idb"][:])
                C.inc(ins, s_("qtp"))
                C.wait(nc.vector, s_("qtp"))
                cc = nc.vector.tensor_scalar(qt_t[:], ptrb[0][:, 0:128],
                                             0.08838834764831845, None, AX.mult)
                C.inc(cc, s_("pe0"))
                C.wait(nc.tensor, s_("pe0"))
                C.wait(nc.tensor, s_("kev"), C.cnt.get(s_("kev"), 0))
                C.wait(nc.tensor, s_("lex"), C.cnt.get(s_("lex"), 0))
                C.wait(nc.tensor, s_("gex"), C.cnt.get(s_("gex"), 0))
                qs = 128 * qb
                for ten, c0, c1, d0 in kt_cols(qs - 64, qs + 192, hh, use_kt0):
                    ins = nc.tensor.matmul(psc[0:128, d0:d0 + (c1 - c0)], qt_t[:],
                                           ten[:, c0:c1], start=True, stop=True)
                C.inc(ins, s_("lsc"))
                C.wait(nc.vector, s_("lsc"))
                C.inc(nc.vector.reduce_max(nmx, psc[0:128, 0:256], AXL.X,
                                           negate=True), s_("lmx"))
                C.wait(nc.scalar, s_("lmx"))
                C.wait(nc.scalar, s_("ptp"), C.cnt.get(s_("ptp"), 0))
                masked = qb in (0, 15)
                e = nc.scalar.activation(probs[:], psc[0:128, 0:256], AF.Exp,
                                         bias=nmx)
                C.inc(e, s_("lex"))
                C.wait(nc.vector, s_("lex"))
                psrc = probs
                if masked:
                    moff = 0 if qb == 0 else 256
                    C.inc(nc.vector.tensor_mul(probs2[:], probs[:],
                                               maskmul[:, moff:moff + 256]),
                          s_("lpm"))
                    C.wait(nc.vector, s_("lpm"))
                    psrc = probs2
                C.inc(nc.vector.reduce_sum(ssum, psrc[:], AXL.X), s_("lsm"))
                C.wait(nc.vector, s_("lsm"))
                C.inc(nc.vector.reciprocal(rsum, ssum), s_("lrc"))
                C.wait(nc.tensor, s_("lex"))
                if masked:
                    C.wait(nc.tensor, s_("lpm"))
                prs = [(ptrb[1][:, 128 * i2:128 * i2 + 128],
                        psrc[:, 128 * i2:128 * i2 + 128]) for i2 in range(2)]
                ins = tp_into(1, prs, P["idb"][:])
                C.inc(ins, s_("ptp"))
                C.wait(nc.vector, s_("ptp"))
                cc = nc.vector.tensor_copy(probsT[:], ptrb[1][:, 0:256])
                C.inc(cc, s_("pe1"))
                C.wait(nc.tensor, s_("pe1"))
                C.wait(nc.tensor, s_("vev"), C.cnt.get(s_("vev"), 0))
                C.wait(nc.tensor, s_("cev"), C.cnt.get(s_("cev"), 0))
                nc.tensor.matmul(pctx[:], probsT[:, 0:128],
                                 vt[vslot(qb)][:, 128 * hh:128 * hh + 128],
                                 start=True, stop=False)
                C.inc(nc.tensor.matmul(pctx[:], probsT[:, 128:256],
                                       vt[vslot(qb + 1)][:, 128 * hh:128 * hh + 128],
                                       start=False, stop=True), s_("pve"))
                C.wait(nc.vector, s_("pve"))
                C.wait(nc.vector, s_("lrc"))
                cst = ctx_st[qb % 2]
                if hh == 0 and C.cnt.get(s_("cso"), 0) >= 32:
                    C.wait(nc.vector, s_("cso"), C.cnt[s_("cso")] - 16)
                C.inc(nc.vector.tensor_scalar(cst[:, 128 * hh:128 * hh + 128],
                                              pctx[:], rsum, None, AX.mult),
                      s_("cev"))
            C.wait(nc.scalar, s_("cev"))
            C.inc(nc.scalar.dma_start(
                h["ctxh"][qb, 0:128, 1024 * hg:1024 * hg + 1024],
                ctx_st[qb % 2][:]), s_("cso"), 16)

        def flash_step(fc, use_kt0=False):
            p = fcnt[0] % 2
            for hh in range(8):
                hd = 8 * hg + hh
                ten = kt0 if use_kt0 else kt[fc % 2]
                C.wait(nc.tensor, s_("gex"), C.cnt.get(s_("gex"), 0))
                C.wait(nc.tensor, s_("lex"), C.cnt.get(s_("lex"), 0))
                C.wait(nc.tensor, s_("kev"), C.cnt.get(s_("kev"), 0))
                C.inc(nc.tensor.matmul(psc[0:TOPK, 0:512],
                                       P["qgt"][:, TOPK * hd:TOPK * hd + TOPK],
                                       ten[:, 576 * hh:576 * hh + 512],
                                       start=True, stop=True), s_("gsc"))
                C.wait(nc.vector, s_("gsc"))
                C.inc(nc.vector.reduce_max(gmc, psc[0:TOPK, 0:512], AXL.X),
                      s_("gmx"))
                C.wait(nc.vector, s_("gmx"))
                C.inc(nc.vector.tensor_max(rm[1 - p][:, hh:hh + 1],
                                           rm[p][:, hh:hh + 1], gmc), s_("gnm"))
                C.wait(nc.vector, s_("gnm"))
                C.inc(nc.vector.tensor_sub(gda, rm[p][:, hh:hh + 1],
                                           rm[1 - p][:, hh:hh + 1]), s_("gda"))
                C.inc(nc.vector.tensor_scalar(gnn, rm[1 - p][:, hh:hh + 1],
                                              -1.0, None, AX.mult), s_("gnn"))
                C.wait(nc.scalar, s_("gda"))
                C.inc(nc.scalar.activation(gal, gda, AF.Exp), s_("gal"))
                C.wait(nc.scalar, s_("gnn"))
                C.wait(nc.scalar, s_("gtp"), C.cnt.get(s_("gtp"), 0))
                e = nc.scalar.activation(pgb[:], psc[0:TOPK, 0:512], AF.Exp,
                                         bias=gnn)
                C.inc(e, s_("gex"))
                C.wait(nc.vector, s_("gex"))
                C.inc(nc.vector.reduce_sum(gcs, pgb[:], AXL.X), s_("gcr"))
                C.wait(nc.vector, s_("gcr"))
                C.wait(nc.vector, s_("gal"))
                C.inc(nc.vector.scalar_tensor_tensor(
                    rs[1 - p][:, hh:hh + 1], rs[p][:, hh:hh + 1], gal, gcs,
                    AX.mult, AX.add), s_("grs"))
                prs = [(ptrb[1][:, 64 * i2:64 * i2 + 64],
                        pgb[:, 128 * i2:128 * i2 + 128]) for i2 in range(4)]
                C.wait(nc.tensor, s_("gex"))
                ins = tp_into(1, prs, P["idb"][0:TOPK, 0:TOPK])
                C.inc(ins, s_("gtp"))
                C.wait(nc.vector, s_("gtp"))
                cc = nc.vector.tensor_copy(pgbT[:], ptrb[1][:, 0:256])
                C.inc(cc, s_("pe1"))
                C.wait(nc.tensor, s_("pe1"))
                C.wait(nc.tensor, s_("vev"), C.cnt.get(s_("vev"), 0))
                for i2 in range(4):
                    m = 4 * fc + i2
                    ins = nc.tensor.matmul(ppv[:], pgbT[:, 64 * i2:64 * i2 + 64],
                                           vt[vslot(m)][:, 128 * hh:128 * hh + 128],
                                           start=(i2 == 0), stop=(i2 == 3))
                C.inc(ins, s_("gpv"))
                C.wait(nc.vector, s_("gpv"))
                C.inc(nc.vector.scalar_tensor_tensor(
                    gc[1 - p][:, 128 * hh:128 * hh + 128],
                    gc[p][:, 128 * hh:128 * hh + 128], gal, ppv[:],
                    AX.mult, AX.add), s_("ggc"))
                C.wait(nc.tensor, s_("ggc"))
            fcnt[0] += 1

        ready = {c: [] for c in range(NCH)}
        for qb in range(1, NSUB):
            ready[(128 * qb + 191) // 512].append(qb)

        for c in range(NCH):
            xn_rebuild(c)
            k_proj(c)
            v_proj(c)
            if c >= 1:
                flash_step(c)
            for qb in ready[c]:
                local_qb(qb)
        flash_step(0, use_kt0=True)
        local_qb(0, use_kt0=True)

        pfin = fcnt[0] % 2
        for hh in range(8):
            hd = 8 * hg + hh
            C.inc(nc.vector.reciprocal(grr, rs[pfin][:, hh:hh + 1]), s_("gfr"))
            C.wait(nc.vector, s_("gfr"))
            C.inc(nc.vector.tensor_scalar(P["gfin"][:, 128 * hd:128 * hd + 128],
                                          gc[pfin][:, 128 * hh:128 * hh + 128],
                                          grr, None, AX.mult), s_("gfw"))
        C.wait(nc.sync, s_("cso"))
        C.wait(nc.sync, s_("gfw"))
        if hg == 0 and "ktz" in C.dbg:
            C.wait(nc.gpsimd, s_("kev"), C.cnt[s_("kev")])
            C.wait(nc.gpsimd, s_("vev"), C.cnt[s_("vev")])
            C.dbg_dump("ktz", kt0[:], [128, 8 * 576], BF16)
            C.dbg_dump("kta", kt[0][:], [128, 8 * 576], BF16)
            C.dbg_dump("ktb", kt[1][:], [128, 8 * 576], BF16)
            for i in range(4):
                C.dbg_dump(f"vt{i}", vt[i][:], [128, 1024], BF16)


def phase_c(C, h, P):
    nc = C.nc
    s_ = lambda n: "c_" + n
    SC = 1.0 / float(DM)
    with ExitStack() as st:
        sb = lambda n, sh, dt=F32: st.enter_context(nc.sbuf_tensor(s_(n), sh, dt))
        ps = lambda n, sh, dt=F32: st.enter_context(nc.psum_tensor(s_(n), sh, dt))
        r_sb = sb("r", [128, DFF])
        xqt = sb("xqt", [128, 16 * 256], BF16)
        x2qt = sb("x2qt", [128, 64 * 256], BF16)
        w1sb = sb("w1sb", [128, DFF], BF16)
        w2sb = sb("w2sb", [128, DFF], BF16)
        g2b = sb("g2b", [128, DFF], BF16)
        g1b = sb("g1b", [128, DM], BF16)
        xsub = [sb(f"xs{i}", [128, DM]) for i in range(2)]
        ctxb = [sb(f"cb{i}", [128, DM], BF16) for i in range(2)]
        accp = [sb(f"acc{i}", [128, DM]) for i in range(2)]
        yst = [sb(f"yst{i}", [128, DM]) for i in range(2)]
        scrA = sb("scrA", [128, 512])
        scrB = sb("scrB", [128, 512])
        scrC = sb("scrC", [128, 512])
        scrI = sb("scrI", [128, 512], I32)
        qs_bf = sb("qs_bf", [128, 512], BF16)
        mtb = sb("mtb", [128, TOPK], BF16)
        mbf = sb("mbf", [TOPK, 128], BF16)
        mxp = sb("mxp", [128, 16])
        s4p = sb("s4p", [128, 16])
        selbar = sb("selbar", [128, NH])
        n_ = {}
        for nm in ("u1 u2 u3 rms1 rr1 mxa mxn mxn2 mxc rmx s1 cA colsc d1 rdc c1 "
                   "c2 c4 tt tt2 rms2 rr2 m2a m2b m2c rm2 s2 cs2a cs2b d2 rd2 c2f "
                   "mxu s4").split():
            n_[nm] = sb("n_" + nm, [128, 1])
        c1c = sb("c1c", [128, 2])
        c2fc = sb("c2fc", [128, 2])
        ptrb = ps("ptrb", [128, 1024], BF16)
        ptrf = ps("ptrf", [128, 512])
        g = [ps(f"g{i}", [128, 512]) for i in range(6)]

        C.inc(nc.gpsimd.dma_start(g2b[:], h["g2"][0:128, :]), s_("gld"), 16)
        C.inc(nc.gpsimd.dma_start(g1b[:], h["g1"][0:128, :]), s_("gld"), 16)
        C.wait(nc.vector, s_("gld"))

        def gate(ins, tag, eng):
            C.inc(ins, s_(tag))
            C.wait(eng, s_(tag))

        def tp_into(pairs, ident, pbank):
            C.wait(nc.tensor, s_("pe"), C.cnt.get(s_("pe"), 0))
            ins = None
            for dst, srcp in pairs:
                ins = nc.tensor.transpose(dst, srcp, ident)
            return ins

        for T in range(NCH):
            for ls in range(2):
                sub = 2 * T + ls
                if T > 0:
                    C.wait(nc.sync, s_("acc"), C.cnt.get(s_("acc"), 0))
                    C.wait(nc.sync, s_("bl"), C.cnt.get(s_("bl"), 0))
                C.inc(nc.sync.dma_start(xsub[ls][:],
                                        h["x"][128 * sub:128 * sub + 128, :]),
                      s_("xd"), 16)
                C.inc(nc.sync.dma_start(ctxb[ls][:], h["ctxh"][sub]), s_("cd"), 16)
            for ls in range(2):
                sub = 2 * T + ls
                C.wait(nc.vector, s_("xd"), 16 * (2 * T + ls + 1))
                for sl in range(4):
                    c0 = 512 * sl
                    gate(nc.vector.scalar_tensor_tensor(
                        scrA[:], xsub[ls][:, c0:c0 + 512],
                        P["stats_m"][:, sub:sub + 1], g1b[:, c0:c0 + 512],
                        AX.subtract, AX.mult), "t1", nc.vector)
                    C.inc(nc.vector.reduce_max(mxp[:, sl:sl + 1], scrA[:], AXL.X,
                                               apply_absolute_value=True), s_("t2p"))
                C.wait(nc.vector, s_("t2p"))
                gate(nc.vector.reduce_max(n_["mxa"][:], mxp[:, 0:4], AXL.X),
                     "nx", nc.vector)
                isd = P["stats_isd"][:, sub:sub + 1]
                gate(nc.vector.tensor_mul(n_["u1"][:], isd, isd), "nx", nc.vector)
                gate(nc.vector.tensor_scalar(n_["u2"][:], n_["u1"][:], -1.0e-5,
                                             None, AX.mult), "nx", nc.vector)
                gate(nc.vector.tensor_scalar(n_["u3"][:], n_["u2"][:], 1.000001,
                                             None, AX.add), "nx", nc.vector)
                C.wait(nc.scalar, s_("nx"))
                gate(nc.scalar.activation(n_["rms1"][:], n_["u3"][:], AF.Sqrt),
                     "ns", nc.vector)
                gate(nc.vector.reciprocal(n_["rr1"][:], n_["rms1"][:]), "nx", nc.vector)
                gate(nc.vector.tensor_mul(n_["mxn"][:], n_["mxa"][:], isd),
                     "nx", nc.vector)
                gate(nc.vector.tensor_mul(n_["mxn2"][:], n_["mxn"][:], n_["rr1"][:]),
                     "nx", nc.vector)
                gate(nc.vector.tensor_scalar(n_["mxc"][:], n_["mxn2"][:], 1.0e-5,
                                             None, AX.max), "nx", nc.vector)
                gate(nc.vector.reciprocal(n_["rmx"][:], n_["mxc"][:]), "nx", nc.vector)
                gate(nc.vector.tensor_scalar(n_["s1"][:], n_["rmx"][:], 127.0,
                                             None, AX.mult), "nx", nc.vector)
                gate(nc.vector.tensor_mul(n_["cA"][:], isd, n_["rr1"][:]),
                     "nx", nc.vector)
                gate(nc.vector.tensor_mul(n_["colsc"][:], n_["cA"][:], n_["s1"][:]),
                     "nx", nc.vector)
                gate(nc.vector.tensor_mul(n_["d1"][:], n_["s1"][:],
                                          P["swv"][:, 0:1]), "nx", nc.vector)
                gate(nc.vector.reciprocal(n_["rdc"][:], n_["d1"][:]), "nx", nc.vector)
                gate(nc.vector.tensor_scalar(c1c[:, ls:ls + 1], n_["rdc"][:], 0.5,
                                             None, AX.mult), "nx", nc.vector)
                for sl in range(4):
                    c0 = 512 * sl
                    gate(nc.vector.scalar_tensor_tensor(
                        scrB[:], xsub[ls][:, c0:c0 + 512],
                        P["stats_m"][:, sub:sub + 1], g1b[:, c0:c0 + 512],
                        AX.subtract, AX.mult), "t1b", nc.vector)
                    gate(nc.vector.tensor_scalar(scrA[:], scrB[:],
                                                 n_["colsc"][:], MAGIC,
                                                 AX.mult, AX.add), "qv", nc.vector)
                    gate(nc.vector.tensor_scalar(scrI[:], scrA[:], MAGIC, None,
                                                 AX.subtract), "qv", nc.vector)
                    C.wait(nc.scalar, s_("qv"))
                    if C.cnt.get(s_("qtp"), 0) > 0:
                        C.wait(nc.scalar, s_("qtp"))
                    C.inc(nc.scalar.copy(qs_bf[:], scrI[:]), s_("qb"))
                    C.wait(nc.tensor, s_("qb"))
                    prs = [(ptrb[:, 128 * j:128 * j + 128],
                            qs_bf[:, 128 * j:128 * j + 128]) for j in range(4)]
                    ins = tp_into(prs, P["idb"][:], ptrb)
                    C.inc(ins, s_("qtp"))
                    C.wait(nc.vector, s_("qtp"))
                    for j in range(4):
                        k = 4 * sl + j
                        cc = nc.vector.tensor_copy(
                            xqt[:, 256 * k + 128 * ls:256 * k + 128 * ls + 128],
                            ptrb[:, 128 * j:128 * j + 128])
                        C.inc(cc, s_("pe"))
                C.wait(nc.tensor, s_("pe"))
                C.wait(nc.scalar, s_("x2a"), C.cnt.get(s_("x2a"), 0))
                for ffs in range(16):
                    wb = w1sb if ffs % 2 == 0 else w2sb
                    if ffs == 0 and C.cnt.get(s_("f2m"), 0) > 0:
                        C.wait(nc.gpsimd, s_("f2m"))
                    if C.cnt.get(s_("f1m"), 0) > 1:
                        C.wait(nc.gpsimd, s_("f1m"), C.cnt[s_("f1m")] - 1)
                    C.inc(nc.gpsimd.dma_start(wb[:], h["w1t"][ffs]), s_("w1d"), 16)
                    C.wait(nc.tensor, s_("w1d"))
                    pb = g[ffs % 2]
                    if C.cnt.get(s_("rel"), 0) > 1:
                        C.wait(nc.tensor, s_("rel"), C.cnt[s_("rel")] - 1)
                    for k in range(16):
                        ins = nc.tensor.matmul(
                            pb[:], xqt[:, 256 * k + 128 * ls:256 * k + 128 * ls + 128],
                            wb[:, 512 * k:512 * k + 512],
                            start=(k == 0), stop=(k == 15))
                    C.inc(ins, s_("f1m"))
                    C.wait(nc.scalar, s_("f1m"))
                    C.inc(nc.scalar.activation(r_sb[:, 512 * ffs:512 * ffs + 512],
                                               pb[:], AF.Relu), s_("rel"))
                C.wait(nc.vector, s_("rel"), C.cnt[s_("rel")])
                if T == 0 and ls == 0 and "r0" in C.dbg:
                    C.wait(nc.gpsimd, s_("rel"), C.cnt[s_("rel")])
                    C.dbg_dump("r0", r_sb[:], [128, DFF])
                C.wait(nc.scalar, s_("ytp"), C.cnt.get(s_("ytp"), 0))
                for sl in range(16):
                    c0 = 512 * sl
                    nc.vector.tensor_mul(scrA[:], r_sb[:, c0:c0 + 512], g2b[:, c0:c0 + 512])
                    gate(nc.vector.tensor_mul(scrB[:], scrA[:], r_sb[:, c0:c0 + 512]),
                         "st1", nc.vector)
                    C.inc(nc.vector.reduce_max(mxp[:, sl:sl + 1], scrB[:], AXL.X,
                                               apply_absolute_value=True), s_("st2"))
                    C.wait(nc.scalar, s_("rel"), C.cnt[s_("rel")])
                    C.inc(nc.scalar.square(scrC[:], r_sb[:, c0:c0 + 512]), s_("sq1"))
                    C.wait(nc.scalar, s_("sq1"))
                    C.wait(nc.scalar, s_("st1"), C.cnt.get(s_("st1"), 0))
                    C.inc(nc.scalar.square(scrA[:], scrC[:]), s_("sq2"))
                    C.wait(nc.vector, s_("sq2"))
                    C.inc(nc.vector.reduce_sum(s4p[:, sl:sl + 1], scrA[:], AXL.X),
                          s_("st3"))
                C.wait(nc.vector, s_("st2"))
                C.wait(nc.vector, s_("st3"))
                gate(nc.vector.reduce_max(n_["mxu"][:], mxp[:], AXL.X), "nx", nc.vector)
                gate(nc.vector.reduce_sum(n_["s4"][:], s4p[:], AXL.X), "nx", nc.vector)
                c1 = c1c[:, ls:ls + 1]
                gate(nc.vector.tensor_mul(n_["c2"][:], c1, c1), "nx", nc.vector)
                gate(nc.vector.tensor_mul(n_["c4"][:], n_["c2"][:], n_["c2"][:]),
                     "nx", nc.vector)
                gate(nc.vector.tensor_mul(n_["tt"][:], n_["s4"][:], n_["c4"][:]),
                     "nx", nc.vector)
                gate(nc.vector.tensor_scalar(n_["tt2"][:], n_["tt"][:], 1.0 / DFF,
                                             None, AX.mult), "nx", nc.vector)
                C.wait(nc.scalar, s_("nx"))
                gate(nc.scalar.activation(n_["rms2"][:], n_["tt2"][:], AF.Sqrt,
                                          bias=P["eps6"][:]), "ns", nc.vector)
                gate(nc.vector.reciprocal(n_["rr2"][:], n_["rms2"][:]), "nx", nc.vector)
                gate(nc.vector.tensor_mul(n_["m2a"][:], n_["mxu"][:], n_["c2"][:]),
                     "nx", nc.vector)
                gate(nc.vector.tensor_mul(n_["m2b"][:], n_["m2a"][:], n_["rr2"][:]),
                     "nx", nc.vector)
                gate(nc.vector.tensor_scalar(n_["m2c"][:], n_["m2b"][:], 1.0e-5,
                                             None, AX.max), "nx", nc.vector)
                gate(nc.vector.reciprocal(n_["rm2"][:], n_["m2c"][:]), "nx", nc.vector)
                gate(nc.vector.tensor_scalar(n_["s2"][:], n_["rm2"][:], 127.0,
                                             None, AX.mult), "nx", nc.vector)
                gate(nc.vector.tensor_mul(n_["cs2a"][:], n_["c2"][:], n_["rr2"][:]),
                     "nx", nc.vector)
                gate(nc.vector.tensor_mul(n_["cs2b"][:], n_["cs2a"][:], n_["s2"][:]),
                     "nx", nc.vector)
                gate(nc.vector.tensor_mul(n_["d2"][:], n_["s2"][:], P["swv"][:, 1:2]),
                     "nx", nc.vector)
                gate(nc.vector.reciprocal(n_["rd2"][:], n_["d2"][:]), "nx", nc.vector)
                gate(nc.vector.tensor_scalar(c2fc[:, ls:ls + 1], n_["rd2"][:], 0.5,
                                             None, AX.mult), "nx", nc.vector)
                for sl in range(16):
                    c0 = 512 * sl
                    nc.vector.tensor_mul(scrA[:], r_sb[:, c0:c0 + 512],
                                         g2b[:, c0:c0 + 512])
                    gate(nc.vector.tensor_mul(scrB[:], scrA[:], r_sb[:, c0:c0 + 512]),
                         "x2a", nc.vector)
                    gate(nc.vector.tensor_scalar(scrA[:], scrB[:], n_["cs2b"][:],
                                                 MAGIC, AX.mult, AX.add),
                         "x2b", nc.vector)
                    gate(nc.vector.tensor_scalar(scrI[:], scrA[:], MAGIC, None,
                                                 AX.subtract), "x2c", nc.vector)
                    C.wait(nc.scalar, s_("x2c"))
                    if C.cnt.get(s_("x2tp"), 0) > 0:
                        C.wait(nc.scalar, s_("x2tp"))
                    C.inc(nc.scalar.copy(qs_bf[:], scrI[:]), s_("x2q"))
                    C.wait(nc.tensor, s_("x2q"))
                    prs = [(ptrb[:, 128 * j:128 * j + 128],
                            qs_bf[:, 128 * j:128 * j + 128]) for j in range(4)]
                    ins = tp_into(prs, P["idb"][:], ptrb)
                    C.inc(ins, s_("x2tp"))
                    C.wait(nc.vector, s_("x2tp"))
                    for j in range(4):
                        f = 4 * sl + j
                        cc = nc.vector.tensor_copy(
                            x2qt[:, 256 * f + 128 * ls:256 * f + 128 * ls + 128],
                            ptrb[:, 128 * j:128 * j + 128])
                        C.inc(cc, s_("pe"))
            C.wait(nc.vector, s_("yo"), C.cnt.get(s_("yo"), 0))
            for ls in range(2):
                sub = 2 * T + ls
                gate(nc.vector.tensor_scalar(selbar[:], P["sel_tm"][:, 16 * sub:16 * sub + 16],
                                             -1.0, 1.0, AX.mult, AX.add),
                     "sbv", nc.vector)
                C.wait(nc.vector, s_("cd"), 16 * (2 * T + ls + 1))
                for hd in range(NH):
                    gate(nc.vector.tensor_scalar(
                        mtb[:], P["iota64"][:],
                        P["rank_tm"][:, 16 * sub + hd:16 * sub + hd + 1],
                        P["sel_tm"][:, 16 * sub + hd:16 * sub + hd + 1],
                        AX.is_equal, AX.mult), "mtv", nc.tensor)
                    ins = tp_into([(ptrb[0:TOPK, 0:128], mtb[:])], P["idb"][:], ptrb)
                    C.inc(ins, s_("mtp"))
                    C.wait(nc.vector, s_("mtp"))
                    cc = nc.vector.tensor_copy(mbf[:], ptrb[0:TOPK, 0:128])
                    C.inc(cc, s_("pe"))
                    C.wait(nc.tensor, s_("pe"))
                    if hd == 0:
                        C.wait(nc.tensor, s_("bl"), C.cnt.get(s_("bl"), 0))
                    C.inc(nc.tensor.matmul(
                        g[2 + hd // 4][:, 128 * (hd % 4):128 * (hd % 4) + 128],
                        mbf[:], P["gfin"][:, 128 * hd:128 * hd + 128],
                        start=True, stop=True), s_("cgm"))
                C.wait(nc.vector, s_("cgm"))
                for hd in range(NH):
                    C.inc(nc.vector.scalar_tensor_tensor(
                        yst[ls][:, 128 * hd:128 * hd + 128],
                        ctxb[ls][:, 128 * hd:128 * hd + 128],
                        selbar[:, hd:hd + 1],
                        g[2 + hd // 4][:, 128 * (hd % 4):128 * (hd % 4) + 128],
                        AX.mult, AX.add), s_("bl"))
                C.wait(nc.vector, s_("bl"))
                C.inc(nc.vector.tensor_add(accp[ls][:], yst[ls][:], xsub[ls][:]),
                      s_("acc"))
            C.wait(nc.tensor, s_("pe"))
            if T == 0 and "acc0" in C.dbg:
                C.wait(nc.gpsimd, s_("acc"), C.cnt[s_("acc")])
                C.wait(nc.gpsimd, s_("pe"), C.cnt[s_("pe")])
                C.wait(nc.gpsimd, s_("nx"), C.cnt[s_("nx")])
                C.dbg_dump("acc0", accp[0][:], [128, DM])
                C.dbg_dump("acc1", accp[1][:], [128, DM])
                C.dbg_dump("x2qt0", x2qt[:], [128, 64 * 256], BF16)
                C.dbg_dump("xqt0", xqt[:], [128, 16 * 256], BF16)
                C.dbg_dump("c1c", c1c[:], [128, 2])
                C.dbg_dump("c2fc", c2fc[:], [128, 2])
            C.wait(nc.vector, s_("acc"))
            for b in range(16):
                wb2 = w2sb if b % 2 == 0 else w1sb
                if b == 0:
                    C.wait(nc.gpsimd, s_("f1m"), C.cnt[s_("f1m")])
                if C.cnt.get(s_("f2m"), 0) > 1:
                    C.wait(nc.gpsimd, s_("f2m"), C.cnt[s_("f2m")] - 1)
                C.inc(nc.gpsimd.dma_start(wb2[:], h["w2t"][b]), s_("w2d"), 16)
                C.wait(nc.tensor, s_("w2d"))
                pb = g[b % 2]
                if C.cnt.get(s_("o2e"), 0) > 1:
                    C.wait(nc.tensor, s_("o2e"), C.cnt[s_("o2e")] - 1)
                for k in range(64):
                    ins = nc.tensor.matmul(pb[:, 0:256], wb2[:, 128 * k:128 * k + 128],
                                           x2qt[:, 256 * k:256 * k + 256],
                                           start=(k == 0), stop=(k == 63))
                C.inc(ins, s_("f2m"))
                C.wait(nc.vector, s_("f2m"))
                C.inc(nc.vector.tensor_copy(scrC[:, 0:256], pb[:, 0:256]), s_("o2e"))
                C.wait(nc.tensor, s_("o2e"))
                C.wait(nc.tensor, s_("yw"), C.cnt.get(s_("yw"), 0))
                for ls in range(2):
                    ins = nc.tensor.transpose(ptrf[:, 128 * ls:128 * ls + 128],
                                              scrC[:, 128 * ls:128 * ls + 128],
                                              P["idf"][:])
                C.inc(ins, s_("ytp"))
                C.wait(nc.vector, s_("ytp"))
                for ls in range(2):
                    cc = nc.vector.scalar_tensor_tensor(
                        yst[ls][:, 128 * b:128 * b + 128],
                        ptrf[:, 128 * ls:128 * ls + 128], c2fc[:, ls:ls + 1],
                        accp[ls][:, 128 * b:128 * b + 128], AX.mult, AX.add)
                    C.inc(cc, s_("yw"))
            C.wait(nc.scalar, s_("yw"))
            for ls in range(2):
                sub = 2 * T + ls
                C.inc(nc.scalar.dma_start(h["y"][128 * sub:128 * sub + 128, :],
                                          yst[ls][:]), s_("yo"), 16)
        C.wait(nc.sync, s_("yo"))


def _consts():
    c = {}
    c["ident_bf"] = np.eye(128, dtype=ml_dtypes.bfloat16)
    c["ident_f32"] = np.eye(128, dtype=np.float32)
    c["triu_bf"] = np.triu(np.ones((128, 128))).astype(ml_dtypes.bfloat16)
    c["ones_col_bf"] = np.ones((128, 1), dtype=ml_dtypes.bfloat16)
    c["ones_row_bf"] = np.ones((1, 128), dtype=ml_dtypes.bfloat16)
    c["ones_col_f"] = np.ones((128, 1), dtype=np.float32)
    c["ones_row_f"] = np.ones((1, 128), dtype=np.float32)
    c["iota64_rep"] = np.ascontiguousarray(
        np.tile(np.arange(1, TOPK + 1, dtype=np.float32), (128, 1)))
    return c


def _stage_core(inputs, core):
    seq = core // 2
    half = core % 2
    off = HALF * half
    x = np.asarray(inputs["L0_attn_hidden_states"][seq])
    m = {}
    m["x_seq"] = np.ascontiguousarray(np.roll(x, -off, axis=0))
    m["Wq"] = np.asarray(inputs["L0_attn_Wq"])
    m["Wk"] = np.asarray(inputs["L0_attn_Wk"])
    m["Wv"] = np.asarray(inputs["L0_attn_Wv"])
    m["w1"] = np.asarray(inputs["L0_ffn_w1"])
    m["w2"] = np.asarray(inputs["L0_ffn_w2"])
    m["bq_rep"] = np.ascontiguousarray(
        np.tile(np.asarray(inputs["L0_attn_bq"])[None, :], (128, 1)))
    m["bv_rep"] = np.ascontiguousarray(
        np.tile(np.asarray(inputs["L0_attn_bv"])[None, :], (128, 1)))
    m["bk_hm"] = np.ascontiguousarray(
        np.asarray(inputs["L0_attn_bk"]).reshape(NH, DH).T)
    m["g1_rep"] = np.ascontiguousarray(
        np.tile(np.asarray(inputs["L0_ffn_g1"])[None, :], (128, 1)))
    m["g2_rep"] = np.ascontiguousarray(
        np.tile(np.asarray(inputs["L0_ffn_g2"])[None, :], (128, 1)))
    m.update(_consts())
    mask = np.zeros((2, 128, 256), dtype=np.float32)
    if half == 0:
        mask[0, :, 0:64] = NEG
    else:
        mask[1, :, 192:256] = NEG
    m["maskadd"] = mask
    be = np.zeros((NH, SEQ), dtype=np.float32)
    be[:, (0 - off) % SEQ] = 1.0
    be[:, (SEQ - 1 - off) % SEQ] = 1.0
    m["boseos"] = be
    return m


def kernel(**inputs):
    nc = build_program(dbg=_DBG, phases=_PHASES)
    in_maps = [_stage_core(inputs, c) for c in range(8)]
    res = run_bass_kernel_spmd(nc, in_maps, core_ids=list(range(8)))
    out = np.empty((4, SEQ, DM), dtype=np.float32)
    for c in range(8):
        seq, half = c // 2, c % 2
        out[seq, HALF * half:HALF * half + HALF] = res.results[c]["y"]
    return out
